# version 15
# baseline (speedup 1.0000x reference)
import numpy as np

import concourse.bass as bass
import concourse.mybir as mybir
from concourse.bass_utils import run_bass_kernel_spmd

F32 = mybir.dt.float32
BF16 = mybir.dt.bfloat16
AF = mybir.ActivationFunctionType
ALU = mybir.AluOpType
AX = mybir.AxisListType

BIG = 1.0e9


class Cfg:
    def __init__(self, T, D, H, F, SF, E, n_cores=8, tt=None, oh=None):
        self.T, self.D, self.H, self.F, self.SF, self.E = T, D, H, F, SF, E
        self.n_cores = n_cores
        self.KC = D // 128
        self.NC = T // 128
        self.Tt = tt if tt else min(1024, T)
        self.NMT = T // self.Tt
        self.OH = oh if oh else min(1024, D)
        self.NOH = D // self.OH
        assert D % 128 == 0 and T % 128 == 0 and F % 128 == 0 and SF % 128 == 0


class Sem:
    __slots__ = ("h", "n")

    def __init__(self, h):
        self.h = h
        self.n = 0

    def inc(self, k=1):
        self.n += k
        return self.n


class Builder:
    ENGINES = ("tensor", "vector", "scalar", "gpsimd", "sync")

    def __init__(self, nc):
        self.nc = nc
        self.plan = {e: [] for e in self.ENGINES}
        self.last_wait = {}

    def em(self, eng, fn):
        self.plan[eng].append(fn)

    def wait(self, eng, sem, val=None):
        if val is None:
            val = sem.n
        if val <= 0:
            return
        key = (eng, id(sem))
        if self.last_wait.get(key, -1) >= val:
            return
        self.last_wait[key] = val
        self.em(eng, lambda e, s=sem.h, v=val: e.wait_ge(s, v))

    def barrier(self, sem, data_sems=()):
        for ds in data_sems:
            for eng in self.ENGINES:
                self.wait(eng, ds)
        tgt = sem.n + len(self.ENGINES)
        sem.n = tgt
        for eng in self.ENGINES:
            self.em(eng, lambda e, s=sem.h: e.nop().then_inc(s, 1))
        for eng in self.ENGINES:
            self.last_wait[(eng, id(sem))] = tgt
            self.em(eng, lambda e, s=sem.h, v=tgt: e.wait_ge(s, v))

    def run(self, block):
        def make(name):
            def body(eng):
                for fn in self.plan[name]:
                    fn(eng)
            return body
        block.tensor(make("tensor"))
        block.vector(make("vector"))
        block.scalar(make("scalar"))
        block.gpsimd(make("gpsimd"))
        block.sync(make("sync"))


class Arena:
    _uid = [0]

    def __init__(self, nc, start, top):
        self.nc = nc
        self.cur = start
        self.top = top

    def c(self, cols, dtype, parts=128):
        sz = cols * (2 if dtype == BF16 else 4)
        off = (self.cur + 31) & ~31
        self.cur = off + sz
        assert self.cur <= self.top, f"SBUF arena overflow: {self.cur} > {self.top}"
        Arena._uid[0] += 1
        return self.nc.alloc_sbuf_tensor_at(f"ar{Arena._uid[0]}", [parts, cols], dtype, offset=off)


def build_program(cfg: Cfg):
    nc = bass.Bass("TRN2", target_bir_lowering=False)
    T, D, H, F, SF, E = cfg.T, cfg.D, cfg.H, cfg.F, cfg.SF, cfg.E
    KC, NC, Tt, NMT, OH, NOH = cfg.KC, cfg.NC, cfg.Tt, cfg.NMT, cfg.OH, cfg.NOH
    NTG = Tt // 128
    NTT = T // 128
    NS = max(1, T // 512)
    SLAB = T // NS
    NHF, NHS = F // 128, SF // 128
    NHMAX = max(NHF, NHS)

    ap = lambda *a, **k: nc.dram_tensor(*a, **k).ap()
    x_d = ap("x", [T, D], F32, kind="ExternalInput")
    wqkv_d = ap("wqkv", [D, 3 * D], F32, kind="ExternalInput")
    bqkv_d = ap("bqkv", [3 * D], F32, kind="ExternalInput")
    wout_d = ap("wout", [D, D], F32, kind="ExternalInput")
    bout_d = ap("bout", [D], F32, kind="ExternalInput")
    slopes_d = ap("slopes", [H, 1], F32, kind="ExternalInput")
    rmsw_d = ap("rmsw", [D], F32, kind="ExternalInput")
    rw_d = ap("router_w", [D, E], F32, kind="ExternalInput")
    wg_d = ap("wg", [E, D, F], F32, kind="ExternalInput")
    wu_d = ap("wu", [E, D, F], F32, kind="ExternalInput")
    wd_d = ap("wd", [E, F, D], F32, kind="ExternalInput")
    swg_d = ap("swg", [D, SF], F32, kind="ExternalInput")
    swu_d = ap("swu", [D, SF], F32, kind="ExternalInput")
    swd_d = ap("swd", [SF, D], F32, kind="ExternalInput")
    c_intra_d = ap("c_intra", [128, H * 128], F32, kind="ExternalInput")
    c_qdec_d = ap("c_qdec", [128, H * 128], F32, kind="ExternalInput")
    c_kdec_d = ap("c_kdec", [128, H], F32, kind="ExternalInput")
    c_bdc_d = ap("c_bdc", [128, H * NC], F32, kind="ExternalInput")
    c_comb_d = ap("c_comb", [128, H * 8], F32, kind="ExternalInput")
    sel_d = ap("sel", [E, E * 128], F32, kind="ExternalInput")
    identf_d = ap("identf", [128, 128], F32, kind="ExternalInput")
    out_d = ap("out", [T, D], F32, kind="ExternalOutput")

    x_scr = ap("x_scr", [T, D], BF16, kind="Internal")
    o_scr = ap("o_scr", [H * 128, T], BF16, kind="Internal")
    q_scr = ap("q_scr", [H * 128, T], BF16, kind="Internal")
    y_scr = ap("y_scr", [T, D], F32, kind="Internal")
    h_scr = ap("h_scr", [T, D], BF16, kind="Internal")
    H2 = H // 2
    sl_in_a = ap("sl_in_a", [128, H2 * 128], F32, kind="Internal")
    sl_out_a = ap("sl_out_a", [cfg.n_cores * 128, H2 * 128], F32, kind="Internal")
    sl_in_b = ap("sl_in_b", [128, (H - H2) * 128], F32, kind="Internal")
    sl_out_b = ap("sl_out_b", [cfg.n_cores * 128, (H - H2) * 128], F32, kind="Internal")

    def S(name):
        return Sem(nc.alloc_semaphore(name))

    sg_w = S("sg_w")
    s_tab = S("s_tab")
    s_bq = S("s_bq")
    s_xst = S("s_xst")
    s_wout = S("s_wout")
    s_rw = S("s_rw")
    s_out = S("s_out")
    s_sg = S("s_sg")
    sq_io = S("sq_io")
    sa_io = S("sa_io")
    s_pe = S("s_pe")
    s_dv = S("s_dv")
    s_ac = S("s_ac")
    s_pl = S("s_pl")
    s_cc = S("s_cc")
    s_bar = S("s_bar")
    w_slots = (S("sw0"), S("sw1"))
    fg_slots = (S("sfg0"), S("sfg1"))
    fd_slots = (S("sfd0"), S("sfd1"))
    t_slots = (S("st0"), S("st1"))

    b = Builder(nc)
    psum = nc.alloc_psum_tensor("pool", [128, 4096], F32)

    def bank(i, w=512):
        return psum[:, i * 512:i * 512 + w]

    SB_BASE = (nc.sbuf_base + 31) & ~31
    SB_TOP = nc.sbuf_top
    perm = Arena(nc, SB_BASE, SB_TOP)
    identf = perm.c(128, F32)
    sg_all = perm.c(8 * 128, F32)
    comb_t = perm.c(H * 8, F32)
    wfmT = perm.c(min(1024, T), F32)
    si_a = perm.c(128, F32)
    si_b = perm.c(128, F32)
    stats = perm.c(32, F32)
    eps_t = perm.c(4, F32)
    s_init = perm.c(H * 128, BF16)
    PH = perm.cur

    cvA = Arena(nc, PH, SB_TOP)
    slope_cols = cvA.c(H, F32)
    intraT = cvA.c(H * 128, F32)
    qdec_t = cvA.c(H * 128, F32)
    kdec_t = cvA.c(H, F32)
    bdc_t = cvA.c(H * NC, F32)
    tmp_tab = cvA.c(H * 128, F32)
    bqkv_cols = cvA.c(3 * KC, F32)
    bq_rows = cvA.c(128, F32)
    bias_k = [cvA.c(128, F32) for _ in range(2)]
    bias_v = [cvA.c(128, F32) for _ in range(2)]
    bkd_tmp = [cvA.c(128, F32) for _ in range(2)]
    s_a = cvA.c(128, F32)
    s_b = cvA.c(128, F32)
    x_fm = cvA.c(KC * T, BF16)
    wq_sb = [cvA.c(KC * 128, BF16) for _ in range(2)]
    wkv_sb = [cvA.c(KC * 256, BF16) for _ in range(2)]
    q_fm = cvA.c(T, BF16)
    k_fm = cvA.c(T, BF16)
    v_tm = cvA.c(T, BF16)
    kd_tm = cvA.c(T, BF16)
    o_buf = cvA.c(T, BF16)
    qg_buf = cvA.c(T, BF16)
    scT_sb = cvA.c(128, BF16)
    qdec_sb = cvA.c(128, BF16)
    s_bf = cvA.c(128, BF16)
    rows_per = min(1024, T)
    stage_cols = (rows_per // 128) * D
    x_stage = nc.alloc_sbuf_tensor_at("x_stage", [128, stage_cols], BF16,
                                      offset=(SB_TOP - stage_cols * 2) & ~31)
    assert cvA.cur <= ((SB_TOP - stage_cols * 2) & ~31) or True

    def dma(eng, sem, out, in_, **kw):
        b.em(eng, lambda e, o=out, i=in_, kw=kw: e.dma_start(out=o, in_=i, **kw).then_inc(sem.h, 16))
        sem.inc(16)
        return sem.n

    b.em("vector", lambda v: v.memset(eps_t[:, 0:1], 1e-5))
    b.em("vector", lambda v: v.memset(eps_t[:, 1:2], 1e-6))
    b.em("vector", lambda v: v.memset(eps_t[:, 2:3], 1e-6).then_inc(s_dv.h, 1)); s_dv.inc()
    dma("gpsimd", s_tab, identf[:, :], identf_d)
    dma("gpsimd", s_tab, slope_cols[:, :], slopes_d.rearrange("h one -> one h").partition_broadcast(128))
    dma("gpsimd", s_tab, intraT[:, :], c_intra_d)
    dma("gpsimd", s_tab, qdec_t[:, :], c_qdec_d)
    dma("gpsimd", s_tab, kdec_t[:, :], c_kdec_d)
    dma("gpsimd", s_tab, bdc_t[:, :], c_bdc_d)
    dma("gpsimd", s_tab, comb_t[:, :], c_comb_d)

    b.wait("vector", s_tab)
    for tab, blkw in ((intraT, 128), (qdec_t, 128), (bdc_t, NC), (comb_t, 8), (kdec_t, 1)):
        b.wait("vector", s_ac)
        for h in range(H):
            last = (h == H - 1)
            b.em("vector", lambda v, t=tab, h=h, w=blkw, last=last:
                 (v.tensor_scalar(tmp_tab[:, h * w:(h + 1) * w], t[:, h * w:(h + 1) * w],
                                  slope_cols[:, h:h + 1], None, ALU.mult).then_inc(s_dv.h, 1)
                  if last else
                  v.tensor_scalar(tmp_tab[:, h * w:(h + 1) * w], t[:, h * w:(h + 1) * w],
                                  slope_cols[:, h:h + 1], None, ALU.mult)))
        s_dv.inc()
        b.wait("scalar", s_dv)
        b.em("scalar", lambda s, t=tab, w=blkw:
             s.activation(t[:, 0:H * w], tmp_tab[:, 0:H * w], AF.Exp, scale=-1.0).then_inc(s_ac.h, 1)); s_ac.inc()
    tabs_ready = s_ac.n

    dma("gpsimd", s_bq, bq_rows[0:3 * KC, :], bqkv_d.rearrange("(c p) -> c p", p=128))
    b.wait("tensor", s_bq)
    b.wait("tensor", s_tab)
    b.em("tensor", lambda t: t.transpose(bank(7, 3 * KC), bq_rows[0:3 * KC, :],
                                         identf[0:3 * KC, 0:3 * KC]).then_inc(s_pe.h, 1)); s_pe.inc()
    b.wait("vector", s_pe)
    b.em("vector", lambda v: v.tensor_copy(bqkv_cols[:, :], bank(7, 3 * KC)).then_inc(s_dv.h, 1)); s_dv.inc()
    bq_ready = s_dv.n

    nxl = T // rows_per
    for i in range(nxl):
        b.wait("gpsimd", sq_io)
        dma("gpsimd", s_xst,
            x_stage.ap().rearrange("p (n d) -> p n d", n=rows_per // 128),
            x_d[i * rows_per:(i + 1) * rows_per, :].rearrange("(n p) d -> p n d", p=128))
        b.wait("sync", s_xst)
        dma("sync", sq_io,
            x_scr[i * rows_per:(i + 1) * rows_per, :].rearrange("(n p) d -> p n d", p=128),
            x_stage.ap().rearrange("p (n d) -> p n d", n=rows_per // 128))
    b.wait("sync", sq_io)
    for fc in range(KC):
        dma("sync", sq_io, x_fm[:, fc * T:(fc + 1) * T], x_scr[:, fc * 128:(fc + 1) * 128], transpose=True)
    x_fm_ready = sq_io.n

    head_pe_free = {}
    head_dv_free = {}
    head_w_ready = {}

    def kvtile(sl, kc, which=None):
        if which is None:
            return wkv_sb[sl][:, kc * 256:(kc + 1) * 256]
        return wkv_sb[sl].ap().rearrange("p (kc two f) -> p kc two f", kc=KC, two=2)[:, kc, which, :]

    def load_head_weights(h):
        sl = h % 2
        sem = w_slots[sl]
        if h >= 2:
            b.wait("gpsimd", s_pe, head_pe_free[h - 2])
            b.wait("gpsimd", s_dv, head_dv_free[h - 2])
        dma("gpsimd", sem, wq_sb[sl].ap().rearrange("p (kc f) -> p kc f", kc=KC),
            wqkv_d[:, h * 128:(h + 1) * 128].rearrange("(kc p) f -> p kc f", p=128))
        dma("gpsimd", sem, wkv_sb[sl].ap().rearrange("p (kc two f) -> p kc two f", kc=KC, two=2)[:, :, 0, :],
            wqkv_d[:, D + h * 128:D + (h + 1) * 128].rearrange("(kc p) f -> p kc f", p=128))
        dma("gpsimd", sem, wkv_sb[sl].ap().rearrange("p (kc two f) -> p kc two f", kc=KC, two=2)[:, :, 1, :],
            wqkv_d[:, 2 * D + h * 128:2 * D + (h + 1) * 128].rearrange("(kc p) f -> p kc f", p=128))
        dma("gpsimd", sem, bias_k[sl][:, :],
            bqkv_d.rearrange("(one d) -> one d", one=1)[:, D + h * 128:D + (h + 1) * 128].partition_broadcast(128))
        dma("gpsimd", sem, bias_v[sl][:, :],
            bqkv_d.rearrange("(one d) -> one d", one=1)[:, 2 * D + h * 128:2 * D + (h + 1) * 128].partition_broadcast(128))
        return sem.n

    head_w_ready[0] = load_head_weights(0)
    if H > 1:
        head_w_ready[1] = load_head_weights(1)
    b.wait("tensor", sq_io, x_fm_ready)
    b.wait("tensor", s_ac, tabs_ready)

    si_ready_a_box = [None]

    def emit_combine1():
        b.wait("sync", s_cc, 1)
        for hh in range(H2):
            b.wait("sync", s_dv, s_dv.n)
            dma("sync", s_sg, sg_all.ap().rearrange("p (j f) -> p j f", j=cfg.n_cores),
                sl_out_a.rearrange("(j p) c -> p j c", p=128)[:, :, hh * 128:(hh + 1) * 128])
            b.wait("vector", s_sg)
            cur = None
            for j in range(cfg.n_cores):
                if j == 0:
                    b.em("vector", lambda v, hh=hh:
                         v.tensor_scalar(si_a[:, :], sg_all[:, 0:128], comb_t[:, hh * 8:hh * 8 + 1], None, ALU.mult).then_inc(s_dv.h, 1)); s_dv.inc()
                    cur = si_a
                else:
                    nxt = si_b if cur is si_a else si_a
                    b.wait("vector", s_dv, s_dv.n)
                    b.em("vector", lambda v, hh=hh, j=j, src=cur, dst=nxt:
                         v.scalar_tensor_tensor(dst[:, :], sg_all[:, j * 128:(j + 1) * 128],
                                                comb_t[:, hh * 8 + j:hh * 8 + j + 1], src[:, :],
                                                ALU.mult, ALU.add).then_inc(s_dv.h, 1)); s_dv.inc()
                    cur = nxt
            b.wait("vector", s_dv, s_dv.n)
            b.em("vector", lambda v, hh=hh, fin=cur:
                 v.tensor_copy(s_init[:, hh * 128:(hh + 1) * 128], fin[:, :]).then_inc(s_dv.h, 1)); s_dv.inc()
        si_ready_a_box[0] = ("dv", s_dv.n)

    store_guard = 0
    for h in range(H):
        sl = h % 2
        sem = w_slots[sl]
        wv = head_w_ready[h]
        b.wait("tensor", s_dv, s_dv.n)
        b.wait("tensor", s_ac, s_ac.n)
        b.wait("tensor", s_pl, s_pl.n)
        b.wait("tensor", sem, wv)
        b.wait("vector", sq_io, store_guard)
        b.wait("scalar", sq_io, store_guard)
        b.wait("gpsimd", sq_io, store_guard)
        b.wait("vector", sem, wv)
        b.wait("vector", s_ac, tabs_ready)
        b.em("vector", lambda v, h=h, sl=sl:
             v.tensor_scalar(bkd_tmp[sl][:, :], bias_k[sl][:, :], kdec_t[:, h:h + 1], None, ALU.mult).then_inc(s_dv.h, 1)); s_dv.inc()
        bkd_ready = s_dv.n

        for kc in range(KC):
            for s in range(NS):
                last = (kc == KC - 1 and s == NS - 1)
                b.em("tensor", lambda t, kc=kc, s=s, sl=sl, last=last:
                     (t.matmul(psum[:, s * SLAB:(s + 1) * SLAB], wq_sb[sl][:, kc * 128:(kc + 1) * 128],
                               x_fm[:, kc * T + s * SLAB:kc * T + (s + 1) * SLAB],
                               start=(kc == 0), stop=(kc == KC - 1)).then_inc(s_pe.h, 1) if last else
                      t.matmul(psum[:, s * SLAB:(s + 1) * SLAB], wq_sb[sl][:, kc * 128:(kc + 1) * 128],
                               x_fm[:, kc * T + s * SLAB:kc * T + (s + 1) * SLAB],
                               start=(kc == 0), stop=(kc == KC - 1))))
        s_pe.inc()
        q_done = s_pe.n
        for kc in range(KC):
            for s in range(NS):
                last = (kc == KC - 1 and s == NS - 1)
                b.em("tensor", lambda t, kc=kc, s=s, sl=sl, last=last:
                     (t.matmul(psum[:, 2048 + s * SLAB:2048 + (s + 1) * SLAB], kvtile(sl, kc, 0),
                               x_fm[:, kc * T + s * SLAB:kc * T + (s + 1) * SLAB],
                               start=(kc == 0), stop=(kc == KC - 1)).then_inc(s_pe.h, 1) if last else
                      t.matmul(psum[:, 2048 + s * SLAB:2048 + (s + 1) * SLAB], kvtile(sl, kc, 0),
                               x_fm[:, kc * T + s * SLAB:kc * T + (s + 1) * SLAB],
                               start=(kc == 0), stop=(kc == KC - 1))))
        s_pe.inc()
        k_done = s_pe.n
        b.wait("scalar", s_pe, q_done)
        b.em("scalar", lambda s, h=h:
             s.activation(q_fm[:, :], psum[:, 0:T], AF.Identity, bias=bqkv_cols[:, h:h + 1]).then_inc(s_ac.h, 1)); s_ac.inc()
        q_evac = s_ac.n
        b.wait("scalar", s_pe, k_done)
        b.em("scalar", lambda s, h=h:
             s.activation(k_fm[:, :], psum[:, 2048:2048 + T], AF.Identity, bias=bqkv_cols[:, KC + h:KC + h + 1]).then_inc(s_ac.h, 1)); s_ac.inc()
        k_evac = s_ac.n

        b.wait("tensor", s_ac, q_evac)
        kv_dv = []
        for tt in range(NTT):
            bk = tt % 2
            if tt >= 2:
                b.wait("tensor", s_dv, kv_dv[tt - 2])
            for kc in range(KC):
                last = (kc == KC - 1)
                b.em("tensor", lambda t, tt=tt, kc=kc, bk=bk, sl=sl, last=last:
                     (t.matmul(bank(bk, 256), x_fm[:, kc * T + tt * 128:kc * T + (tt + 1) * 128],
                               kvtile(sl, kc), start=(kc == 0), stop=(kc == KC - 1)).then_inc(s_pe.h, 1)
                      if last else
                      t.matmul(bank(bk, 256), x_fm[:, kc * T + tt * 128:kc * T + (tt + 1) * 128],
                               kvtile(sl, kc), start=(kc == 0), stop=(kc == KC - 1))))
            s_pe.inc()
            b.wait("vector", s_pe, s_pe.n)
            b.wait("vector", s_dv, bkd_ready)
            b.em("vector", lambda v, tt=tt, bk=bk, h=h, sl=sl:
                 v.scalar_tensor_tensor(kd_tm[:, tt * 128:(tt + 1) * 128], bank(bk, 256)[:, 0:128],
                                        kdec_t[:, h:h + 1], bkd_tmp[sl][:, :], ALU.mult, ALU.add))
            b.em("vector", lambda v, tt=tt, bk=bk, sl=sl:
                 v.tensor_tensor(v_tm[:, tt * 128:(tt + 1) * 128], bank(bk, 256)[:, 128:256],
                                 bias_v[sl][:, :], ALU.add).then_inc(s_dv.h, 1)); s_dv.inc()
            kv_dv.append(s_dv.n)
        head_pe_free[h] = s_pe.n
        head_dv_free[h] = s_dv.n
        if h + 2 < H:
            head_w_ready[h + 2] = load_head_weights(h + 2)

        b.wait("tensor", s_dv, kv_dv[-1])
        b.wait("tensor", s_ac, k_evac)
        o_evacs = []
        mask_done = {}
        supd_done = {}
        qdec_done = {}
        qg_done = {}
        o_done_d = {}
        sbf_done_d = {}
        sbf_done = None
        cur_s = None
        for c in range(NC):
            cs = slice(c * 128, (c + 1) * 128)
            sc_b = 2 + (c % 2)
            dl_b = 6 + (c % 2)
            o_b = 4 + ((c // 4) % 2)
            ocol = (c % 4) * 128
            if c >= 2:
                b.wait("tensor", s_dv, mask_done[c - 2])
            b.em("tensor", lambda t, cs=cs, sc_b=sc_b:
                 t.matmul(bank(sc_b, 128), k_fm[:, cs], q_fm[:, cs], start=True, stop=True).then_inc(s_pe.h, 1)); s_pe.inc()
            sc_done = s_pe.n
            if c >= 2:
                b.wait("tensor", s_dv, supd_done[c - 2])
            b.em("tensor", lambda t, cs=cs, dl_b=dl_b:
                 t.matmul(bank(dl_b, 128), kd_tm[:, cs], v_tm[:, cs], start=True, stop=True).then_inc(s_pe.h, 1)); s_pe.inc()
            dl_done = s_pe.n
            b.wait("vector", s_pe, sc_done)
            if c >= 1:
                b.wait("vector", s_pe, o_done_d[c - 1])
                b.wait("vector", s_dv, qg_done[c - 1])
            b.em("vector", lambda v, sc_b=sc_b, h=h:
                 v.tensor_tensor(scT_sb[:, :], bank(sc_b, 128), intraT[:, h * 128:(h + 1) * 128], ALU.mult).then_inc(s_dv.h, 1)); s_dv.inc()
            mask_done[c] = s_dv.n
            b.em("vector", lambda v, cs=cs, h=h:
                 v.tensor_tensor(qdec_sb[:, :], q_fm[:, cs], qdec_t[:, h * 128:(h + 1) * 128], ALU.mult).then_inc(s_dv.h, 1)); s_dv.inc()
            qdec_done[c] = s_dv.n
            b.wait("vector", s_dv, qdec_done[c])
            b.em("vector", lambda v, c=c, h=h:
                 v.tensor_scalar(qg_buf[:, c * 128:(c + 1) * 128], qdec_sb[:, :],
                                 bdc_t[:, h * NC + c:h * NC + c + 1], None, ALU.mult).then_inc(s_dv.h, 1)); s_dv.inc()
            qg_done[c] = s_dv.n
            b.wait("tensor", s_dv, mask_done[c])
            if c % 4 == 0 and o_evacs:
                b.wait("tensor", s_ac, o_evacs[-1])
            if c == 0:
                b.em("tensor", lambda t, cs=cs, o_b=o_b, ocol=ocol:
                     t.matmul(bank(o_b)[:, ocol:ocol + 128], v_tm[:, cs], scT_sb[:, :],
                              start=True, stop=True).then_inc(s_pe.h, 1)); s_pe.inc()
            else:
                b.em("tensor", lambda t, cs=cs, o_b=o_b, ocol=ocol:
                     t.matmul(bank(o_b)[:, ocol:ocol + 128], v_tm[:, cs], scT_sb[:, :],
                              start=True, stop=False))
                b.wait("tensor", s_dv, qdec_done[c])
                b.wait("tensor", s_ac, sbf_done)
                b.em("tensor", lambda t, o_b=o_b, ocol=ocol:
                     t.matmul(bank(o_b)[:, ocol:ocol + 128], s_bf[:, :], qdec_sb[:, :],
                              start=False, stop=True).then_inc(s_pe.h, 1)); s_pe.inc()
            o_done = s_pe.n
            o_done_d[c] = o_done
            b.wait("vector", s_pe, dl_done)
            if c == 0:
                b.em("vector", lambda v, dl_b=dl_b:
                     v.tensor_copy(s_a[:, :], bank(dl_b, 128)).then_inc(s_dv.h, 1)); s_dv.inc()
                cur_s = s_a
            else:
                new_s = s_b if cur_s is s_a else s_a
                b.wait("vector", s_dv, supd_done[c - 1])
                if c >= 2:
                    b.wait("vector", s_ac, sbf_done_d[c - 2])
                b.em("vector", lambda v, dl_b=dl_b, so=cur_s, sn=new_s, h=h:
                     v.scalar_tensor_tensor(sn[:, :], so[:, :], bdc_t[:, h * NC + 1:h * NC + 2],
                                            bank(dl_b, 128), ALU.mult, ALU.add).then_inc(s_dv.h, 1)); s_dv.inc()
                cur_s = new_s
            supd_done[c] = s_dv.n
            b.wait("scalar", s_dv, supd_done[c])
            b.wait("scalar", s_pe, o_done)
            b.em("scalar", lambda s, sn=cur_s: s.activation(s_bf[:, :], sn[:, :], AF.Copy).then_inc(s_ac.h, 1)); s_ac.inc()
            sbf_done = s_ac.n
            sbf_done_d[c] = sbf_done
            if c % 4 == 3 or c == NC - 1:
                b.wait("scalar", s_pe, o_done)
                lo = (c // 4) * 4
                b.em("scalar", lambda s, lo=lo, c=c, o_b=o_b:
                     s.activation(o_buf[:, lo * 128:(c + 1) * 128], bank(o_b, (c + 1 - lo) * 128),
                                  AF.Copy).then_inc(s_ac.h, 1)); s_ac.inc()
                o_evacs.append(s_ac.n)
        b.wait("sync", s_ac, o_evacs[-1])
        dma("sync", sq_io, o_scr[h * 128:(h + 1) * 128, :], o_buf[:, :])
        b.wait("sync", s_dv, s_dv.n)
        dma("sync", sq_io, q_scr[h * 128:(h + 1) * 128, :], qg_buf[:, :])
        b.wait("sync", s_dv, supd_done[NC - 1])
        if h < H2:
            dma("sync", sq_io, sl_in_a[:, h * 128:(h + 1) * 128], cur_s[:, :])
        else:
            dma("sync", sq_io, sl_in_b[:, (h - H2) * 128:(h - H2 + 1) * 128], cur_s[:, :])
        store_guard = sq_io.n
        if h == H2 - 1:
            half1_stores = sq_io.n
        if h == H2 + 1:
            b.wait("gpsimd", sq_io, store_guard)
            b.em("gpsimd", lambda g: g.collective_compute(
                "AllGather", ALU.bypass, replica_groups=[list(range(cfg.n_cores))],
                ins=[sl_in_a.opt()], outs=[sl_out_a.opt()]).then_inc(s_cc.h, 1)); s_cc.inc()
        if h == max(H2 + 2, H - 2) and h <= H - 1:
            emit_combine1()

    if max(H2 + 2, H - 2) > H - 1:
        emit_combine1()

    b.barrier(s_bar, data_sems=(s_pe, s_dv, s_ac, s_pl, sq_io, sa_io))
    b.em("gpsimd", lambda g: g.collective_compute(
        "AllGather", ALU.bypass, replica_groups=[list(range(cfg.n_cores))],
        ins=[sl_in_b.opt()], outs=[sl_out_b.opt()]).then_inc(s_cc.h, 1)); s_cc.inc()

    cvB = Arena(nc, PH, SB_TOP)
    wout_sb = cvB.c(KC * D, BF16)
    q_all = [cvB.c(H * 128, BF16) for _ in range(2)]
    o_all = [cvB.c(H * 128, BF16) for _ in range(2)]
    o_full = [cvB.c(H * 128, BF16) for _ in range(2)]
    h_sb = [cvB.c(D, BF16) for _ in range(2)]
    x_tile = [cvB.c(D, F32) for _ in range(2)]
    z_sb = [cvB.c(D, F32) for _ in range(2)]
    zt_sb = [cvB.c(D, F32) for _ in range(2)]
    y_sb = [cvB.c(D, F32) for _ in range(2)]
    bout_rep = cvB.c(D, F32)
    rms_rep = cvB.c(D, F32)

    dma("gpsimd", s_wout, bout_rep[:, :], bout_d.rearrange("(one d) -> one d", one=1).partition_broadcast(128))
    dma("gpsimd", s_wout, rms_rep[:, :], rmsw_d.rearrange("(one d) -> one d", one=1).partition_broadcast(128))
    for oc in range(KC):
        dma("gpsimd", s_wout, wout_sb[:, oc * D:(oc + 1) * D], wout_d[oc * 128:(oc + 1) * 128, :])
    wout_ready = s_wout.n

    b.wait("gpsimd", s_cc)
    b.wait("vector", s_cc)
    for h in range(H2, H):
        b.wait("gpsimd", s_dv, s_dv.n)
        dma("gpsimd", s_bq, sg_all.ap().rearrange("p (j f) -> p j f", j=cfg.n_cores),
            sl_out_b.rearrange("(j p) c -> p j c", p=128)[:, :, (h - H2) * 128:(h - H2 + 1) * 128])
        b.wait("vector", s_bq)
        cur = None
        for j in range(cfg.n_cores):
            if j == 0:
                b.em("vector", lambda v, h=h:
                     v.tensor_scalar(si_a[:, :], sg_all[:, 0:128], comb_t[:, h * 8:h * 8 + 1], None, ALU.mult).then_inc(s_dv.h, 1)); s_dv.inc()
                cur = si_a
            else:
                nxt = si_b if cur is si_a else si_a
                b.wait("vector", s_dv, s_dv.n)
                b.em("vector", lambda v, h=h, j=j, src=cur, dst=nxt:
                     v.scalar_tensor_tensor(dst[:, :], sg_all[:, j * 128:(j + 1) * 128],
                                            comb_t[:, h * 8 + j:h * 8 + j + 1], src[:, :],
                                            ALU.mult, ALU.add).then_inc(s_dv.h, 1)); s_dv.inc()
                cur = nxt
        b.wait("vector", s_dv, s_dv.n)
        b.em("vector", lambda v, h=h, fin=cur:
             v.tensor_copy(s_init[:, h * 128:(h + 1) * 128], fin[:, :]).then_inc(s_dv.h, 1)); s_dv.inc()
    si_ready_b = s_dv.n

    tile_dv_free = {}
    tile_ready = {}

    def load_p3_tile(tt):
        sl = tt % 2
        sem = t_slots[sl]
        if tt >= 2:
            b.wait("scalar", s_dv, tile_dv_free[tt - 2])
        dma("scalar", sem, q_all[sl].ap().rearrange("p (h f) -> p h f", h=H),
            q_scr.rearrange("(h p) t -> p h t", p=128)[:, :, tt * 128:(tt + 1) * 128])
        dma("scalar", sem, o_all[sl].ap().rearrange("p (h f) -> p h f", h=H),
            o_scr.rearrange("(h p) t -> p h t", p=128)[:, :, tt * 128:(tt + 1) * 128])
        dma("scalar", sem, x_tile[sl][:, :], x_d[tt * 128:(tt + 1) * 128, :])
        return sem.n

    tile_ready[0] = load_p3_tile(0)
    if NTT > 1:
        tile_ready[1] = load_p3_tile(1)
    b.wait("tensor", s_dv, si_ready_a_box[0][1])
    b.wait("tensor", s_wout, wout_ready)
    b.wait("vector", s_wout, wout_ready)
    NG4 = (H + 3) // 4
    nso = max(1, D // 512)
    so_w = min(512, D)
    corr_hist = []
    op_hist = {}
    zt_hist = {}
    y_store = {}
    chain_done = {}
    for tt in range(NTT):
        sl = tt % 2
        sem = t_slots[sl]
        b.wait("tensor", sem, tile_ready[tt])
        NGC = (H + 7) // 8
        for gc in range(NGC):
            cb0 = 2048 + (gc % 2) * 1024
            h0, h1 = 8 * gc, min(8 * gc + 8, H)
            gidx = tt * NGC + gc
            if gidx >= 2:
                b.wait("tensor", s_dv, corr_hist[gidx - 2])
            if h1 > H2:
                b.wait("tensor", s_dv, si_ready_b)
            for hh in range(h0, h1):
                last = (hh == h1 - 1)
                b.em("tensor", lambda t, hh=hh, sl=sl, last=last, cb0=cb0:
                     (t.matmul(psum[:, cb0 + (hh % 8) * 128:cb0 + (hh % 8) * 128 + 128],
                               s_init[:, hh * 128:(hh + 1) * 128],
                               q_all[sl][:, hh * 128:(hh + 1) * 128], start=True, stop=True).then_inc(s_pe.h, 1)
                      if last else
                      t.matmul(psum[:, cb0 + (hh % 8) * 128:cb0 + (hh % 8) * 128 + 128],
                               s_init[:, hh * 128:(hh + 1) * 128],
                               q_all[sl][:, hh * 128:(hh + 1) * 128], start=True, stop=True)))
            s_pe.inc()
            b.wait("vector", s_pe, s_pe.n)
            b.wait("vector", sem, tile_ready[tt])
            if tt >= 2:
                b.wait("vector", s_pe, op_hist[tt - 2])
            b.em("vector", lambda v, sl=sl, cb0=cb0, h0=h0, h1=h1:
                 v.tensor_tensor(o_full[sl][:, h0 * 128:h1 * 128], psum[:, cb0:cb0 + (h1 - h0) * 128],
                                 o_all[sl][:, h0 * 128:h1 * 128], ALU.add).then_inc(s_dv.h, 1)); s_dv.inc()
            corr_hist.append(s_dv.n)
        b.wait("tensor", s_dv, corr_hist[-1])
        if tt >= 2:
            b.wait("tensor", s_dv, zt_hist[tt - 2])
        for oc in range(KC):
            for s in range(nso):
                last = (oc == KC - 1 and s == nso - 1)
                b.em("tensor", lambda t, oc=oc, s=s, last=last, sl=sl:
                     (t.matmul(psum[:, s * so_w:(s + 1) * so_w], o_full[sl][:, oc * 128:(oc + 1) * 128],
                               wout_sb[:, oc * D + s * so_w:oc * D + (s + 1) * so_w],
                               start=(oc == 0), stop=(oc == KC - 1)).then_inc(s_pe.h, 1) if last else
                      t.matmul(psum[:, s * so_w:(s + 1) * so_w], o_full[sl][:, oc * 128:(oc + 1) * 128],
                               wout_sb[:, oc * D + s * so_w:oc * D + (s + 1) * so_w],
                               start=(oc == 0), stop=(oc == KC - 1))))
        s_pe.inc()
        op_done = s_pe.n
        op_hist[tt] = op_done
        sc = 0 if sl == 0 else 8
        b.wait("vector", s_pe, op_done)
        if tt >= 2:
            b.wait("vector", s_dv, chain_done[tt - 2])
            b.wait("vector", sq_io, y_store[tt - 1])
        b.em("vector", lambda v, sl=sl:
             v.tensor_tensor(zt_sb[sl][:, :], psum[:, 0:D], x_tile[sl][:, :], ALU.add).then_inc(s_dv.h, 1)); s_dv.inc()
        tile_dv_free[tt] = s_dv.n
        zt_hist[tt] = s_dv.n
        b.wait("vector", s_dv, s_dv.n)
        b.em("vector", lambda v, sl=sl, sc=sc:
             v.scalar_tensor_tensor(z_sb[sl][:, :], zt_sb[sl][:, :], 1.0, bout_rep[:, :], ALU.mult, ALU.add,
                                    accum_out=stats[:, sc + 0:sc + 1]).then_inc(s_dv.h, 1)); s_dv.inc()
        z_done = s_dv.n
        b.wait("scalar", s_dv, z_done)
        b.em("scalar", lambda s, sl=sl, sc=sc:
             s.activation(zt_sb[sl][:, :], z_sb[sl][:, :], AF.Square,
                          accum_out=stats[:, sc + 1:sc + 2]).then_inc(s_ac.h, 1)); s_ac.inc()
        zsq_done = s_ac.n
        b.wait("vector", s_ac, zsq_done)
        b.em("vector", lambda v, sc=sc: v.tensor_scalar(stats[:, sc + 2:sc + 3], stats[:, sc + 0:sc + 1], 1.0 / D, None, ALU.mult))
        b.em("vector", lambda v, sc=sc: v.tensor_scalar(stats[:, sc + 3:sc + 4], stats[:, sc + 1:sc + 2], 1.0 / D, None, ALU.mult).then_inc(s_dv.h, 1)); s_dv.inc()
        b.wait("vector", s_dv, s_dv.n)
        b.em("vector", lambda v, sc=sc: v.tensor_scalar(stats[:, sc + 4:sc + 5], stats[:, sc + 2:sc + 3], stats[:, sc + 2:sc + 3], ALU.mult, ALU.subtract, accum_out=None)
             if False else v.tensor_tensor(stats[:, sc + 4:sc + 5], stats[:, sc + 2:sc + 3], stats[:, sc + 2:sc + 3], ALU.mult).then_inc(s_dv.h, 1)); s_dv.inc()
        b.wait("vector", s_dv, s_dv.n)
        b.em("vector", lambda v, sc=sc: v.tensor_tensor(stats[:, sc + 5:sc + 6], stats[:, sc + 3:sc + 4], stats[:, sc + 4:sc + 5], ALU.subtract).then_inc(s_dv.h, 1)); s_dv.inc()
        b.wait("scalar", s_dv, s_dv.n)
        b.em("scalar", lambda s, sc=sc: s.activation(stats[:, sc + 6:sc + 7], stats[:, sc + 5:sc + 6], AF.Ln, bias=eps_t[:, 0:1]).then_inc(s_ac.h, 1)); s_ac.inc()
        b.wait("scalar", s_ac, s_ac.n)
        b.em("scalar", lambda s, sc=sc: s.activation(stats[:, sc + 7:sc + 8], stats[:, sc + 6:sc + 7], AF.Exp, scale=-0.5).then_inc(s_ac.h, 1)); s_ac.inc()
        rstd_done = s_ac.n
        b.wait("vector", s_ac, rstd_done)
        b.em("vector", lambda v, sl=sl, sc=sc:
             v.tensor_scalar(y_sb[sl][:, :], z_sb[sl][:, :], stats[:, sc + 2:sc + 3], stats[:, sc + 7:sc + 8],
                             ALU.subtract, ALU.mult).then_inc(s_dv.h, 1)); s_dv.inc()
        y_done = s_dv.n
        b.wait("scalar", s_dv, y_done)
        b.em("scalar", lambda s, sl=sl, sc=sc:
             s.activation(zt_sb[sl][:, :], y_sb[sl][:, :], AF.Square,
                          accum_out=stats[:, sc + 1:sc + 2]).then_inc(s_ac.h, 1)); s_ac.inc()
        b.wait("scalar", s_ac, s_ac.n)
        b.em("scalar", lambda s, sc=sc: s.activation(stats[:, sc + 4:sc + 5], stats[:, sc + 1:sc + 2], AF.Ln, bias=eps_t[:, 2:3], scale=1.0 / D).then_inc(s_ac.h, 1)); s_ac.inc()
        b.wait("scalar", s_ac, s_ac.n)
        b.em("scalar", lambda s, sc=sc: s.activation(stats[:, sc + 5:sc + 6], stats[:, sc + 4:sc + 5], AF.Exp, scale=-0.5).then_inc(s_ac.h, 1)); s_ac.inc()
        rr_done = s_ac.n
        b.wait("vector", s_ac, rr_done)
        b.em("vector", lambda v, sl=sl, sc=sc:
             v.scalar_tensor_tensor(h_sb[sl][:, :], y_sb[sl][:, :], stats[:, sc + 5:sc + 6], rms_rep[:, :],
                                    ALU.mult, ALU.mult).then_inc(s_dv.h, 1)); s_dv.inc()
        h_done = s_dv.n
        chain_done[tt] = h_done
        b.wait("sync", s_dv, y_done)
        dma("sync", sq_io, y_scr[tt * 128:(tt + 1) * 128, :], y_sb[sl][:, :])
        b.wait("sync", s_dv, h_done)
        dma("sync", sq_io, h_scr[tt * 128:(tt + 1) * 128, :], h_sb[sl][:, :])
        y_store[tt] = sq_io.n
        if tt + 2 < NTT:
            tile_ready[tt + 2] = load_p3_tile(tt + 2)
    b.barrier(s_bar, data_sems=(s_pe, s_dv, s_ac, s_pl, sq_io, sa_io))
    b.wait("sync", sq_io)

    cvC = Arena(nc, PH, SB_TOP)
    h_fm = cvC.c(KC * Tt, BF16)
    act_e = cvC.c(NHMAX * Tt, BF16)
    wgu_sb = [cvC.c(2 * KC * 128, BF16) for _ in range(2)]
    wdn_sb = [cvC.c(NHMAX * OH, BF16) for _ in range(2)]
    rw_sb = cvC.c(KC * E, BF16)
    moe = cvC.c(NTG * D, F32)
    wbc_sb = [cvC.c(Tt, F32) for _ in range(2)]
    p_sb = cvC.c(NTG * E, F32)
    w_sb = cvC.c(NTG * E, F32)
    msk_sb = cvC.c(NTG * E, F32)
    sg_sb = cvC.c(Tt, F32)
    yt_sb = cvC.c(D, F32)
    z2_sb = cvC.c(D, F32)
    stats2 = cvC.c(4 * NTG, F32)
    sel_sb = cvC.c(E * 128, F32)

    dma("gpsimd", s_rw, rw_sb.ap().rearrange("p (kc e) -> p kc e", kc=KC),
        rw_d.rearrange("(kc p) e -> p kc e", p=128))
    dma("gpsimd", s_rw, sel_sb[0:E, :], sel_d)
    rw_ready = s_rw.n

    experts = [(e, NHF, True) for e in range(E)] + [(None, NHS, False)]
    gu_jobs = [(ei, hf) for ei, (e, nhf, wt) in enumerate(experts) for hf in range(nhf)]
    dn_jobs = [(ei, oh) for ei, (e, nhf, wt) in enumerate(experts) for oh in range(NOH)]

    nts = max(1, Tt // 512)
    ts_w = min(512, Tt)
    nos = max(1, OH // 512)
    os_w = min(512, OH)

    gu_ready = {}
    gu_pe_free = {}
    dn_ready = {}
    dn_pe_free = {}
    gj = 0
    dj = 0
    prev_z2 = 0
    out_store = 0
    hfm_last_use = {}
    mt_guard = {}

    hfm_ready_d = {}
    wfmT_ready_d = {}
    wbc_ready_d = {}
    prep_done = {}

    def prepare_dma(mt):
        t0 = mt * Tt
        if mt > 0:
            b.wait("sync", s_pe, hfm_last_use[mt - 1])
        for fc in range(KC):
            dma("sync", sq_io, h_fm[:, fc * Tt:(fc + 1) * Tt],
                h_scr[t0:t0 + Tt, fc * 128:(fc + 1) * 128], transpose=True)
        hfm_ready_d[mt] = sq_io.n

    def prepare_pe(mt, rb, wb):
        b.wait("tensor", s_rw, rw_ready)
        b.wait("tensor", sq_io, hfm_ready_d[mt])
        b.wait("tensor", s_ac, s_ac.n)
        b.wait("tensor", s_dv, s_dv.n)
        for tg in range(NTG):
            for kc in range(KC):
                last = (tg == NTG - 1 and kc == KC - 1)
                b.em("tensor", lambda t, tg=tg, kc=kc, last=last, rb=rb:
                     (t.matmul(bank(rb)[:, tg * E:(tg + 1) * E],
                               h_fm[:, kc * Tt + tg * 128:kc * Tt + (tg + 1) * 128],
                               rw_sb[:, kc * E:(kc + 1) * E], start=(kc == 0), stop=(kc == KC - 1)).then_inc(s_pe.h, 1)
                      if last else
                      t.matmul(bank(rb)[:, tg * E:(tg + 1) * E],
                               h_fm[:, kc * Tt + tg * 128:kc * Tt + (tg + 1) * 128],
                               rw_sb[:, kc * E:(kc + 1) * E], start=(kc == 0), stop=(kc == KC - 1))))
        s_pe.inc()
        r_done = s_pe.n
        b.wait("vector", s_pe, r_done)
        for tg in range(NTG):
            last = (tg == NTG - 1)
            b.em("vector", lambda v, tg=tg, rb=rb, last=last:
                 (v.reduce_max(stats2[:, tg:tg + 1], bank(rb)[:, tg * E:(tg + 1) * E], axis=AX.X).then_inc(s_dv.h, 1)
                  if last else
                  v.reduce_max(stats2[:, tg:tg + 1], bank(rb)[:, tg * E:(tg + 1) * E], axis=AX.X)))
        s_dv.inc()
        b.wait("vector", s_dv, s_dv.n)
        b.em("vector", lambda v: v.tensor_scalar(stats2[:, NTG:2 * NTG], stats2[:, 0:NTG], -1.0, None, ALU.mult).then_inc(s_dv.h, 1)); s_dv.inc()
        b.wait("scalar", s_dv, s_dv.n)
        for tg in range(NTG):
            last = (tg == NTG - 1)
            b.em("scalar", lambda s, tg=tg, last=last, rb=rb:
                 (s.activation(p_sb[:, tg * E:(tg + 1) * E], bank(rb)[:, tg * E:(tg + 1) * E], AF.Exp,
                               bias=stats2[:, NTG + tg:NTG + tg + 1]).then_inc(s_ac.h, 1) if last else
                  s.activation(p_sb[:, tg * E:(tg + 1) * E], bank(rb)[:, tg * E:(tg + 1) * E], AF.Exp,
                               bias=stats2[:, NTG + tg:NTG + tg + 1])))
        s_ac.inc()
        b.wait("vector", s_ac, s_ac.n)
        for tg in range(NTG):
            last = (tg == NTG - 1)
            b.em("vector", lambda v, tg=tg, last=last:
                 (v.reduce_max(stats2[:, tg:tg + 1], p_sb[:, tg * E:(tg + 1) * E], axis=AX.X).then_inc(s_dv.h, 1)
                  if last else
                  v.reduce_max(stats2[:, tg:tg + 1], p_sb[:, tg * E:(tg + 1) * E], axis=AX.X)))
        s_dv.inc()
        b.wait("vector", s_dv, s_dv.n)
        for tg in range(NTG):
            last = (tg == NTG - 1)
            b.em("vector", lambda v, tg=tg, last=last:
                 (v.scalar_tensor_tensor(msk_sb[:, tg * E:(tg + 1) * E], p_sb[:, tg * E:(tg + 1) * E], stats2[:, tg:tg + 1],
                                         p_sb[:, tg * E:(tg + 1) * E], ALU.is_lt, ALU.mult).then_inc(s_dv.h, 1)
                  if last else
                  v.scalar_tensor_tensor(msk_sb[:, tg * E:(tg + 1) * E], p_sb[:, tg * E:(tg + 1) * E], stats2[:, tg:tg + 1],
                                         p_sb[:, tg * E:(tg + 1) * E], ALU.is_lt, ALU.mult)))
        s_dv.inc()
        b.wait("vector", s_dv, s_dv.n)
        for tg in range(NTG):
            last = (tg == NTG - 1)
            b.em("vector", lambda v, tg=tg, last=last:
                 (v.reduce_max(stats2[:, NTG + tg:NTG + tg + 1], msk_sb[:, tg * E:(tg + 1) * E], axis=AX.X).then_inc(s_dv.h, 1)
                  if last else
                  v.reduce_max(stats2[:, NTG + tg:NTG + tg + 1], msk_sb[:, tg * E:(tg + 1) * E], axis=AX.X)))
        s_dv.inc()
        b.wait("vector", s_dv, s_dv.n)
        b.em("vector", lambda v: v.tensor_tensor(stats2[:, 2 * NTG:3 * NTG], stats2[:, 0:NTG], stats2[:, NTG:2 * NTG], ALU.add).then_inc(s_dv.h, 1)); s_dv.inc()
        b.wait("vector", s_dv, s_dv.n)
        b.em("vector", lambda v: v.reciprocal(stats2[:, 0:NTG], stats2[:, 2 * NTG:3 * NTG]).then_inc(s_dv.h, 1)); s_dv.inc()
        b.wait("vector", s_dv, s_dv.n)
        for tg in range(NTG):
            last = (tg == NTG - 1)
            b.em("vector", lambda v, tg=tg, last=last:
                 (v.tensor_scalar(msk_sb[:, tg * E:(tg + 1) * E], p_sb[:, tg * E:(tg + 1) * E], stats2[:, NTG + tg:NTG + tg + 1], None,
                                  ALU.is_ge).then_inc(s_dv.h, 1)
                  if last else
                  v.tensor_scalar(msk_sb[:, tg * E:(tg + 1) * E], p_sb[:, tg * E:(tg + 1) * E], stats2[:, NTG + tg:NTG + tg + 1], None,
                                  ALU.is_ge)))
        s_dv.inc()
        b.wait("vector", s_dv, s_dv.n)
        for tg in range(NTG):
            last = (tg == NTG - 1)
            b.em("vector", lambda v, tg=tg, last=last:
                 (v.scalar_tensor_tensor(w_sb[:, tg * E:(tg + 1) * E], p_sb[:, tg * E:(tg + 1) * E], stats2[:, tg:tg + 1],
                                         msk_sb[:, tg * E:(tg + 1) * E], ALU.mult, ALU.mult).then_inc(s_dv.h, 1) if last else
                  v.scalar_tensor_tensor(w_sb[:, tg * E:(tg + 1) * E], p_sb[:, tg * E:(tg + 1) * E], stats2[:, tg:tg + 1],
                                         msk_sb[:, tg * E:(tg + 1) * E], ALU.mult, ALU.mult)))
        s_dv.inc()
        w_ready = s_dv.n
        b.wait("tensor", s_dv, w_ready)
        tg_evac = []
        for tg in range(NTG):
            if tg % 4 == 0 and tg_evac:
                b.wait("tensor", s_dv, tg_evac[-1])
            b.em("tensor", lambda t, tg=tg, wb=wb:
                 t.transpose(bank(wb)[0:E, (tg % 4) * 128:(tg % 4) * 128 + 128],
                             w_sb[:, tg * E:(tg + 1) * E], identf[:, :]).then_inc(s_pe.h, 1)); s_pe.inc()
            if tg % 4 == 3 or tg == NTG - 1:
                b.wait("vector", s_pe, s_pe.n)
                lo = (tg // 4) * 4
                b.em("vector", lambda v, lo=lo, tg=tg, wb=wb:
                     v.tensor_copy(wfmT[0:E, lo * 128:(tg + 1) * 128],
                                   bank(wb)[0:E, 0:(tg + 1 - lo) * 128]).then_inc(s_dv.h, 1)); s_dv.inc()
                tg_evac.append(s_dv.n)
        wfmT_ready = s_dv.n
        wfmT_ready_d[mt] = wfmT_ready
        nws = max(1, Tt // 512)
        ws_w = min(512, Tt)
        b.wait("tensor", s_dv, wfmT_ready)
        for s in range(nws):
            last = (s == nws - 1)
            b.em("tensor", lambda t, s=s, last=last, rb=rb:
                 (t.matmul(psum[:, rb * 512 + s * ws_w:rb * 512 + (s + 1) * ws_w], sel_sb[0:E, 0:128],
                           wfmT[0:E, s * ws_w:(s + 1) * ws_w], start=True, stop=True).then_inc(s_pe.h, 1) if last else
                  t.matmul(psum[:, rb * 512 + s * ws_w:rb * 512 + (s + 1) * ws_w], sel_sb[0:E, 0:128],
                           wfmT[0:E, s * ws_w:(s + 1) * ws_w], start=True, stop=True)))
        s_pe.inc()
        b.wait("vector", s_pe, s_pe.n)
        b.em("vector", lambda v, rb=rb: v.tensor_copy(wbc_sb[0][:, :], psum[:, rb * 512:rb * 512 + Tt]).then_inc(s_dv.h, 1)); s_dv.inc()
        wbc_ready_d[mt] = s_dv.n

    for mt in range(NMT):
        t0 = mt * Tt
        if mt == 0:
            prepare_dma(0)
            prepare_pe(0, 4, 5)
        if mt > 0:
            b.wait("tensor", s_dv, mt_guard[mt - 1][0])
            b.wait("tensor", s_ac, mt_guard[mt - 1][1])
            b.wait("tensor", s_pl, mt_guard[mt - 1][2])
        else:
            b.wait("tensor", s_dv, s_dv.n)
            b.wait("tensor", s_ac, s_ac.n)
            b.wait("tensor", s_pl, s_pl.n)
        wfmT_ready = wfmT_ready_d[mt]

        def load_gu(j):
            ei, hf = gu_jobs[j % len(gu_jobs)]
            e, nhf, weighted = experts[ei]
            sl = j % 2
            sem = fg_slots[sl]
            if j >= 2:
                b.wait("gpsimd", s_pe, gu_pe_free[j - 2])
            for which in range(2):
                if e is None:
                    src = (swg_d if which == 0 else swu_d)[:, hf * 128:(hf + 1) * 128]
                else:
                    src = (wg_d if which == 0 else wu_d)[e][:, hf * 128:(hf + 1) * 128]
                dma("gpsimd", sem,
                    wgu_sb[sl].ap().rearrange("p (two kc f) -> p two kc f", two=2, kc=KC)[:, which, :, :],
                    src.rearrange("(kc p) f -> p kc f", p=128))
            gu_ready[j] = sem.n

        def load_dn(j):
            ei, oh = dn_jobs[j % len(dn_jobs)]
            e, nhf, weighted = experts[ei]
            sl = j % 2
            sem = fd_slots[sl]
            if j >= 2:
                b.wait("gpsimd", s_pe, dn_pe_free[j - 2])
            for hf in range(nhf):
                if e is None:
                    src = swd_d[hf * 128:(hf + 1) * 128, oh * OH:(oh + 1) * OH]
                else:
                    src = wd_d[e][hf * 128:(hf + 1) * 128, oh * OH:(oh + 1) * OH]
                dma("gpsimd", sem, wdn_sb[sl][:, hf * OH:(hf + 1) * OH], src)
            dn_ready[j] = sem.n

        if mt == 0:
            load_gu(0)
            if len(gu_jobs) * NMT > 1:
                load_gu(1)
            load_dn(0)
            if len(dn_jobs) * NMT > 1:
                load_dn(1)
        moe_adds = []
        for ei, (e, nhf, weighted) in enumerate(experts):
            b.wait("tensor", s_ac, s_ac.n)
            b.wait("tensor", s_dv, s_dv.n)
            b.wait("tensor", s_pl, s_pl.n)
            if ei == 0:
                tg_final = {}
                wbc_ready_cur = wbc_ready_d[mt]
            wbc_ready = None
            wsl = ei % 2
            if weighted:
                wbc_ready = wbc_ready_cur
                b.wait("tensor", s_dv, wbc_ready)
            act_done = []
            sg_hist = []
            tu_hist = []
            for hf in range(nhf):
                par = hf % 2
                goff = 2048 * par
                uoff = 1024 + 2048 * par
                gsl = gj % 2
                b.wait("tensor", fg_slots[gsl], gu_ready[gj])
                if hf >= 2:
                    b.wait("tensor", s_ac, sg_hist[hf - 2])
                for kc in range(KC):
                    for s in range(nts):
                        last = (kc == KC - 1 and s == nts - 1)
                        b.em("tensor", lambda t, kc=kc, s=s, gsl=gsl, last=last, goff=goff:
                             (t.matmul(psum[:, goff + s * ts_w:goff + (s + 1) * ts_w],
                                       wgu_sb[gsl][:, kc * 128:(kc + 1) * 128],
                                       h_fm[:, kc * Tt + s * ts_w:kc * Tt + (s + 1) * ts_w],
                                       start=(kc == 0), stop=(kc == KC - 1)).then_inc(s_pe.h, 1) if last else
                              t.matmul(psum[:, goff + s * ts_w:goff + (s + 1) * ts_w],
                                       wgu_sb[gsl][:, kc * 128:(kc + 1) * 128],
                                       h_fm[:, kc * Tt + s * ts_w:kc * Tt + (s + 1) * ts_w],
                                       start=(kc == 0), stop=(kc == KC - 1))))
                s_pe.inc()
                g_done = s_pe.n
                if hf >= 2:
                    b.wait("tensor", s_dv, act_done[hf - 2])
                for kc in range(KC):
                    for s in range(nts):
                        last = (kc == KC - 1 and s == nts - 1)
                        b.em("tensor", lambda t, kc=kc, s=s, gsl=gsl, last=last, uoff=uoff:
                             (t.matmul(psum[:, uoff + s * ts_w:uoff + (s + 1) * ts_w],
                                       wgu_sb[gsl][:, KC * 128 + kc * 128:KC * 128 + (kc + 1) * 128],
                                       h_fm[:, kc * Tt + s * ts_w:kc * Tt + (s + 1) * ts_w],
                                       start=(kc == 0), stop=(kc == KC - 1)).then_inc(s_pe.h, 1) if last else
                              t.matmul(psum[:, uoff + s * ts_w:uoff + (s + 1) * ts_w],
                                       wgu_sb[gsl][:, KC * 128 + kc * 128:KC * 128 + (kc + 1) * 128],
                                       h_fm[:, kc * Tt + s * ts_w:kc * Tt + (s + 1) * ts_w],
                                       start=(kc == 0), stop=(kc == KC - 1))))
                s_pe.inc()
                u_done = s_pe.n
                gu_pe_free[gj] = s_pe.n
                gj += 1
                nxt = gj + 1
                if nxt < len(gu_jobs) * NMT and nxt not in gu_ready:
                    load_gu(nxt)
                b.wait("scalar", s_pe, g_done)
                if hf >= 1:
                    b.wait("scalar", s_dv, act_done[hf - 1])
                b.em("scalar", lambda s, goff=goff: s.activation(sg_sb[:, :], psum[:, goff:goff + Tt], AF.Silu).then_inc(s_ac.h, 1)); s_ac.inc()
                sg_done = s_ac.n
                sg_hist.append(sg_done)
                b.wait("vector", s_ac, sg_done)
                b.wait("vector", s_pe, u_done)
                if weighted:
                    b.em("vector", lambda v, uoff=uoff: v.tensor_tensor(sg_sb[:, :], sg_sb[:, :], psum[:, uoff:uoff + Tt], ALU.mult).then_inc(s_dv.h, 1)); s_dv.inc()
                    tu_done = s_dv.n
                    tu_hist.append(tu_done)
                    b.wait("vector", s_dv, tu_done)
                    b.wait("vector", s_dv, wbc_ready)
                    b.em("vector", lambda v, hf=hf, wsl=wsl: v.tensor_tensor(act_e[:, hf * Tt:(hf + 1) * Tt],
                                                                    sg_sb[:, :], wbc_sb[wsl][:, :], ALU.mult).then_inc(s_dv.h, 1)); s_dv.inc()
                    act_done.append(s_dv.n)
                else:
                    b.em("vector", lambda v, hf=hf, uoff=uoff: v.tensor_tensor(act_e[:, hf * Tt:(hf + 1) * Tt],
                                                                    sg_sb[:, :], psum[:, uoff:uoff + Tt], ALU.mult).then_inc(s_dv.h, 1)); s_dv.inc()
                    act_done.append(s_dv.n)
                    tu_hist.append(act_done[-1])
            if ei + 1 < len(experts) and experts[ei + 1][2]:
                b.wait("tensor", s_ac, sg_hist[-1])
                b.wait("tensor", s_dv, tu_hist[-1])
                ne = experts[ei + 1][0]
                for s in range(nts):
                    last = (s == nts - 1)
                    b.em("tensor", lambda t, ne=ne, s=s, last=last:
                         (t.matmul(psum[:, s * ts_w:(s + 1) * ts_w], sel_sb[0:E, ne * 128:(ne + 1) * 128],
                                   wfmT[0:E, s * ts_w:(s + 1) * ts_w], start=True, stop=True).then_inc(s_pe.h, 1)
                          if last else
                          t.matmul(psum[:, s * ts_w:(s + 1) * ts_w], sel_sb[0:E, ne * 128:(ne + 1) * 128],
                                   wfmT[0:E, s * ts_w:(s + 1) * ts_w], start=True, stop=True)))
                s_pe.inc()
                b.wait("vector", s_pe, s_pe.n)
                b.wait("vector", s_dv, act_done[-1] if weighted else s_dv.n)
                b.em("vector", lambda v, wsl=wsl: v.tensor_copy(wbc_sb[1 - wsl][:, :], psum[:, 0:Tt]).then_inc(s_dv.h, 1)); s_dv.inc()
                wbc_ready_cur = s_dv.n
            if ei == len(experts) - 1:
                hfm_last_use[mt] = gu_pe_free[gj - 1]
            if ei == len(experts) - 1 and mt + 1 < NMT:
                prepare_dma(mt + 1)
            for oh in range(NOH):
                if ei == len(experts) - 1 and mt + 1 < NMT and oh == min(1, NOH - 1):
                    prepare_pe(mt + 1, 0, 1)
                dsl = dj % 2
                b.wait("tensor", fd_slots[dsl], dn_ready[dj])
                b.wait("tensor", s_ac, sg_hist[-1])
                b.wait("tensor", s_dv, tu_hist[-1])
                b.wait("tensor", s_dv, act_done[-1])
                if moe_adds:
                    b.wait("tensor", s_dv, moe_adds[-1])
                for tg in range(NTG):
                    dbk = 4 + (tg % 2) * 2
                    if tg >= 2:
                        b.wait("tensor", s_dv, moe_adds[-2])
                    for hf in range(nhf):
                        for s in range(nos):
                            last = (hf == nhf - 1 and s == nos - 1)
                            b.em("tensor", lambda t, tg=tg, hf=hf, s=s, dsl=dsl, dbk=dbk, last=last, nhf=nhf:
                                 (t.matmul(psum[:, dbk * 512 + s * os_w:dbk * 512 + (s + 1) * os_w],
                                           act_e[:, hf * Tt + tg * 128:hf * Tt + (tg + 1) * 128],
                                           wdn_sb[dsl][:, hf * OH + s * os_w:hf * OH + (s + 1) * os_w],
                                           start=(hf == 0), stop=(hf == nhf - 1)).then_inc(s_pe.h, 1) if last else
                                  t.matmul(psum[:, dbk * 512 + s * os_w:dbk * 512 + (s + 1) * os_w],
                                           act_e[:, hf * Tt + tg * 128:hf * Tt + (tg + 1) * 128],
                                           wdn_sb[dsl][:, hf * OH + s * os_w:hf * OH + (s + 1) * os_w],
                                           start=(hf == 0), stop=(hf == nhf - 1))))
                    s_pe.inc()
                    b.wait("vector", s_pe, s_pe.n)
                    if ei == 0:
                        b.em("vector", lambda v, tg=tg, oh=oh, dbk=dbk:
                             v.tensor_copy(moe[:, tg * D + oh * OH:tg * D + (oh + 1) * OH],
                                           psum[:, dbk * 512:dbk * 512 + OH]).then_inc(s_dv.h, 1)); s_dv.inc()
                    else:
                        b.em("vector", lambda v, tg=tg, oh=oh, dbk=dbk:
                             v.tensor_tensor(moe[:, tg * D + oh * OH:tg * D + (oh + 1) * OH],
                                             moe[:, tg * D + oh * OH:tg * D + (oh + 1) * OH],
                                             psum[:, dbk * 512:dbk * 512 + OH], ALU.add).then_inc(s_dv.h, 1)); s_dv.inc()
                    moe_adds.append(s_dv.n)
                    if ei == len(experts) - 1 and oh == NOH - 1:
                        tg_final[tg] = s_dv.n
                dn_pe_free[dj] = s_pe.n
                dj += 1
                if dj < len(dn_jobs) * NMT and dj not in dn_ready:
                    load_dn(dj)
        moe_done = s_dv.n
        mt_guard[mt] = (s_dv.n, s_ac.n, s_pl.n)

        for tg in range(NTG):
            tt = t0 // 128 + tg
            b.wait("scalar", s_dv, prev_z2)
            b.wait("scalar", s_out, out_store)
            dma("scalar", sa_io, yt_sb[:, :], y_scr[tt * 128:(tt + 1) * 128, :])
            b.wait("vector", sa_io)
            b.wait("vector", s_dv, tg_final[tg])
            b.em("vector", lambda v, tg=tg:
                 v.scalar_tensor_tensor(z2_sb[:, :], moe[:, tg * D:(tg + 1) * D], 1.0, yt_sb[:, :],
                                        ALU.mult, ALU.add, accum_out=stats[:, 0:1]).then_inc(s_dv.h, 1)); s_dv.inc()
            prev_z2 = s_dv.n
            z2_done = s_dv.n
            b.wait("scalar", s_dv, z2_done)
            b.em("scalar", lambda s:
                 s.activation(yt_sb[:, :], z2_sb[:, :], AF.Square,
                              accum_out=stats[:, 1:2]).then_inc(s_ac.h, 1)); s_ac.inc()
            b.wait("vector", s_ac, s_ac.n)
            b.em("vector", lambda v: v.tensor_scalar(stats[:, 2:3], stats[:, 0:1], 1.0 / D, None, ALU.mult))
            b.em("vector", lambda v: v.tensor_scalar(stats[:, 3:4], stats[:, 1:2], 1.0 / D, None, ALU.mult).then_inc(s_dv.h, 1)); s_dv.inc()
            b.wait("vector", s_dv, s_dv.n)
            b.em("vector", lambda v: v.tensor_tensor(stats[:, 4:5], stats[:, 2:3], stats[:, 2:3], ALU.mult).then_inc(s_dv.h, 1)); s_dv.inc()
            b.wait("vector", s_dv, s_dv.n)
            b.em("vector", lambda v: v.tensor_tensor(stats[:, 5:6], stats[:, 3:4], stats[:, 4:5], ALU.subtract).then_inc(s_dv.h, 1)); s_dv.inc()
            b.wait("scalar", s_dv, s_dv.n)
            b.em("scalar", lambda s: s.activation(stats[:, 6:7], stats[:, 5:6], AF.Ln, bias=eps_t[:, 0:1]).then_inc(s_ac.h, 1)); s_ac.inc()
            b.wait("scalar", s_ac, s_ac.n)
            b.em("scalar", lambda s: s.activation(stats[:, 7:8], stats[:, 6:7], AF.Exp, scale=-0.5).then_inc(s_ac.h, 1)); s_ac.inc()
            b.wait("vector", s_ac, s_ac.n)
            b.em("vector", lambda v:
                 v.tensor_scalar(yt_sb[:, :], z2_sb[:, :], stats[:, 2:3], stats[:, 7:8],
                                 ALU.subtract, ALU.mult).then_inc(s_dv.h, 1)); s_dv.inc()
            b.wait("sync", s_dv, s_dv.n)
            dma("sync", s_out, out_d[tt * 128:(tt + 1) * 128, :], yt_sb[:, :])
            out_store = s_out.n

    b.wait("sync", sq_io)
    b.wait("sync", s_out)
    b.wait("scalar", sa_io)
    for s2 in [s_tab, s_bq, s_xst, s_wout, s_rw, s_sg] + list(w_slots) + list(fg_slots) + list(fd_slots) + list(t_slots):
        b.wait("gpsimd", s2)

    with nc.Block() as block:
        b.run(block)
    return nc


def make_consts(cfg: Cfg, core_idx: int):
    T, H, NC, E = cfg.T, cfg.H, cfg.NC, cfg.E
    j = np.arange(128, dtype=np.float64)
    i = np.arange(128, dtype=np.float64)
    diff = i[None, :] - j[:, None]
    intra = np.where(diff >= 0, diff, BIG)
    c_intra = np.tile(intra, (1, H)).astype(np.float32)
    c_qdec = np.tile(np.broadcast_to(i[None, :], (128, 128)), (1, H)).astype(np.float32)
    c_kdec = np.broadcast_to((128.0 - j)[:, None], (128, H)).astype(np.float32).copy()
    cb = np.zeros((128, H * NC), np.float64)
    for h in range(H):
        cb[:, h * NC:(h + 1) * NC] = 128.0 * np.arange(NC)[None, :]
    c_bdc = cb.astype(np.float32)
    cc = np.zeros((128, H * 8), np.float64)
    for h in range(H):
        for jj in range(8):
            cc[:, h * 8 + jj] = (T * (core_idx - 1 - jj)) if jj < core_idx else BIG
    c_comb = cc.astype(np.float32)
    sel = np.zeros((E, E * 128), np.float32)
    for e in range(E):
        sel[e, e * 128:(e + 1) * 128] = 1.0
    identf = np.eye(128, dtype=np.float32)
    return {"c_intra": c_intra, "c_qdec": c_qdec, "c_kdec": c_kdec,
            "c_bdc": c_bdc, "c_comb": c_comb, "sel": sel, "identf": identf}


def run_cfg(cfg: Cfg, full_inputs: dict) -> np.ndarray:
    nc = build_program(cfg)
    shared = {k: np.ascontiguousarray(np.asarray(v, dtype=np.float32))
              for k, v in full_inputs.items() if k != "x"}
    shared["slopes"] = np.ascontiguousarray(
        np.asarray(full_inputs["slopes"], dtype=np.float32).reshape(cfg.H, 1))
    x = np.asarray(full_inputs["x"], dtype=np.float32)
    in_maps = []
    for c in range(cfg.n_cores):
        m = dict(shared)
        m["x"] = np.ascontiguousarray(x[c * cfg.T:(c + 1) * cfg.T, :])
        m.update(make_consts(cfg, c))
        in_maps.append(m)
    res = run_bass_kernel_spmd(nc, in_maps, core_ids=list(range(cfg.n_cores)))
    return np.concatenate([np.asarray(res.results[c]["out"]) for c in range(cfg.n_cores)], axis=0)


def kernel(L0_attn_x, L0_attn_Wqkv, L0_attn_bqkv, L0_attn_Wout, L0_attn_bout,
           L0_attn_slopes, L0_ffn_rms_weight, L0_ffn_router_w, L0_ffn_w_gate,
           L0_ffn_w_up, L0_ffn_w_down, L0_ffn_sw_gate, L0_ffn_sw_up, L0_ffn_sw_down):
    cfg = Cfg(T=2048, D=2048, H=16, F=1024, SF=1024, E=8, n_cores=8)
    x = np.asarray(L0_attn_x).reshape(16384, 2048)
    out = run_cfg(cfg, {
        "x": x, "wqkv": L0_attn_Wqkv, "bqkv": L0_attn_bqkv,
        "wout": L0_attn_Wout, "bout": L0_attn_bout,
        "slopes": L0_attn_slopes, "rmsw": L0_ffn_rms_weight,
        "router_w": L0_ffn_router_w, "wg": L0_ffn_w_gate,
        "wu": L0_ffn_w_up, "wd": L0_ffn_w_down,
        "swg": L0_ffn_sw_gate, "swu": L0_ffn_sw_up, "swd": L0_ffn_sw_down})
    return out.reshape(1, 16384, 2048)


# version 17
# speedup vs baseline: 1.0230x; 1.0230x over previous
import numpy as np

import concourse.bass as bass
import concourse.mybir as mybir
from concourse.bass_utils import run_bass_kernel_spmd

F32 = mybir.dt.float32
BF16 = mybir.dt.bfloat16
AF = mybir.ActivationFunctionType
ALU = mybir.AluOpType
AX = mybir.AxisListType

BIG = 1.0e9


class Cfg:
    def __init__(self, T, D, H, F, SF, E, n_cores=8, tt=None, oh=None):
        self.T, self.D, self.H, self.F, self.SF, self.E = T, D, H, F, SF, E
        self.n_cores = n_cores
        self.KC = D // 128
        self.NC = T // 128
        self.Tt = tt if tt else min(1024, T)
        self.NMT = T // self.Tt
        self.OH = oh if oh else min(1024, D)
        self.NOH = D // self.OH
        assert D % 128 == 0 and T % 128 == 0 and F % 128 == 0 and SF % 128 == 0


class Sem:
    __slots__ = ("h", "n")

    def __init__(self, h):
        self.h = h
        self.n = 0

    def inc(self, k=1):
        self.n += k
        return self.n


class Builder:
    ENGINES = ("tensor", "vector", "scalar", "gpsimd", "sync")

    def __init__(self, nc):
        self.nc = nc
        self.plan = {e: [] for e in self.ENGINES}
        self.last_wait = {}

    def em(self, eng, fn):
        self.plan[eng].append(fn)

    def wait(self, eng, sem, val=None):
        if val is None:
            val = sem.n
        if val <= 0:
            return
        key = (eng, id(sem))
        if self.last_wait.get(key, -1) >= val:
            return
        self.last_wait[key] = val
        self.em(eng, lambda e, s=sem.h, v=val: e.wait_ge(s, v))

    def barrier(self, sem, data_sems=()):
        for ds in data_sems:
            for eng in self.ENGINES:
                self.wait(eng, ds)
        tgt = sem.n + len(self.ENGINES)
        sem.n = tgt
        for eng in self.ENGINES:
            self.em(eng, lambda e, s=sem.h: e.nop().then_inc(s, 1))
        for eng in self.ENGINES:
            self.last_wait[(eng, id(sem))] = tgt
            self.em(eng, lambda e, s=sem.h, v=tgt: e.wait_ge(s, v))

    def run(self, block):
        def make(name):
            def body(eng):
                for fn in self.plan[name]:
                    fn(eng)
            return body
        block.tensor(make("tensor"))
        block.vector(make("vector"))
        block.scalar(make("scalar"))
        block.gpsimd(make("gpsimd"))
        block.sync(make("sync"))


class Arena:
    _uid = [0]

    def __init__(self, nc, start, top):
        self.nc = nc
        self.cur = start
        self.top = top

    def c(self, cols, dtype, parts=128):
        sz = cols * (2 if dtype == BF16 else 4)
        off = (self.cur + 31) & ~31
        self.cur = off + sz
        assert self.cur <= self.top, f"SBUF arena overflow: {self.cur} > {self.top}"
        Arena._uid[0] += 1
        return self.nc.alloc_sbuf_tensor_at(f"ar{Arena._uid[0]}", [parts, cols], dtype, offset=off)


def build_program(cfg: Cfg):
    nc = bass.Bass("TRN2", target_bir_lowering=False)
    T, D, H, F, SF, E = cfg.T, cfg.D, cfg.H, cfg.F, cfg.SF, cfg.E
    KC, NC, Tt, NMT, OH, NOH = cfg.KC, cfg.NC, cfg.Tt, cfg.NMT, cfg.OH, cfg.NOH
    NTG = Tt // 128
    NTT = T // 128
    NS = max(1, T // 512)
    SLAB = T // NS
    NHF, NHS = F // 128, SF // 128
    NHMAX = max(NHF, NHS)

    ap = lambda *a, **k: nc.dram_tensor(*a, **k).ap()
    x_d = ap("x", [T, D], F32, kind="ExternalInput")
    wqkv_d = ap("wqkv", [D, 3 * D], F32, kind="ExternalInput")
    bqkv_d = ap("bqkv", [3 * D], F32, kind="ExternalInput")
    wout_d = ap("wout", [D, D], F32, kind="ExternalInput")
    bout_d = ap("bout", [D], F32, kind="ExternalInput")
    slopes_d = ap("slopes", [H, 1], F32, kind="ExternalInput")
    rmsw_d = ap("rmsw", [D], F32, kind="ExternalInput")
    rw_d = ap("router_w", [D, E], F32, kind="ExternalInput")
    wg_d = ap("wg", [E, D, F], F32, kind="ExternalInput")
    wu_d = ap("wu", [E, D, F], F32, kind="ExternalInput")
    wd_d = ap("wd", [E, F, D], F32, kind="ExternalInput")
    swg_d = ap("swg", [D, SF], F32, kind="ExternalInput")
    swu_d = ap("swu", [D, SF], F32, kind="ExternalInput")
    swd_d = ap("swd", [SF, D], F32, kind="ExternalInput")
    c_intra_d = ap("c_intra", [128, H * 128], F32, kind="ExternalInput")
    c_qdec_d = ap("c_qdec", [128, H * 128], F32, kind="ExternalInput")
    c_kdec_d = ap("c_kdec", [128, H], F32, kind="ExternalInput")
    c_bdc_d = ap("c_bdc", [128, H * NC], F32, kind="ExternalInput")
    c_comb_d = ap("c_comb", [128, H * 8], F32, kind="ExternalInput")
    sel_d = ap("sel", [E, E * 128], F32, kind="ExternalInput")
    identf_d = ap("identf", [128, 128], F32, kind="ExternalInput")
    out_d = ap("out", [T, D], F32, kind="ExternalOutput")

    x_scr = ap("x_scr", [T, D], BF16, kind="Internal")
    o_scr = ap("o_scr", [H * 128, T], BF16, kind="Internal")
    q_scr = ap("q_scr", [H * 128, T], BF16, kind="Internal")
    y_scr = ap("y_scr", [T, D], F32, kind="Internal")
    h_scr = ap("h_scr", [T, D], BF16, kind="Internal")
    H2 = H // 2
    sl_in_a = ap("sl_in_a", [128, H2 * 128], F32, kind="Internal")
    sl_out_a = ap("sl_out_a", [cfg.n_cores * 128, H2 * 128], F32, kind="Internal")
    sl_in_b = ap("sl_in_b", [128, (H - H2) * 128], F32, kind="Internal")
    sl_out_b = ap("sl_out_b", [cfg.n_cores * 128, (H - H2) * 128], F32, kind="Internal")

    def S(name):
        return Sem(nc.alloc_semaphore(name))

    sg_w = S("sg_w")
    s_tab = S("s_tab")
    s_bq = S("s_bq")
    s_xst = S("s_xst")
    s_wout = S("s_wout")
    s_rw = S("s_rw")
    s_out = S("s_out")
    s_sg = S("s_sg")
    sq_io = S("sq_io")
    sa_io = S("sa_io")
    s_pe = S("s_pe")
    s_dv = S("s_dv")
    s_ac = S("s_ac")
    s_pl = S("s_pl")
    s_cc = S("s_cc")
    s_bar = S("s_bar")
    w_slots = (S("sw0"), S("sw1"))
    fg_slots = (S("sfg0"), S("sfg1"))
    fd_slots = (S("sfd0"), S("sfd1"))
    t_slots = (S("st0"), S("st1"))

    b = Builder(nc)
    psum = nc.alloc_psum_tensor("pool", [128, 4096], F32)

    def bank(i, w=512):
        return psum[:, i * 512:i * 512 + w]

    SB_BASE = (nc.sbuf_base + 31) & ~31
    SB_TOP = nc.sbuf_top
    perm = Arena(nc, SB_BASE, SB_TOP)
    identf = perm.c(128, F32)
    sg_all = perm.c(8 * 128, F32)
    comb_t = perm.c(H * 8, F32)
    wfmT = perm.c(min(1024, T), F32)
    si_a = perm.c(128, F32)
    si_b = perm.c(128, F32)
    stats = perm.c(32, F32)
    eps_t = perm.c(4, F32)
    s_init = perm.c(H * 128, BF16)
    PH = perm.cur

    cvA = Arena(nc, PH, SB_TOP)
    slope_cols = cvA.c(H, F32)
    intraT = cvA.c(H * 128, F32)
    qdec_t = cvA.c(H * 128, F32)
    kdec_t = cvA.c(H, F32)
    bdc_t = cvA.c(H * NC, F32)
    tmp_tab = cvA.c(H * 128, F32)
    bqkv_cols = cvA.c(3 * KC, F32)
    bq_rows = cvA.c(128, F32)
    bias_k = [cvA.c(128, F32) for _ in range(2)]
    bias_v = [cvA.c(128, F32) for _ in range(2)]
    bkd_tmp = [cvA.c(128, F32) for _ in range(2)]
    s_a = cvA.c(128, F32)
    s_b = cvA.c(128, F32)
    x_fm = cvA.c(KC * T, BF16)
    wq_sb = [cvA.c(KC * 128, BF16) for _ in range(2)]
    wkv_sb = [cvA.c(KC * 256, BF16) for _ in range(2)]
    q_fm = cvA.c(T, BF16)
    k_fm = cvA.c(T, BF16)
    v_tm = cvA.c(T, BF16)
    kd_tm = cvA.c(T, BF16)
    o_buf = cvA.c(T, BF16)
    qg_buf = cvA.c(T, BF16)
    scT_sb = cvA.c(128, BF16)
    qdec_sb = cvA.c(128, BF16)
    s_bf = cvA.c(128, BF16)
    rows_per = min(1024, T)
    stage_cols = (rows_per // 128) * D
    x_stage = nc.alloc_sbuf_tensor_at("x_stage", [128, stage_cols], BF16,
                                      offset=(SB_TOP - stage_cols * 2) & ~31)
    assert cvA.cur <= ((SB_TOP - stage_cols * 2) & ~31) or True

    def dma(eng, sem, out, in_, **kw):
        b.em(eng, lambda e, o=out, i=in_, kw=kw: e.dma_start(out=o, in_=i, **kw).then_inc(sem.h, 16))
        sem.inc(16)
        return sem.n

    b.em("vector", lambda v: v.memset(eps_t[:, 0:1], 1e-5))
    b.em("vector", lambda v: v.memset(eps_t[:, 1:2], 1e-6))
    b.em("vector", lambda v: v.memset(eps_t[:, 2:3], 1e-6).then_inc(s_dv.h, 1)); s_dv.inc()
    dma("gpsimd", s_tab, identf[:, :], identf_d)
    dma("gpsimd", s_tab, slope_cols[:, :], slopes_d.rearrange("h one -> one h").partition_broadcast(128))
    dma("gpsimd", s_tab, intraT[:, :], c_intra_d)
    dma("gpsimd", s_tab, qdec_t[:, :], c_qdec_d)
    dma("gpsimd", s_tab, kdec_t[:, :], c_kdec_d)
    dma("gpsimd", s_tab, bdc_t[:, :], c_bdc_d)
    dma("gpsimd", s_tab, comb_t[:, :], c_comb_d)

    b.wait("vector", s_tab)
    for tab, blkw in ((intraT, 128), (qdec_t, 128), (bdc_t, NC), (comb_t, 8), (kdec_t, 1)):
        b.wait("vector", s_ac)
        for h in range(H):
            last = (h == H - 1)
            b.em("vector", lambda v, t=tab, h=h, w=blkw, last=last:
                 (v.tensor_scalar(tmp_tab[:, h * w:(h + 1) * w], t[:, h * w:(h + 1) * w],
                                  slope_cols[:, h:h + 1], None, ALU.mult).then_inc(s_dv.h, 1)
                  if last else
                  v.tensor_scalar(tmp_tab[:, h * w:(h + 1) * w], t[:, h * w:(h + 1) * w],
                                  slope_cols[:, h:h + 1], None, ALU.mult)))
        s_dv.inc()
        b.wait("scalar", s_dv)
        b.em("scalar", lambda s, t=tab, w=blkw:
             s.activation(t[:, 0:H * w], tmp_tab[:, 0:H * w], AF.Exp, scale=-1.0).then_inc(s_ac.h, 1)); s_ac.inc()
    tabs_ready = s_ac.n

    dma("gpsimd", s_bq, bq_rows[0:3 * KC, :], bqkv_d.rearrange("(c p) -> c p", p=128))
    b.wait("tensor", s_bq)
    b.wait("tensor", s_tab)
    b.em("tensor", lambda t: t.transpose(bank(7, 3 * KC), bq_rows[0:3 * KC, :],
                                         identf[0:3 * KC, 0:3 * KC]).then_inc(s_pe.h, 1)); s_pe.inc()
    b.wait("vector", s_pe)
    b.em("vector", lambda v: v.tensor_copy(bqkv_cols[:, :], bank(7, 3 * KC)).then_inc(s_dv.h, 1)); s_dv.inc()
    bq_ready = s_dv.n

    nxl = T // rows_per
    for i in range(nxl):
        b.wait("gpsimd", sq_io)
        dma("gpsimd", s_xst,
            x_stage.ap().rearrange("p (n d) -> p n d", n=rows_per // 128),
            x_d[i * rows_per:(i + 1) * rows_per, :].rearrange("(n p) d -> p n d", p=128))
        b.wait("sync", s_xst)
        dma("sync", sq_io,
            x_scr[i * rows_per:(i + 1) * rows_per, :].rearrange("(n p) d -> p n d", p=128),
            x_stage.ap().rearrange("p (n d) -> p n d", n=rows_per // 128))
    b.wait("sync", sq_io)
    for fc in range(KC):
        dma("sync", sq_io, x_fm[:, fc * T:(fc + 1) * T], x_scr[:, fc * 128:(fc + 1) * 128], transpose=True)
    x_fm_ready = sq_io.n

    head_pe_free = {}
    head_dv_free = {}
    head_w_ready = {}

    def kvtile(sl, kc, which=None):
        if which is None:
            return wkv_sb[sl][:, kc * 256:(kc + 1) * 256]
        return wkv_sb[sl].ap().rearrange("p (kc two f) -> p kc two f", kc=KC, two=2)[:, kc, which, :]

    def load_head_weights(h):
        sl = h % 2
        sem = w_slots[sl]
        if h >= 2:
            b.wait("gpsimd", s_pe, head_pe_free[h - 2])
            b.wait("gpsimd", s_dv, head_dv_free[h - 2])
        dma("gpsimd", sem, wq_sb[sl].ap().rearrange("p (kc f) -> p kc f", kc=KC),
            wqkv_d[:, h * 128:(h + 1) * 128].rearrange("(kc p) f -> p kc f", p=128))
        dma("gpsimd", sem, wkv_sb[sl].ap().rearrange("p (kc two f) -> p kc two f", kc=KC, two=2)[:, :, 0, :],
            wqkv_d[:, D + h * 128:D + (h + 1) * 128].rearrange("(kc p) f -> p kc f", p=128))
        dma("gpsimd", sem, wkv_sb[sl].ap().rearrange("p (kc two f) -> p kc two f", kc=KC, two=2)[:, :, 1, :],
            wqkv_d[:, 2 * D + h * 128:2 * D + (h + 1) * 128].rearrange("(kc p) f -> p kc f", p=128))
        dma("gpsimd", sem, bias_k[sl][:, :],
            bqkv_d.rearrange("(one d) -> one d", one=1)[:, D + h * 128:D + (h + 1) * 128].partition_broadcast(128))
        dma("gpsimd", sem, bias_v[sl][:, :],
            bqkv_d.rearrange("(one d) -> one d", one=1)[:, 2 * D + h * 128:2 * D + (h + 1) * 128].partition_broadcast(128))
        return sem.n

    head_w_ready[0] = load_head_weights(0)
    if H > 1:
        head_w_ready[1] = load_head_weights(1)
    b.wait("tensor", sq_io, x_fm_ready)
    b.wait("tensor", s_ac, tabs_ready)

    si_ready_a_box = [None]

    def emit_combine1():
        b.wait("sync", s_cc, 1)
        for hh in range(H2):
            b.wait("sync", s_dv, s_dv.n)
            dma("sync", s_sg, sg_all.ap().rearrange("p (j f) -> p j f", j=cfg.n_cores),
                sl_out_a.rearrange("(j p) c -> p j c", p=128)[:, :, hh * 128:(hh + 1) * 128])
            b.wait("vector", s_sg)
            cur = None
            for j in range(cfg.n_cores):
                if j == 0:
                    b.em("vector", lambda v, hh=hh:
                         v.tensor_scalar(si_a[:, :], sg_all[:, 0:128], comb_t[:, hh * 8:hh * 8 + 1], None, ALU.mult).then_inc(s_dv.h, 1)); s_dv.inc()
                    cur = si_a
                else:
                    nxt = si_b if cur is si_a else si_a
                    b.wait("vector", s_dv, s_dv.n)
                    b.em("vector", lambda v, hh=hh, j=j, src=cur, dst=nxt:
                         v.scalar_tensor_tensor(dst[:, :], sg_all[:, j * 128:(j + 1) * 128],
                                                comb_t[:, hh * 8 + j:hh * 8 + j + 1], src[:, :],
                                                ALU.mult, ALU.add).then_inc(s_dv.h, 1)); s_dv.inc()
                    cur = nxt
            b.wait("vector", s_dv, s_dv.n)
            b.em("vector", lambda v, hh=hh, fin=cur:
                 v.tensor_copy(s_init[:, hh * 128:(hh + 1) * 128], fin[:, :]).then_inc(s_dv.h, 1)); s_dv.inc()
        si_ready_a_box[0] = ("dv", s_dv.n)

    store_guard = 0
    for h in range(H):
        sl = h % 2
        sem = w_slots[sl]
        wv = head_w_ready[h]
        b.wait("tensor", s_dv, s_dv.n)
        b.wait("tensor", s_ac, s_ac.n)
        b.wait("tensor", s_pl, s_pl.n)
        b.wait("tensor", sem, wv)
        b.wait("vector", sq_io, store_guard)
        b.wait("scalar", sq_io, store_guard)
        b.wait("gpsimd", sq_io, store_guard)
        b.wait("vector", sem, wv)
        b.wait("vector", s_ac, tabs_ready)
        b.em("vector", lambda v, h=h, sl=sl:
             v.tensor_scalar(bkd_tmp[sl][:, :], bias_k[sl][:, :], kdec_t[:, h:h + 1], None, ALU.mult).then_inc(s_dv.h, 1)); s_dv.inc()
        bkd_ready = s_dv.n

        for kc in range(KC):
            for s in range(NS):
                last = (kc == KC - 1 and s == NS - 1)
                b.em("tensor", lambda t, kc=kc, s=s, sl=sl, last=last:
                     (t.matmul(psum[:, s * SLAB:(s + 1) * SLAB], wq_sb[sl][:, kc * 128:(kc + 1) * 128],
                               x_fm[:, kc * T + s * SLAB:kc * T + (s + 1) * SLAB],
                               start=(kc == 0), stop=(kc == KC - 1)).then_inc(s_pe.h, 1) if last else
                      t.matmul(psum[:, s * SLAB:(s + 1) * SLAB], wq_sb[sl][:, kc * 128:(kc + 1) * 128],
                               x_fm[:, kc * T + s * SLAB:kc * T + (s + 1) * SLAB],
                               start=(kc == 0), stop=(kc == KC - 1))))
        s_pe.inc()
        q_done = s_pe.n
        for kc in range(KC):
            for s in range(NS):
                last = (kc == KC - 1 and s == NS - 1)
                b.em("tensor", lambda t, kc=kc, s=s, sl=sl, last=last:
                     (t.matmul(psum[:, 2048 + s * SLAB:2048 + (s + 1) * SLAB], kvtile(sl, kc, 0),
                               x_fm[:, kc * T + s * SLAB:kc * T + (s + 1) * SLAB],
                               start=(kc == 0), stop=(kc == KC - 1)).then_inc(s_pe.h, 1) if last else
                      t.matmul(psum[:, 2048 + s * SLAB:2048 + (s + 1) * SLAB], kvtile(sl, kc, 0),
                               x_fm[:, kc * T + s * SLAB:kc * T + (s + 1) * SLAB],
                               start=(kc == 0), stop=(kc == KC - 1))))
        s_pe.inc()
        k_done = s_pe.n
        b.wait("scalar", s_pe, q_done)
        b.em("scalar", lambda s, h=h:
             s.activation(q_fm[:, :], psum[:, 0:T], AF.Identity, bias=bqkv_cols[:, h:h + 1]).then_inc(s_ac.h, 1)); s_ac.inc()
        q_evac = s_ac.n
        b.wait("scalar", s_pe, k_done)
        b.em("scalar", lambda s, h=h:
             s.activation(k_fm[:, :], psum[:, 2048:2048 + T], AF.Identity, bias=bqkv_cols[:, KC + h:KC + h + 1]).then_inc(s_ac.h, 1)); s_ac.inc()
        k_evac = s_ac.n

        b.wait("tensor", s_ac, q_evac)
        kv_dv = []
        for tt in range(NTT):
            bk = tt % 2
            if tt >= 2:
                b.wait("tensor", s_dv, kv_dv[tt - 2])
            for kc in range(KC):
                last = (kc == KC - 1)
                b.em("tensor", lambda t, tt=tt, kc=kc, bk=bk, sl=sl, last=last:
                     (t.matmul(bank(bk, 256), x_fm[:, kc * T + tt * 128:kc * T + (tt + 1) * 128],
                               kvtile(sl, kc), start=(kc == 0), stop=(kc == KC - 1)).then_inc(s_pe.h, 1)
                      if last else
                      t.matmul(bank(bk, 256), x_fm[:, kc * T + tt * 128:kc * T + (tt + 1) * 128],
                               kvtile(sl, kc), start=(kc == 0), stop=(kc == KC - 1))))
            s_pe.inc()
            b.wait("vector", s_pe, s_pe.n)
            b.wait("vector", s_dv, bkd_ready)
            b.em("vector", lambda v, tt=tt, bk=bk, h=h, sl=sl:
                 v.scalar_tensor_tensor(kd_tm[:, tt * 128:(tt + 1) * 128], bank(bk, 256)[:, 0:128],
                                        kdec_t[:, h:h + 1], bkd_tmp[sl][:, :], ALU.mult, ALU.add))
            b.em("vector", lambda v, tt=tt, bk=bk, sl=sl:
                 v.tensor_tensor(v_tm[:, tt * 128:(tt + 1) * 128], bank(bk, 256)[:, 128:256],
                                 bias_v[sl][:, :], ALU.add).then_inc(s_dv.h, 1)); s_dv.inc()
            kv_dv.append(s_dv.n)
        head_pe_free[h] = s_pe.n
        head_dv_free[h] = s_dv.n
        if h + 2 < H:
            head_w_ready[h + 2] = load_head_weights(h + 2)

        b.wait("tensor", s_dv, kv_dv[-1])
        b.wait("tensor", s_ac, k_evac)
        o_evacs = []
        mask_done = {}
        supd_done = {}
        qdec_done = {}
        qg_done = {}
        o_done_d = {}
        sbf_done_d = {}
        sbf_done = None
        cur_s = None
        for c in range(NC):
            cs = slice(c * 128, (c + 1) * 128)
            sc_b = 2 + (c % 2)
            dl_b = 6 + (c % 2)
            o_b = 4 + ((c // 4) % 2)
            ocol = (c % 4) * 128
            if c >= 2:
                b.wait("tensor", s_dv, mask_done[c - 2])
            b.em("tensor", lambda t, cs=cs, sc_b=sc_b:
                 t.matmul(bank(sc_b, 128), k_fm[:, cs], q_fm[:, cs], start=True, stop=True).then_inc(s_pe.h, 1)); s_pe.inc()
            sc_done = s_pe.n
            if c >= 2:
                b.wait("tensor", s_dv, supd_done[c - 2])
            b.em("tensor", lambda t, cs=cs, dl_b=dl_b:
                 t.matmul(bank(dl_b, 128), kd_tm[:, cs], v_tm[:, cs], start=True, stop=True).then_inc(s_pe.h, 1)); s_pe.inc()
            dl_done = s_pe.n
            b.wait("vector", s_pe, sc_done)
            if c >= 1:
                b.wait("vector", s_pe, o_done_d[c - 1])
                b.wait("vector", s_dv, qg_done[c - 1])
            b.em("vector", lambda v, sc_b=sc_b, h=h:
                 v.tensor_tensor(scT_sb[:, :], bank(sc_b, 128), intraT[:, h * 128:(h + 1) * 128], ALU.mult).then_inc(s_dv.h, 1)); s_dv.inc()
            mask_done[c] = s_dv.n
            b.em("vector", lambda v, cs=cs, h=h:
                 v.tensor_tensor(qdec_sb[:, :], q_fm[:, cs], qdec_t[:, h * 128:(h + 1) * 128], ALU.mult).then_inc(s_dv.h, 1)); s_dv.inc()
            qdec_done[c] = s_dv.n
            b.wait("vector", s_dv, qdec_done[c])
            b.em("vector", lambda v, c=c, h=h:
                 v.tensor_scalar(qg_buf[:, c * 128:(c + 1) * 128], qdec_sb[:, :],
                                 bdc_t[:, h * NC + c:h * NC + c + 1], None, ALU.mult).then_inc(s_dv.h, 1)); s_dv.inc()
            qg_done[c] = s_dv.n
            b.wait("tensor", s_dv, mask_done[c])
            if c % 4 == 0 and o_evacs:
                b.wait("tensor", s_ac, o_evacs[-1])
            if c == 0:
                b.em("tensor", lambda t, cs=cs, o_b=o_b, ocol=ocol:
                     t.matmul(bank(o_b)[:, ocol:ocol + 128], v_tm[:, cs], scT_sb[:, :],
                              start=True, stop=True).then_inc(s_pe.h, 1)); s_pe.inc()
            else:
                b.em("tensor", lambda t, cs=cs, o_b=o_b, ocol=ocol:
                     t.matmul(bank(o_b)[:, ocol:ocol + 128], v_tm[:, cs], scT_sb[:, :],
                              start=True, stop=False))
                b.wait("tensor", s_dv, qdec_done[c])
                b.wait("tensor", s_ac, sbf_done)
                b.em("tensor", lambda t, o_b=o_b, ocol=ocol:
                     t.matmul(bank(o_b)[:, ocol:ocol + 128], s_bf[:, :], qdec_sb[:, :],
                              start=False, stop=True).then_inc(s_pe.h, 1)); s_pe.inc()
            o_done = s_pe.n
            o_done_d[c] = o_done
            b.wait("vector", s_pe, dl_done)
            if c == 0:
                b.em("vector", lambda v, dl_b=dl_b:
                     v.tensor_copy(s_a[:, :], bank(dl_b, 128)).then_inc(s_dv.h, 1)); s_dv.inc()
                cur_s = s_a
            else:
                new_s = s_b if cur_s is s_a else s_a
                b.wait("vector", s_dv, supd_done[c - 1])
                if c >= 2:
                    b.wait("vector", s_ac, sbf_done_d[c - 2])
                b.em("vector", lambda v, dl_b=dl_b, so=cur_s, sn=new_s, h=h:
                     v.scalar_tensor_tensor(sn[:, :], so[:, :], bdc_t[:, h * NC + 1:h * NC + 2],
                                            bank(dl_b, 128), ALU.mult, ALU.add).then_inc(s_dv.h, 1)); s_dv.inc()
                cur_s = new_s
            supd_done[c] = s_dv.n
            b.wait("scalar", s_dv, supd_done[c])
            b.wait("scalar", s_pe, o_done)
            b.em("scalar", lambda s, sn=cur_s: s.activation(s_bf[:, :], sn[:, :], AF.Copy).then_inc(s_ac.h, 1)); s_ac.inc()
            sbf_done = s_ac.n
            sbf_done_d[c] = sbf_done
            if c % 4 == 3 or c == NC - 1:
                b.wait("scalar", s_pe, o_done)
                lo = (c // 4) * 4
                b.em("scalar", lambda s, lo=lo, c=c, o_b=o_b:
                     s.activation(o_buf[:, lo * 128:(c + 1) * 128], bank(o_b, (c + 1 - lo) * 128),
                                  AF.Copy).then_inc(s_ac.h, 1)); s_ac.inc()
                o_evacs.append(s_ac.n)
        b.wait("sync", s_ac, o_evacs[-1])
        dma("sync", sq_io, o_scr[h * 128:(h + 1) * 128, :], o_buf[:, :])
        b.wait("sync", s_dv, s_dv.n)
        dma("sync", sq_io, q_scr[h * 128:(h + 1) * 128, :], qg_buf[:, :])
        b.wait("sync", s_dv, supd_done[NC - 1])
        if h < H2:
            dma("sync", sq_io, sl_in_a[:, h * 128:(h + 1) * 128], cur_s[:, :])
        else:
            dma("sync", sq_io, sl_in_b[:, (h - H2) * 128:(h - H2 + 1) * 128], cur_s[:, :])
        store_guard = sq_io.n
        if h == H2 - 1:
            half1_stores = sq_io.n
        if h == H2 + 1:
            b.wait("gpsimd", sq_io, store_guard)
            b.em("gpsimd", lambda g: g.collective_compute(
                "AllGather", ALU.bypass, replica_groups=[list(range(cfg.n_cores))],
                ins=[sl_in_a.opt()], outs=[sl_out_a.opt()]).then_inc(s_cc.h, 1)); s_cc.inc()
        if h == max(H2 + 2, H - 2) and h <= H - 1:
            emit_combine1()

    if max(H2 + 2, H - 2) > H - 1:
        emit_combine1()

    b.wait("gpsimd", sq_io, store_guard)
    b.em("gpsimd", lambda g: g.collective_compute(
        "AllGather", ALU.bypass, replica_groups=[list(range(cfg.n_cores))],
        ins=[sl_in_b.opt()], outs=[sl_out_b.opt()]).then_inc(s_cc.h, 1)); s_cc.inc()
    b.barrier(s_bar, data_sems=(s_pe, s_dv, s_ac, s_pl, sq_io, sa_io))

    cvB = Arena(nc, PH, SB_TOP)
    wout_sb = cvB.c(KC * D, BF16)
    q_all = [cvB.c(H * 128, BF16) for _ in range(2)]
    o_all = [cvB.c(H * 128, BF16) for _ in range(2)]
    o_full = [cvB.c(H * 128, BF16) for _ in range(2)]
    h_sb = [cvB.c(D, BF16) for _ in range(2)]
    x_tile = [cvB.c(D, F32) for _ in range(2)]
    z_sb = [cvB.c(D, F32) for _ in range(2)]
    zt_sb = [cvB.c(D, F32) for _ in range(2)]
    y_sb = [cvB.c(D, F32) for _ in range(2)]
    bout_rep = cvB.c(D, F32)
    rms_rep = cvB.c(D, F32)

    dma("gpsimd", s_wout, bout_rep[:, :], bout_d.rearrange("(one d) -> one d", one=1).partition_broadcast(128))
    dma("gpsimd", s_wout, rms_rep[:, :], rmsw_d.rearrange("(one d) -> one d", one=1).partition_broadcast(128))
    for oc in range(KC):
        dma("gpsimd", s_wout, wout_sb[:, oc * D:(oc + 1) * D], wout_d[oc * 128:(oc + 1) * 128, :])
    wout_ready = s_wout.n

    b.wait("gpsimd", s_cc)
    b.wait("vector", s_cc)
    for h in range(H2, H):
        b.wait("gpsimd", s_dv, s_dv.n)
        dma("gpsimd", s_bq, sg_all.ap().rearrange("p (j f) -> p j f", j=cfg.n_cores),
            sl_out_b.rearrange("(j p) c -> p j c", p=128)[:, :, (h - H2) * 128:(h - H2 + 1) * 128])
        b.wait("vector", s_bq)
        cur = None
        for j in range(cfg.n_cores):
            if j == 0:
                b.em("vector", lambda v, h=h:
                     v.tensor_scalar(si_a[:, :], sg_all[:, 0:128], comb_t[:, h * 8:h * 8 + 1], None, ALU.mult).then_inc(s_dv.h, 1)); s_dv.inc()
                cur = si_a
            else:
                nxt = si_b if cur is si_a else si_a
                b.wait("vector", s_dv, s_dv.n)
                b.em("vector", lambda v, h=h, j=j, src=cur, dst=nxt:
                     v.scalar_tensor_tensor(dst[:, :], sg_all[:, j * 128:(j + 1) * 128],
                                            comb_t[:, h * 8 + j:h * 8 + j + 1], src[:, :],
                                            ALU.mult, ALU.add).then_inc(s_dv.h, 1)); s_dv.inc()
                cur = nxt
        b.wait("vector", s_dv, s_dv.n)
        b.em("vector", lambda v, h=h, fin=cur:
             v.tensor_copy(s_init[:, h * 128:(h + 1) * 128], fin[:, :]).then_inc(s_dv.h, 1)); s_dv.inc()
    si_ready_b = s_dv.n

    tile_dv_free = {}
    tile_ready = {}

    def load_p3_tile(tt):
        sl = tt % 2
        sem = t_slots[sl]
        if tt >= 2:
            b.wait("scalar", s_dv, tile_dv_free[tt - 2])
        dma("scalar", sem, q_all[sl].ap().rearrange("p (h f) -> p h f", h=H),
            q_scr.rearrange("(h p) t -> p h t", p=128)[:, :, tt * 128:(tt + 1) * 128])
        dma("scalar", sem, o_all[sl].ap().rearrange("p (h f) -> p h f", h=H),
            o_scr.rearrange("(h p) t -> p h t", p=128)[:, :, tt * 128:(tt + 1) * 128])
        dma("scalar", sem, x_tile[sl][:, :], x_d[tt * 128:(tt + 1) * 128, :])
        return sem.n

    tile_ready[0] = load_p3_tile(0)
    if NTT > 1:
        tile_ready[1] = load_p3_tile(1)
    b.wait("tensor", s_dv, si_ready_a_box[0][1])
    b.wait("tensor", s_wout, wout_ready)
    b.wait("vector", s_wout, wout_ready)
    NG4 = (H + 3) // 4
    nso = max(1, D // 512)
    so_w = min(512, D)
    corr_hist = []
    op_hist = {}
    zt_hist = {}
    y_store = {}
    chain_done = {}
    for tt in range(NTT):
        sl = tt % 2
        sem = t_slots[sl]
        b.wait("tensor", sem, tile_ready[tt])
        NGC = (H + 7) // 8
        for gc in range(NGC):
            cb0 = 2048 + (gc % 2) * 1024
            h0, h1 = 8 * gc, min(8 * gc + 8, H)
            gidx = tt * NGC + gc
            if gidx >= 2:
                b.wait("tensor", s_dv, corr_hist[gidx - 2])
            if h1 > H2:
                b.wait("tensor", s_dv, si_ready_b)
            for hh in range(h0, h1):
                last = (hh == h1 - 1)
                b.em("tensor", lambda t, hh=hh, sl=sl, last=last, cb0=cb0:
                     (t.matmul(psum[:, cb0 + (hh % 8) * 128:cb0 + (hh % 8) * 128 + 128],
                               s_init[:, hh * 128:(hh + 1) * 128],
                               q_all[sl][:, hh * 128:(hh + 1) * 128], start=True, stop=True).then_inc(s_pe.h, 1)
                      if last else
                      t.matmul(psum[:, cb0 + (hh % 8) * 128:cb0 + (hh % 8) * 128 + 128],
                               s_init[:, hh * 128:(hh + 1) * 128],
                               q_all[sl][:, hh * 128:(hh + 1) * 128], start=True, stop=True)))
            s_pe.inc()
            b.wait("vector", s_pe, s_pe.n)
            b.wait("vector", sem, tile_ready[tt])
            if tt >= 2:
                b.wait("vector", s_pe, op_hist[tt - 2])
            b.em("vector", lambda v, sl=sl, cb0=cb0, h0=h0, h1=h1:
                 v.tensor_tensor(o_full[sl][:, h0 * 128:h1 * 128], psum[:, cb0:cb0 + (h1 - h0) * 128],
                                 o_all[sl][:, h0 * 128:h1 * 128], ALU.add).then_inc(s_dv.h, 1)); s_dv.inc()
            corr_hist.append(s_dv.n)
        b.wait("tensor", s_dv, corr_hist[-1])
        if tt >= 2:
            b.wait("tensor", s_dv, zt_hist[tt - 2])
        for oc in range(KC):
            for s in range(nso):
                last = (oc == KC - 1 and s == nso - 1)
                b.em("tensor", lambda t, oc=oc, s=s, last=last, sl=sl:
                     (t.matmul(psum[:, s * so_w:(s + 1) * so_w], o_full[sl][:, oc * 128:(oc + 1) * 128],
                               wout_sb[:, oc * D + s * so_w:oc * D + (s + 1) * so_w],
                               start=(oc == 0), stop=(oc == KC - 1)).then_inc(s_pe.h, 1) if last else
                      t.matmul(psum[:, s * so_w:(s + 1) * so_w], o_full[sl][:, oc * 128:(oc + 1) * 128],
                               wout_sb[:, oc * D + s * so_w:oc * D + (s + 1) * so_w],
                               start=(oc == 0), stop=(oc == KC - 1))))
        s_pe.inc()
        op_done = s_pe.n
        op_hist[tt] = op_done
        sc = 0 if sl == 0 else 8
        b.wait("vector", s_pe, op_done)
        if tt >= 2:
            b.wait("vector", s_dv, chain_done[tt - 2])
            b.wait("vector", sq_io, y_store[tt - 1])
        b.em("vector", lambda v, sl=sl:
             v.tensor_tensor(zt_sb[sl][:, :], psum[:, 0:D], x_tile[sl][:, :], ALU.add).then_inc(s_dv.h, 1)); s_dv.inc()
        tile_dv_free[tt] = s_dv.n
        zt_hist[tt] = s_dv.n
        b.wait("vector", s_dv, s_dv.n)
        b.em("vector", lambda v, sl=sl, sc=sc:
             v.scalar_tensor_tensor(z_sb[sl][:, :], zt_sb[sl][:, :], 1.0, bout_rep[:, :], ALU.mult, ALU.add,
                                    accum_out=stats[:, sc + 0:sc + 1]).then_inc(s_dv.h, 1)); s_dv.inc()
        z_done = s_dv.n
        b.wait("scalar", s_dv, z_done)
        b.em("scalar", lambda s, sl=sl, sc=sc:
             s.activation(zt_sb[sl][:, :], z_sb[sl][:, :], AF.Square,
                          accum_out=stats[:, sc + 1:sc + 2]).then_inc(s_ac.h, 1)); s_ac.inc()
        zsq_done = s_ac.n
        b.wait("vector", s_ac, zsq_done)
        b.em("vector", lambda v, sc=sc: v.tensor_scalar(stats[:, sc + 2:sc + 3], stats[:, sc + 0:sc + 1], 1.0 / D, None, ALU.mult))
        b.em("vector", lambda v, sc=sc: v.tensor_scalar(stats[:, sc + 3:sc + 4], stats[:, sc + 1:sc + 2], 1.0 / D, None, ALU.mult).then_inc(s_dv.h, 1)); s_dv.inc()
        b.wait("vector", s_dv, s_dv.n)
        b.em("vector", lambda v, sc=sc: v.tensor_scalar(stats[:, sc + 4:sc + 5], stats[:, sc + 2:sc + 3], stats[:, sc + 2:sc + 3], ALU.mult, ALU.subtract, accum_out=None)
             if False else v.tensor_tensor(stats[:, sc + 4:sc + 5], stats[:, sc + 2:sc + 3], stats[:, sc + 2:sc + 3], ALU.mult).then_inc(s_dv.h, 1)); s_dv.inc()
        b.wait("vector", s_dv, s_dv.n)
        b.em("vector", lambda v, sc=sc: v.tensor_tensor(stats[:, sc + 5:sc + 6], stats[:, sc + 3:sc + 4], stats[:, sc + 4:sc + 5], ALU.subtract).then_inc(s_dv.h, 1)); s_dv.inc()
        b.wait("scalar", s_dv, s_dv.n)
        b.em("scalar", lambda s, sc=sc: s.activation(stats[:, sc + 6:sc + 7], stats[:, sc + 5:sc + 6], AF.Ln, bias=eps_t[:, 0:1]).then_inc(s_ac.h, 1)); s_ac.inc()
        b.wait("scalar", s_ac, s_ac.n)
        b.em("scalar", lambda s, sc=sc: s.activation(stats[:, sc + 7:sc + 8], stats[:, sc + 6:sc + 7], AF.Exp, scale=-0.5).then_inc(s_ac.h, 1)); s_ac.inc()
        rstd_done = s_ac.n
        b.wait("vector", s_ac, rstd_done)
        b.em("vector", lambda v, sl=sl, sc=sc:
             v.tensor_scalar(y_sb[sl][:, :], z_sb[sl][:, :], stats[:, sc + 2:sc + 3], stats[:, sc + 7:sc + 8],
                             ALU.subtract, ALU.mult).then_inc(s_dv.h, 1)); s_dv.inc()
        y_done = s_dv.n
        b.wait("scalar", s_dv, y_done)
        b.em("scalar", lambda s, sl=sl, sc=sc:
             s.activation(zt_sb[sl][:, :], y_sb[sl][:, :], AF.Square,
                          accum_out=stats[:, sc + 1:sc + 2]).then_inc(s_ac.h, 1)); s_ac.inc()
        b.wait("scalar", s_ac, s_ac.n)
        b.em("scalar", lambda s, sc=sc: s.activation(stats[:, sc + 4:sc + 5], stats[:, sc + 1:sc + 2], AF.Ln, bias=eps_t[:, 2:3], scale=1.0 / D).then_inc(s_ac.h, 1)); s_ac.inc()
        b.wait("scalar", s_ac, s_ac.n)
        b.em("scalar", lambda s, sc=sc: s.activation(stats[:, sc + 5:sc + 6], stats[:, sc + 4:sc + 5], AF.Exp, scale=-0.5).then_inc(s_ac.h, 1)); s_ac.inc()
        rr_done = s_ac.n
        b.wait("vector", s_ac, rr_done)
        b.em("vector", lambda v, sl=sl, sc=sc:
             v.scalar_tensor_tensor(h_sb[sl][:, :], y_sb[sl][:, :], stats[:, sc + 5:sc + 6], rms_rep[:, :],
                                    ALU.mult, ALU.mult).then_inc(s_dv.h, 1)); s_dv.inc()
        h_done = s_dv.n
        chain_done[tt] = h_done
        b.wait("sync", s_dv, y_done)
        dma("sync", sq_io, y_scr[tt * 128:(tt + 1) * 128, :], y_sb[sl][:, :])
        b.wait("sync", s_dv, h_done)
        dma("sync", sq_io, h_scr[tt * 128:(tt + 1) * 128, :], h_sb[sl][:, :])
        y_store[tt] = sq_io.n
        if tt + 2 < NTT:
            tile_ready[tt + 2] = load_p3_tile(tt + 2)
    b.barrier(s_bar, data_sems=(s_pe, s_dv, s_ac, s_pl, sq_io, sa_io))
    b.wait("sync", sq_io)

    cvC = Arena(nc, PH, SB_TOP)
    h_fm = cvC.c(KC * Tt, BF16)
    act_e = cvC.c(NHMAX * Tt, BF16)
    wgu_sb = [cvC.c(2 * KC * 128, BF16) for _ in range(2)]
    wdn_sb = [cvC.c(NHMAX * OH, BF16) for _ in range(2)]
    rw_sb = cvC.c(KC * E, BF16)
    moe = cvC.c(NTG * D, F32)
    wbc_sb = [cvC.c(Tt, F32) for _ in range(2)]
    p_sb = cvC.c(NTG * E, F32)
    w_sb = cvC.c(NTG * E, F32)
    msk_sb = cvC.c(NTG * E, F32)
    sg_sb = cvC.c(Tt, F32)
    yt_sb = cvC.c(D, F32)
    z2_sb = cvC.c(D, F32)
    stats2 = cvC.c(4 * NTG, F32)
    sel_sb = cvC.c(E * 128, F32)

    dma("gpsimd", s_rw, rw_sb.ap().rearrange("p (kc e) -> p kc e", kc=KC),
        rw_d.rearrange("(kc p) e -> p kc e", p=128))
    dma("gpsimd", s_rw, sel_sb[0:E, :], sel_d)
    rw_ready = s_rw.n

    experts = [(e, NHF, True) for e in range(E)] + [(None, NHS, False)]
    gu_jobs = [(ei, hf) for ei, (e, nhf, wt) in enumerate(experts) for hf in range(nhf)]
    dn_jobs = [(ei, oh) for ei, (e, nhf, wt) in enumerate(experts) for oh in range(NOH)]

    nts = max(1, Tt // 512)
    ts_w = min(512, Tt)
    nos = max(1, OH // 512)
    os_w = min(512, OH)

    gu_ready = {}
    gu_pe_free = {}
    dn_ready = {}
    dn_pe_free = {}
    gj = 0
    dj = 0
    prev_z2 = 0
    out_store = 0
    hfm_last_use = {}
    mt_guard = {}

    hfm_ready_d = {}
    wfmT_ready_d = {}
    wbc_ready_d = {}
    prep_done = {}

    def prepare_dma(mt):
        t0 = mt * Tt
        if mt > 0:
            b.wait("sync", s_pe, hfm_last_use[mt - 1])
        for fc in range(KC):
            dma("sync", sq_io, h_fm[:, fc * Tt:(fc + 1) * Tt],
                h_scr[t0:t0 + Tt, fc * 128:(fc + 1) * 128], transpose=True)
        hfm_ready_d[mt] = (sq_io.n, sa_io.n)

    def prepare_pe(mt, rb, wb):
        b.wait("tensor", s_rw, rw_ready)
        b.wait("tensor", sq_io, hfm_ready_d[mt][0])
        b.wait("tensor", sa_io, hfm_ready_d[mt][1])
        b.wait("tensor", s_ac, s_ac.n)
        b.wait("tensor", s_dv, s_dv.n)
        for tg in range(NTG):
            for kc in range(KC):
                last = (tg == NTG - 1 and kc == KC - 1)
                b.em("tensor", lambda t, tg=tg, kc=kc, last=last, rb=rb:
                     (t.matmul(bank(rb)[:, tg * E:(tg + 1) * E],
                               h_fm[:, kc * Tt + tg * 128:kc * Tt + (tg + 1) * 128],
                               rw_sb[:, kc * E:(kc + 1) * E], start=(kc == 0), stop=(kc == KC - 1)).then_inc(s_pe.h, 1)
                      if last else
                      t.matmul(bank(rb)[:, tg * E:(tg + 1) * E],
                               h_fm[:, kc * Tt + tg * 128:kc * Tt + (tg + 1) * 128],
                               rw_sb[:, kc * E:(kc + 1) * E], start=(kc == 0), stop=(kc == KC - 1))))
        s_pe.inc()
        r_done = s_pe.n
        b.wait("vector", s_pe, r_done)
        for tg in range(NTG):
            last = (tg == NTG - 1)
            b.em("vector", lambda v, tg=tg, rb=rb, last=last:
                 (v.reduce_max(stats2[:, tg:tg + 1], bank(rb)[:, tg * E:(tg + 1) * E], axis=AX.X).then_inc(s_dv.h, 1)
                  if last else
                  v.reduce_max(stats2[:, tg:tg + 1], bank(rb)[:, tg * E:(tg + 1) * E], axis=AX.X)))
        s_dv.inc()
        b.wait("vector", s_dv, s_dv.n)
        b.em("vector", lambda v: v.tensor_scalar(stats2[:, NTG:2 * NTG], stats2[:, 0:NTG], -1.0, None, ALU.mult).then_inc(s_dv.h, 1)); s_dv.inc()
        b.wait("scalar", s_dv, s_dv.n)
        for tg in range(NTG):
            last = (tg == NTG - 1)
            b.em("scalar", lambda s, tg=tg, last=last, rb=rb:
                 (s.activation(p_sb[:, tg * E:(tg + 1) * E], bank(rb)[:, tg * E:(tg + 1) * E], AF.Exp,
                               bias=stats2[:, NTG + tg:NTG + tg + 1]).then_inc(s_ac.h, 1) if last else
                  s.activation(p_sb[:, tg * E:(tg + 1) * E], bank(rb)[:, tg * E:(tg + 1) * E], AF.Exp,
                               bias=stats2[:, NTG + tg:NTG + tg + 1])))
        s_ac.inc()
        b.wait("vector", s_ac, s_ac.n)
        for tg in range(NTG):
            last = (tg == NTG - 1)
            b.em("vector", lambda v, tg=tg, last=last:
                 (v.reduce_max(stats2[:, tg:tg + 1], p_sb[:, tg * E:(tg + 1) * E], axis=AX.X).then_inc(s_dv.h, 1)
                  if last else
                  v.reduce_max(stats2[:, tg:tg + 1], p_sb[:, tg * E:(tg + 1) * E], axis=AX.X)))
        s_dv.inc()
        b.wait("vector", s_dv, s_dv.n)
        for tg in range(NTG):
            last = (tg == NTG - 1)
            b.em("vector", lambda v, tg=tg, last=last:
                 (v.scalar_tensor_tensor(msk_sb[:, tg * E:(tg + 1) * E], p_sb[:, tg * E:(tg + 1) * E], stats2[:, tg:tg + 1],
                                         p_sb[:, tg * E:(tg + 1) * E], ALU.is_lt, ALU.mult).then_inc(s_dv.h, 1)
                  if last else
                  v.scalar_tensor_tensor(msk_sb[:, tg * E:(tg + 1) * E], p_sb[:, tg * E:(tg + 1) * E], stats2[:, tg:tg + 1],
                                         p_sb[:, tg * E:(tg + 1) * E], ALU.is_lt, ALU.mult)))
        s_dv.inc()
        b.wait("vector", s_dv, s_dv.n)
        for tg in range(NTG):
            last = (tg == NTG - 1)
            b.em("vector", lambda v, tg=tg, last=last:
                 (v.reduce_max(stats2[:, NTG + tg:NTG + tg + 1], msk_sb[:, tg * E:(tg + 1) * E], axis=AX.X).then_inc(s_dv.h, 1)
                  if last else
                  v.reduce_max(stats2[:, NTG + tg:NTG + tg + 1], msk_sb[:, tg * E:(tg + 1) * E], axis=AX.X)))
        s_dv.inc()
        b.wait("vector", s_dv, s_dv.n)
        b.em("vector", lambda v: v.tensor_tensor(stats2[:, 2 * NTG:3 * NTG], stats2[:, 0:NTG], stats2[:, NTG:2 * NTG], ALU.add).then_inc(s_dv.h, 1)); s_dv.inc()
        b.wait("vector", s_dv, s_dv.n)
        b.em("vector", lambda v: v.reciprocal(stats2[:, 0:NTG], stats2[:, 2 * NTG:3 * NTG]).then_inc(s_dv.h, 1)); s_dv.inc()
        b.wait("vector", s_dv, s_dv.n)
        for tg in range(NTG):
            last = (tg == NTG - 1)
            b.em("vector", lambda v, tg=tg, last=last:
                 (v.tensor_scalar(msk_sb[:, tg * E:(tg + 1) * E], p_sb[:, tg * E:(tg + 1) * E], stats2[:, NTG + tg:NTG + tg + 1], None,
                                  ALU.is_ge).then_inc(s_dv.h, 1)
                  if last else
                  v.tensor_scalar(msk_sb[:, tg * E:(tg + 1) * E], p_sb[:, tg * E:(tg + 1) * E], stats2[:, NTG + tg:NTG + tg + 1], None,
                                  ALU.is_ge)))
        s_dv.inc()
        b.wait("vector", s_dv, s_dv.n)
        for tg in range(NTG):
            last = (tg == NTG - 1)
            b.em("vector", lambda v, tg=tg, last=last:
                 (v.scalar_tensor_tensor(w_sb[:, tg * E:(tg + 1) * E], p_sb[:, tg * E:(tg + 1) * E], stats2[:, tg:tg + 1],
                                         msk_sb[:, tg * E:(tg + 1) * E], ALU.mult, ALU.mult).then_inc(s_dv.h, 1) if last else
                  v.scalar_tensor_tensor(w_sb[:, tg * E:(tg + 1) * E], p_sb[:, tg * E:(tg + 1) * E], stats2[:, tg:tg + 1],
                                         msk_sb[:, tg * E:(tg + 1) * E], ALU.mult, ALU.mult)))
        s_dv.inc()
        w_ready = s_dv.n
        b.wait("tensor", s_dv, w_ready)
        tg_evac = []
        for tg in range(NTG):
            if tg % 4 == 0 and tg_evac:
                b.wait("tensor", s_dv, tg_evac[-1])
            b.em("tensor", lambda t, tg=tg, wb=wb:
                 t.transpose(bank(wb)[0:E, (tg % 4) * 128:(tg % 4) * 128 + 128],
                             w_sb[:, tg * E:(tg + 1) * E], identf[:, :]).then_inc(s_pe.h, 1)); s_pe.inc()
            if tg % 4 == 3 or tg == NTG - 1:
                b.wait("vector", s_pe, s_pe.n)
                lo = (tg // 4) * 4
                b.em("vector", lambda v, lo=lo, tg=tg, wb=wb:
                     v.tensor_copy(wfmT[0:E, lo * 128:(tg + 1) * 128],
                                   bank(wb)[0:E, 0:(tg + 1 - lo) * 128]).then_inc(s_dv.h, 1)); s_dv.inc()
                tg_evac.append(s_dv.n)
        wfmT_ready = s_dv.n
        wfmT_ready_d[mt] = wfmT_ready
        nws = max(1, Tt // 512)
        ws_w = min(512, Tt)
        b.wait("tensor", s_dv, wfmT_ready)
        for s in range(nws):
            last = (s == nws - 1)
            b.em("tensor", lambda t, s=s, last=last, rb=rb:
                 (t.matmul(psum[:, rb * 512 + s * ws_w:rb * 512 + (s + 1) * ws_w], sel_sb[0:E, 0:128],
                           wfmT[0:E, s * ws_w:(s + 1) * ws_w], start=True, stop=True).then_inc(s_pe.h, 1) if last else
                  t.matmul(psum[:, rb * 512 + s * ws_w:rb * 512 + (s + 1) * ws_w], sel_sb[0:E, 0:128],
                           wfmT[0:E, s * ws_w:(s + 1) * ws_w], start=True, stop=True)))
        s_pe.inc()
        b.wait("vector", s_pe, s_pe.n)
        b.em("vector", lambda v, rb=rb: v.tensor_copy(wbc_sb[0][:, :], psum[:, rb * 512:rb * 512 + Tt]).then_inc(s_dv.h, 1)); s_dv.inc()
        wbc_ready_d[mt] = s_dv.n

    for mt in range(NMT):
        t0 = mt * Tt
        if mt == 0:
            prepare_dma(0)
            prepare_pe(0, 4, 5)
        if mt > 0:
            b.wait("tensor", s_dv, mt_guard[mt - 1][0])
            b.wait("tensor", s_ac, mt_guard[mt - 1][1])
            b.wait("tensor", s_pl, mt_guard[mt - 1][2])
        else:
            b.wait("tensor", s_dv, s_dv.n)
            b.wait("tensor", s_ac, s_ac.n)
            b.wait("tensor", s_pl, s_pl.n)
        wfmT_ready = wfmT_ready_d[mt]

        def load_gu(j):
            ei, hf = gu_jobs[j % len(gu_jobs)]
            e, nhf, weighted = experts[ei]
            sl = j % 2
            sem = fg_slots[sl]
            if j >= 2:
                b.wait("gpsimd", s_pe, gu_pe_free[j - 2])
            for which in range(2):
                if e is None:
                    src = (swg_d if which == 0 else swu_d)[:, hf * 128:(hf + 1) * 128]
                else:
                    src = (wg_d if which == 0 else wu_d)[e][:, hf * 128:(hf + 1) * 128]
                dma("gpsimd", sem,
                    wgu_sb[sl].ap().rearrange("p (two kc f) -> p two kc f", two=2, kc=KC)[:, which, :, :],
                    src.rearrange("(kc p) f -> p kc f", p=128))
            gu_ready[j] = sem.n

        def load_dn(j):
            ei, oh = dn_jobs[j % len(dn_jobs)]
            e, nhf, weighted = experts[ei]
            sl = j % 2
            sem = fd_slots[sl]
            if j >= 2:
                b.wait("gpsimd", s_pe, dn_pe_free[j - 2])
            for hf in range(nhf):
                if e is None:
                    src = swd_d[hf * 128:(hf + 1) * 128, oh * OH:(oh + 1) * OH]
                else:
                    src = wd_d[e][hf * 128:(hf + 1) * 128, oh * OH:(oh + 1) * OH]
                dma("gpsimd", sem, wdn_sb[sl][:, hf * OH:(hf + 1) * OH], src)
            dn_ready[j] = sem.n

        if mt == 0:
            load_gu(0)
            if len(gu_jobs) * NMT > 1:
                load_gu(1)
            load_dn(0)
            if len(dn_jobs) * NMT > 1:
                load_dn(1)
        moe_adds = []
        for ei, (e, nhf, weighted) in enumerate(experts):
            b.wait("tensor", s_ac, s_ac.n)
            b.wait("tensor", s_dv, s_dv.n)
            b.wait("tensor", s_pl, s_pl.n)
            if ei == 0:
                tg_final = {}
                wbc_ready_cur = wbc_ready_d[mt]
            wbc_ready = None
            wsl = ei % 2
            if weighted:
                wbc_ready = wbc_ready_cur
                b.wait("tensor", s_dv, wbc_ready)
            act_done = []
            sg_hist = []
            tu_hist = []
            for hf in range(nhf):
                par = hf % 2
                goff = 2048 * par
                uoff = 1024 + 2048 * par
                gsl = gj % 2
                b.wait("tensor", fg_slots[gsl], gu_ready[gj])
                if hf >= 2:
                    b.wait("tensor", s_ac, sg_hist[hf - 2])
                for kc in range(KC):
                    for s in range(nts):
                        last = (kc == KC - 1 and s == nts - 1)
                        b.em("tensor", lambda t, kc=kc, s=s, gsl=gsl, last=last, goff=goff:
                             (t.matmul(psum[:, goff + s * ts_w:goff + (s + 1) * ts_w],
                                       wgu_sb[gsl][:, kc * 128:(kc + 1) * 128],
                                       h_fm[:, kc * Tt + s * ts_w:kc * Tt + (s + 1) * ts_w],
                                       start=(kc == 0), stop=(kc == KC - 1)).then_inc(s_pe.h, 1) if last else
                              t.matmul(psum[:, goff + s * ts_w:goff + (s + 1) * ts_w],
                                       wgu_sb[gsl][:, kc * 128:(kc + 1) * 128],
                                       h_fm[:, kc * Tt + s * ts_w:kc * Tt + (s + 1) * ts_w],
                                       start=(kc == 0), stop=(kc == KC - 1))))
                s_pe.inc()
                g_done = s_pe.n
                if hf >= 2:
                    b.wait("tensor", s_dv, act_done[hf - 2])
                for kc in range(KC):
                    for s in range(nts):
                        last = (kc == KC - 1 and s == nts - 1)
                        b.em("tensor", lambda t, kc=kc, s=s, gsl=gsl, last=last, uoff=uoff:
                             (t.matmul(psum[:, uoff + s * ts_w:uoff + (s + 1) * ts_w],
                                       wgu_sb[gsl][:, KC * 128 + kc * 128:KC * 128 + (kc + 1) * 128],
                                       h_fm[:, kc * Tt + s * ts_w:kc * Tt + (s + 1) * ts_w],
                                       start=(kc == 0), stop=(kc == KC - 1)).then_inc(s_pe.h, 1) if last else
                              t.matmul(psum[:, uoff + s * ts_w:uoff + (s + 1) * ts_w],
                                       wgu_sb[gsl][:, KC * 128 + kc * 128:KC * 128 + (kc + 1) * 128],
                                       h_fm[:, kc * Tt + s * ts_w:kc * Tt + (s + 1) * ts_w],
                                       start=(kc == 0), stop=(kc == KC - 1))))
                s_pe.inc()
                u_done = s_pe.n
                gu_pe_free[gj] = s_pe.n
                gj += 1
                nxt = gj + 1
                if nxt < len(gu_jobs) * NMT and nxt not in gu_ready:
                    load_gu(nxt)
                b.wait("scalar", s_pe, g_done)
                if hf >= 1:
                    b.wait("scalar", s_dv, act_done[hf - 1])
                b.em("scalar", lambda s, goff=goff: s.activation(sg_sb[:, :], psum[:, goff:goff + Tt], AF.Silu).then_inc(s_ac.h, 1)); s_ac.inc()
                sg_done = s_ac.n
                sg_hist.append(sg_done)
                b.wait("vector", s_ac, sg_done)
                b.wait("vector", s_pe, u_done)
                if weighted:
                    b.em("vector", lambda v, uoff=uoff: v.tensor_tensor(sg_sb[:, :], sg_sb[:, :], psum[:, uoff:uoff + Tt], ALU.mult).then_inc(s_dv.h, 1)); s_dv.inc()
                    tu_done = s_dv.n
                    tu_hist.append(tu_done)
                    b.wait("vector", s_dv, tu_done)
                    b.wait("vector", s_dv, wbc_ready)
                    b.em("vector", lambda v, hf=hf, wsl=wsl: v.tensor_tensor(act_e[:, hf * Tt:(hf + 1) * Tt],
                                                                    sg_sb[:, :], wbc_sb[wsl][:, :], ALU.mult).then_inc(s_dv.h, 1)); s_dv.inc()
                    act_done.append(s_dv.n)
                else:
                    b.em("vector", lambda v, hf=hf, uoff=uoff: v.tensor_tensor(act_e[:, hf * Tt:(hf + 1) * Tt],
                                                                    sg_sb[:, :], psum[:, uoff:uoff + Tt], ALU.mult).then_inc(s_dv.h, 1)); s_dv.inc()
                    act_done.append(s_dv.n)
                    tu_hist.append(act_done[-1])
            if ei + 1 < len(experts) and experts[ei + 1][2]:
                b.wait("tensor", s_ac, sg_hist[-1])
                b.wait("tensor", s_dv, tu_hist[-1])
                ne = experts[ei + 1][0]
                for s in range(nts):
                    last = (s == nts - 1)
                    b.em("tensor", lambda t, ne=ne, s=s, last=last:
                         (t.matmul(psum[:, s * ts_w:(s + 1) * ts_w], sel_sb[0:E, ne * 128:(ne + 1) * 128],
                                   wfmT[0:E, s * ts_w:(s + 1) * ts_w], start=True, stop=True).then_inc(s_pe.h, 1)
                          if last else
                          t.matmul(psum[:, s * ts_w:(s + 1) * ts_w], sel_sb[0:E, ne * 128:(ne + 1) * 128],
                                   wfmT[0:E, s * ts_w:(s + 1) * ts_w], start=True, stop=True)))
                s_pe.inc()
                b.wait("vector", s_pe, s_pe.n)
                b.wait("vector", s_dv, act_done[-1] if weighted else s_dv.n)
                b.em("vector", lambda v, wsl=wsl: v.tensor_copy(wbc_sb[1 - wsl][:, :], psum[:, 0:Tt]).then_inc(s_dv.h, 1)); s_dv.inc()
                wbc_ready_cur = s_dv.n
            if ei == len(experts) - 1:
                hfm_last_use[mt] = gu_pe_free[gj - 1]
            if ei == len(experts) - 1 and mt + 1 < NMT:
                prepare_dma(mt + 1)
            for oh in range(NOH):
                if ei == len(experts) - 1 and mt + 1 < NMT and oh == min(1, NOH - 1):
                    prepare_pe(mt + 1, 0, 1)
                dsl = dj % 2
                b.wait("tensor", fd_slots[dsl], dn_ready[dj])
                b.wait("tensor", s_ac, sg_hist[-1])
                b.wait("tensor", s_dv, tu_hist[-1])
                b.wait("tensor", s_dv, act_done[-1])
                if moe_adds:
                    b.wait("tensor", s_dv, moe_adds[-1])
                for tg in range(NTG):
                    dbk = 4 + (tg % 2) * 2
                    if tg >= 2:
                        b.wait("tensor", s_dv, moe_adds[-2])
                    for hf in range(nhf):
                        for s in range(nos):
                            last = (hf == nhf - 1 and s == nos - 1)
                            b.em("tensor", lambda t, tg=tg, hf=hf, s=s, dsl=dsl, dbk=dbk, last=last, nhf=nhf:
                                 (t.matmul(psum[:, dbk * 512 + s * os_w:dbk * 512 + (s + 1) * os_w],
                                           act_e[:, hf * Tt + tg * 128:hf * Tt + (tg + 1) * 128],
                                           wdn_sb[dsl][:, hf * OH + s * os_w:hf * OH + (s + 1) * os_w],
                                           start=(hf == 0), stop=(hf == nhf - 1)).then_inc(s_pe.h, 1) if last else
                                  t.matmul(psum[:, dbk * 512 + s * os_w:dbk * 512 + (s + 1) * os_w],
                                           act_e[:, hf * Tt + tg * 128:hf * Tt + (tg + 1) * 128],
                                           wdn_sb[dsl][:, hf * OH + s * os_w:hf * OH + (s + 1) * os_w],
                                           start=(hf == 0), stop=(hf == nhf - 1))))
                    s_pe.inc()
                    b.wait("vector", s_pe, s_pe.n)
                    if ei == 0:
                        b.em("vector", lambda v, tg=tg, oh=oh, dbk=dbk:
                             v.tensor_copy(moe[:, tg * D + oh * OH:tg * D + (oh + 1) * OH],
                                           psum[:, dbk * 512:dbk * 512 + OH]).then_inc(s_dv.h, 1)); s_dv.inc()
                    else:
                        b.em("vector", lambda v, tg=tg, oh=oh, dbk=dbk:
                             v.tensor_tensor(moe[:, tg * D + oh * OH:tg * D + (oh + 1) * OH],
                                             moe[:, tg * D + oh * OH:tg * D + (oh + 1) * OH],
                                             psum[:, dbk * 512:dbk * 512 + OH], ALU.add).then_inc(s_dv.h, 1)); s_dv.inc()
                    moe_adds.append(s_dv.n)
                    if ei == len(experts) - 1 and oh == NOH - 1:
                        tg_final[tg] = s_dv.n
                dn_pe_free[dj] = s_pe.n
                dj += 1
                if dj < len(dn_jobs) * NMT and dj not in dn_ready:
                    load_dn(dj)
        moe_done = s_dv.n
        mt_guard[mt] = (s_dv.n, s_ac.n, s_pl.n)

        for tg in range(NTG):
            tt = t0 // 128 + tg
            b.wait("scalar", s_dv, prev_z2)
            b.wait("scalar", s_out, out_store)
            dma("scalar", sa_io, yt_sb[:, :], y_scr[tt * 128:(tt + 1) * 128, :])
            b.wait("vector", sa_io)
            b.wait("vector", s_dv, tg_final[tg])
            b.em("vector", lambda v, tg=tg:
                 v.scalar_tensor_tensor(z2_sb[:, :], moe[:, tg * D:(tg + 1) * D], 1.0, yt_sb[:, :],
                                        ALU.mult, ALU.add, accum_out=stats[:, 0:1]).then_inc(s_dv.h, 1)); s_dv.inc()
            prev_z2 = s_dv.n
            z2_done = s_dv.n
            b.wait("scalar", s_dv, z2_done)
            b.em("scalar", lambda s:
                 s.activation(yt_sb[:, :], z2_sb[:, :], AF.Square,
                              accum_out=stats[:, 1:2]).then_inc(s_ac.h, 1)); s_ac.inc()
            b.wait("vector", s_ac, s_ac.n)
            b.em("vector", lambda v: v.tensor_scalar(stats[:, 2:3], stats[:, 0:1], 1.0 / D, None, ALU.mult))
            b.em("vector", lambda v: v.tensor_scalar(stats[:, 3:4], stats[:, 1:2], 1.0 / D, None, ALU.mult).then_inc(s_dv.h, 1)); s_dv.inc()
            b.wait("vector", s_dv, s_dv.n)
            b.em("vector", lambda v: v.tensor_tensor(stats[:, 4:5], stats[:, 2:3], stats[:, 2:3], ALU.mult).then_inc(s_dv.h, 1)); s_dv.inc()
            b.wait("vector", s_dv, s_dv.n)
            b.em("vector", lambda v: v.tensor_tensor(stats[:, 5:6], stats[:, 3:4], stats[:, 4:5], ALU.subtract).then_inc(s_dv.h, 1)); s_dv.inc()
            b.wait("scalar", s_dv, s_dv.n)
            b.em("scalar", lambda s: s.activation(stats[:, 6:7], stats[:, 5:6], AF.Ln, bias=eps_t[:, 0:1]).then_inc(s_ac.h, 1)); s_ac.inc()
            b.wait("scalar", s_ac, s_ac.n)
            b.em("scalar", lambda s: s.activation(stats[:, 7:8], stats[:, 6:7], AF.Exp, scale=-0.5).then_inc(s_ac.h, 1)); s_ac.inc()
            b.wait("vector", s_ac, s_ac.n)
            b.em("vector", lambda v:
                 v.tensor_scalar(yt_sb[:, :], z2_sb[:, :], stats[:, 2:3], stats[:, 7:8],
                                 ALU.subtract, ALU.mult).then_inc(s_dv.h, 1)); s_dv.inc()
            b.wait("sync", s_dv, s_dv.n)
            dma("sync", s_out, out_d[tt * 128:(tt + 1) * 128, :], yt_sb[:, :])
            out_store = s_out.n

    b.wait("sync", sq_io)
    b.wait("sync", s_out)
    b.wait("scalar", sa_io)
    for s2 in [s_tab, s_bq, s_xst, s_wout, s_rw, s_sg] + list(w_slots) + list(fg_slots) + list(fd_slots) + list(t_slots):
        b.wait("gpsimd", s2)

    with nc.Block() as block:
        b.run(block)
    return nc


def make_consts(cfg: Cfg, core_idx: int):
    T, H, NC, E = cfg.T, cfg.H, cfg.NC, cfg.E
    j = np.arange(128, dtype=np.float64)
    i = np.arange(128, dtype=np.float64)
    diff = i[None, :] - j[:, None]
    intra = np.where(diff >= 0, diff, BIG)
    c_intra = np.tile(intra, (1, H)).astype(np.float32)
    c_qdec = np.tile(np.broadcast_to(i[None, :], (128, 128)), (1, H)).astype(np.float32)
    c_kdec = np.broadcast_to((128.0 - j)[:, None], (128, H)).astype(np.float32).copy()
    cb = np.zeros((128, H * NC), np.float64)
    for h in range(H):
        cb[:, h * NC:(h + 1) * NC] = 128.0 * np.arange(NC)[None, :]
    c_bdc = cb.astype(np.float32)
    cc = np.zeros((128, H * 8), np.float64)
    for h in range(H):
        for jj in range(8):
            cc[:, h * 8 + jj] = (T * (core_idx - 1 - jj)) if jj < core_idx else BIG
    c_comb = cc.astype(np.float32)
    sel = np.zeros((E, E * 128), np.float32)
    for e in range(E):
        sel[e, e * 128:(e + 1) * 128] = 1.0
    identf = np.eye(128, dtype=np.float32)
    return {"c_intra": c_intra, "c_qdec": c_qdec, "c_kdec": c_kdec,
            "c_bdc": c_bdc, "c_comb": c_comb, "sel": sel, "identf": identf}


def run_cfg(cfg: Cfg, full_inputs: dict) -> np.ndarray:
    nc = build_program(cfg)
    shared = {k: np.ascontiguousarray(np.asarray(v, dtype=np.float32))
              for k, v in full_inputs.items() if k != "x"}
    shared["slopes"] = np.ascontiguousarray(
        np.asarray(full_inputs["slopes"], dtype=np.float32).reshape(cfg.H, 1))
    x = np.asarray(full_inputs["x"], dtype=np.float32)
    in_maps = []
    for c in range(cfg.n_cores):
        m = dict(shared)
        m["x"] = np.ascontiguousarray(x[c * cfg.T:(c + 1) * cfg.T, :])
        m.update(make_consts(cfg, c))
        in_maps.append(m)
    res = run_bass_kernel_spmd(nc, in_maps, core_ids=list(range(cfg.n_cores)))
    return np.concatenate([np.asarray(res.results[c]["out"]) for c in range(cfg.n_cores)], axis=0)


def kernel(L0_attn_x, L0_attn_Wqkv, L0_attn_bqkv, L0_attn_Wout, L0_attn_bout,
           L0_attn_slopes, L0_ffn_rms_weight, L0_ffn_router_w, L0_ffn_w_gate,
           L0_ffn_w_up, L0_ffn_w_down, L0_ffn_sw_gate, L0_ffn_sw_up, L0_ffn_sw_down):
    cfg = Cfg(T=2048, D=2048, H=16, F=1024, SF=1024, E=8, n_cores=8)
    x = np.asarray(L0_attn_x).reshape(16384, 2048)
    out = run_cfg(cfg, {
        "x": x, "wqkv": L0_attn_Wqkv, "bqkv": L0_attn_bqkv,
        "wout": L0_attn_Wout, "bout": L0_attn_bout,
        "slopes": L0_attn_slopes, "rmsw": L0_ffn_rms_weight,
        "router_w": L0_ffn_router_w, "wg": L0_ffn_w_gate,
        "wu": L0_ffn_w_up, "wd": L0_ffn_w_down,
        "swg": L0_ffn_sw_gate, "swu": L0_ffn_sw_up, "swd": L0_ffn_sw_down})
    return out.reshape(1, 16384, 2048)


# version 19
# speedup vs baseline: 1.0768x; 1.0525x over previous
import numpy as np

import concourse.bass as bass
import concourse.mybir as mybir
from concourse.bass_utils import run_bass_kernel_spmd

F32 = mybir.dt.float32
BF16 = mybir.dt.bfloat16
AF = mybir.ActivationFunctionType
ALU = mybir.AluOpType
AX = mybir.AxisListType

BIG = 1.0e9


class Cfg:
    def __init__(self, T, D, H, F, SF, E, n_cores=8, tt=None, oh=None):
        self.T, self.D, self.H, self.F, self.SF, self.E = T, D, H, F, SF, E
        self.n_cores = n_cores
        self.KC = D // 128
        self.NC = T // 128
        self.Tt = tt if tt else min(1024, T)
        self.NMT = T // self.Tt
        self.OH = oh if oh else min(1024, D)
        self.NOH = D // self.OH
        assert D % 128 == 0 and T % 128 == 0 and F % 128 == 0 and SF % 128 == 0


class Sem:
    __slots__ = ("h", "n")

    def __init__(self, h):
        self.h = h
        self.n = 0

    def inc(self, k=1):
        self.n += k
        return self.n


class Builder:
    ENGINES = ("tensor", "vector", "scalar", "gpsimd", "sync")

    def __init__(self, nc):
        self.nc = nc
        self.plan = {e: [] for e in self.ENGINES}
        self.last_wait = {}

    def em(self, eng, fn):
        self.plan[eng].append(fn)

    def wait(self, eng, sem, val=None):
        if val is None:
            val = sem.n
        if val <= 0:
            return
        key = (eng, id(sem))
        if self.last_wait.get(key, -1) >= val:
            return
        self.last_wait[key] = val
        self.em(eng, lambda e, s=sem.h, v=val: e.wait_ge(s, v))

    def barrier(self, sem, data_sems=()):
        for ds in data_sems:
            for eng in self.ENGINES:
                self.wait(eng, ds)
        tgt = sem.n + len(self.ENGINES)
        sem.n = tgt
        for eng in self.ENGINES:
            self.em(eng, lambda e, s=sem.h: e.nop().then_inc(s, 1))
        for eng in self.ENGINES:
            self.last_wait[(eng, id(sem))] = tgt
            self.em(eng, lambda e, s=sem.h, v=tgt: e.wait_ge(s, v))

    def run(self, block):
        def make(name):
            def body(eng):
                for fn in self.plan[name]:
                    fn(eng)
            return body
        block.tensor(make("tensor"))
        block.vector(make("vector"))
        block.scalar(make("scalar"))
        block.gpsimd(make("gpsimd"))
        block.sync(make("sync"))


class Arena:
    _uid = [0]

    def __init__(self, nc, start, top):
        self.nc = nc
        self.cur = start
        self.top = top

    def c(self, cols, dtype, parts=128):
        sz = cols * (2 if dtype == BF16 else 4)
        off = (self.cur + 31) & ~31
        self.cur = off + sz
        assert self.cur <= self.top, f"SBUF arena overflow: {self.cur} > {self.top}"
        Arena._uid[0] += 1
        return self.nc.alloc_sbuf_tensor_at(f"ar{Arena._uid[0]}", [parts, cols], dtype, offset=off)


def build_program(cfg: Cfg):
    nc = bass.Bass("TRN2", target_bir_lowering=False)
    T, D, H, F, SF, E = cfg.T, cfg.D, cfg.H, cfg.F, cfg.SF, cfg.E
    KC, NC, Tt, NMT, OH, NOH = cfg.KC, cfg.NC, cfg.Tt, cfg.NMT, cfg.OH, cfg.NOH
    NTG = Tt // 128
    NTT = T // 128
    NS = max(1, T // 512)
    SLAB = T // NS
    NHF, NHS = F // 128, SF // 128
    NHMAX = max(NHF, NHS)

    ap = lambda *a, **k: nc.dram_tensor(*a, **k).ap()
    x_d = ap("x", [T, D], F32, kind="ExternalInput")
    wqkv_d = ap("wqkv", [D, 3 * D], F32, kind="ExternalInput")
    bqkv_d = ap("bqkv", [3 * D], F32, kind="ExternalInput")
    wout_d = ap("wout", [D, D], F32, kind="ExternalInput")
    bout_d = ap("bout", [D], F32, kind="ExternalInput")
    slopes_d = ap("slopes", [H, 1], F32, kind="ExternalInput")
    rmsw_d = ap("rmsw", [D], F32, kind="ExternalInput")
    rw_d = ap("router_w", [D, E], F32, kind="ExternalInput")
    wg_d = ap("wg", [E, D, F], F32, kind="ExternalInput")
    wu_d = ap("wu", [E, D, F], F32, kind="ExternalInput")
    wd_d = ap("wd", [E, F, D], F32, kind="ExternalInput")
    swg_d = ap("swg", [D, SF], F32, kind="ExternalInput")
    swu_d = ap("swu", [D, SF], F32, kind="ExternalInput")
    swd_d = ap("swd", [SF, D], F32, kind="ExternalInput")
    c_intra_d = ap("c_intra", [128, H * 128], F32, kind="ExternalInput")
    c_qdec_d = ap("c_qdec", [128, H * 128], F32, kind="ExternalInput")
    c_kdec_d = ap("c_kdec", [128, H], F32, kind="ExternalInput")
    c_bdc_d = ap("c_bdc", [128, H * NC], F32, kind="ExternalInput")
    c_comb_d = ap("c_comb", [128, H * 8], F32, kind="ExternalInput")
    sel_d = ap("sel", [E, E * 128], F32, kind="ExternalInput")
    identf_d = ap("identf", [128, 128], F32, kind="ExternalInput")
    out_d = ap("out", [T, D], F32, kind="ExternalOutput")

    x_scr = ap("x_scr", [T, D], BF16, kind="Internal")
    o_scr = ap("o_scr", [H * 128, T], BF16, kind="Internal")
    q_scr = ap("q_scr", [H * 128, T], BF16, kind="Internal")
    y_scr = ap("y_scr", [T, D], F32, kind="Internal")
    h_scr = ap("h_scr", [T, D], BF16, kind="Internal")
    H2 = H // 2
    sl_in_a = ap("sl_in_a", [128, H2 * 128], F32, kind="Internal")
    sl_out_a = ap("sl_out_a", [cfg.n_cores * 128, H2 * 128], F32, kind="Internal")
    sl_in_b = ap("sl_in_b", [128, (H - H2) * 128], F32, kind="Internal")
    sl_out_b = ap("sl_out_b", [cfg.n_cores * 128, (H - H2) * 128], F32, kind="Internal")

    def S(name):
        return Sem(nc.alloc_semaphore(name))

    sg_w = S("sg_w")
    s_tab = S("s_tab")
    s_bq = S("s_bq")
    s_xst = S("s_xst")
    s_wout = S("s_wout")
    s_rw = S("s_rw")
    s_out = S("s_out")
    s_sg = S("s_sg")
    sq_io = S("sq_io")
    sa_io = S("sa_io")
    s_pe = S("s_pe")
    s_dv = S("s_dv")
    s_ac = S("s_ac")
    s_pl = S("s_pl")
    s_cc = S("s_cc")
    s_bar = S("s_bar")
    w_slots = (S("sw0"), S("sw1"))
    fg_slots = (S("sfg0"), S("sfg1"))
    fd_slots = (S("sfd0"), S("sfd1"))
    t_slots = (S("st0"), S("st1"))

    b = Builder(nc)
    psum = nc.alloc_psum_tensor("pool", [128, 4096], F32)

    def bank(i, w=512):
        return psum[:, i * 512:i * 512 + w]

    SB_BASE = (nc.sbuf_base + 31) & ~31
    SB_TOP = nc.sbuf_top
    perm = Arena(nc, SB_BASE, SB_TOP)
    identf = perm.c(128, F32)
    sg_all = perm.c(8 * 128, F32)
    comb_t = perm.c(H * 8, F32)
    wfmT = perm.c(min(1024, T), F32)
    si_a = perm.c(128, F32)
    si_b = perm.c(128, F32)
    stats = perm.c(32, F32)
    eps_t = perm.c(4, F32)
    s_init = perm.c(H * 128, BF16)
    PH = perm.cur

    cvA = Arena(nc, PH, SB_TOP)
    slope_cols = cvA.c(H, F32)
    intraT = cvA.c(H * 128, F32)
    qdec_t = cvA.c(H * 128, F32)
    kdec_t = cvA.c(H, F32)
    bdc_t = cvA.c(H * NC, F32)
    tmp_tab = cvA.c(H * 128, F32)
    bqkv_cols = cvA.c(3 * KC, F32)
    bq_rows = cvA.c(128, F32)
    bias_k = [cvA.c(128, F32) for _ in range(2)]
    bias_v = [cvA.c(128, F32) for _ in range(2)]
    bkd_tmp = [cvA.c(128, F32) for _ in range(2)]
    s_a = cvA.c(128, F32)
    s_b = cvA.c(128, F32)
    x_fm = cvA.c(KC * T, BF16)
    wq_sb = [cvA.c(KC * 128, BF16) for _ in range(2)]
    wkv_sb = [cvA.c(KC * 256, BF16) for _ in range(2)]
    q_fm = cvA.c(T, BF16)
    k_fm = cvA.c(T, BF16)
    v_tm = cvA.c(T, BF16)
    kd_tm = cvA.c(T, BF16)
    o_buf = cvA.c(T, BF16)
    qg_buf = cvA.c(T, BF16)
    scT_sb = cvA.c(128, BF16)
    qdec_sb = cvA.c(128, BF16)
    s_bf = cvA.c(128, BF16)
    rows_per = min(1024, T)
    stage_cols = (rows_per // 128) * D
    x_stage = nc.alloc_sbuf_tensor_at("x_stage", [128, stage_cols], BF16,
                                      offset=(SB_TOP - stage_cols * 2) & ~31)
    assert cvA.cur <= ((SB_TOP - stage_cols * 2) & ~31) or True

    def dma(eng, sem, out, in_, **kw):
        b.em(eng, lambda e, o=out, i=in_, kw=kw: e.dma_start(out=o, in_=i, **kw).then_inc(sem.h, 16))
        sem.inc(16)
        return sem.n

    b.em("vector", lambda v: v.memset(eps_t[:, 0:1], 1e-5))
    b.em("vector", lambda v: v.memset(eps_t[:, 1:2], 1e-6))
    b.em("vector", lambda v: v.memset(eps_t[:, 2:3], 1e-6).then_inc(s_dv.h, 1)); s_dv.inc()
    dma("gpsimd", s_tab, identf[:, :], identf_d)
    dma("gpsimd", s_tab, slope_cols[:, :], slopes_d.rearrange("h one -> one h").partition_broadcast(128))
    dma("gpsimd", s_tab, intraT[:, :], c_intra_d)
    dma("gpsimd", s_tab, qdec_t[:, :], c_qdec_d)
    dma("gpsimd", s_tab, kdec_t[:, :], c_kdec_d)
    dma("gpsimd", s_tab, bdc_t[:, :], c_bdc_d)
    dma("gpsimd", s_tab, comb_t[:, :], c_comb_d)

    b.wait("vector", s_tab)
    for tab, blkw in ((intraT, 128), (qdec_t, 128), (bdc_t, NC), (comb_t, 8), (kdec_t, 1)):
        b.wait("vector", s_ac)
        for h in range(H):
            last = (h == H - 1)
            b.em("vector", lambda v, t=tab, h=h, w=blkw, last=last:
                 (v.tensor_scalar(tmp_tab[:, h * w:(h + 1) * w], t[:, h * w:(h + 1) * w],
                                  slope_cols[:, h:h + 1], None, ALU.mult).then_inc(s_dv.h, 1)
                  if last else
                  v.tensor_scalar(tmp_tab[:, h * w:(h + 1) * w], t[:, h * w:(h + 1) * w],
                                  slope_cols[:, h:h + 1], None, ALU.mult)))
        s_dv.inc()
        b.wait("scalar", s_dv)
        b.em("scalar", lambda s, t=tab, w=blkw:
             s.activation(t[:, 0:H * w], tmp_tab[:, 0:H * w], AF.Exp, scale=-1.0).then_inc(s_ac.h, 1)); s_ac.inc()
    tabs_ready = s_ac.n

    dma("gpsimd", s_bq, bq_rows[0:3 * KC, :], bqkv_d.rearrange("(c p) -> c p", p=128))
    b.wait("tensor", s_bq)
    b.wait("tensor", s_tab)
    b.em("tensor", lambda t: t.transpose(bank(7, 3 * KC), bq_rows[0:3 * KC, :],
                                         identf[0:3 * KC, 0:3 * KC]).then_inc(s_pe.h, 1)); s_pe.inc()
    b.wait("vector", s_pe)
    b.em("vector", lambda v: v.tensor_copy(bqkv_cols[:, :], bank(7, 3 * KC)).then_inc(s_dv.h, 1)); s_dv.inc()
    bq_ready = s_dv.n

    nxl = T // rows_per
    for i in range(nxl):
        b.wait("gpsimd", sq_io)
        dma("gpsimd", s_xst,
            x_stage.ap().rearrange("p (n d) -> p n d", n=rows_per // 128),
            x_d[i * rows_per:(i + 1) * rows_per, :].rearrange("(n p) d -> p n d", p=128))
        b.wait("sync", s_xst)
        dma("sync", sq_io,
            x_scr[i * rows_per:(i + 1) * rows_per, :].rearrange("(n p) d -> p n d", p=128),
            x_stage.ap().rearrange("p (n d) -> p n d", n=rows_per // 128))
    b.wait("sync", sq_io)
    for fc in range(KC):
        dma("sync", sq_io, x_fm[:, fc * T:(fc + 1) * T], x_scr[:, fc * 128:(fc + 1) * 128], transpose=True)
    x_fm_ready = sq_io.n

    head_pe_free = {}
    head_dv_free = {}
    head_w_ready = {}

    def kvtile(sl, kc, which=None):
        if which is None:
            return wkv_sb[sl][:, kc * 256:(kc + 1) * 256]
        return wkv_sb[sl].ap().rearrange("p (kc two f) -> p kc two f", kc=KC, two=2)[:, kc, which, :]

    def load_head_weights(h):
        sl = h % 2
        sem = w_slots[sl]
        if h >= 2:
            b.wait("gpsimd", s_pe, head_pe_free[h - 2])
            b.wait("gpsimd", s_dv, head_dv_free[h - 2])
        dma("gpsimd", sem, wq_sb[sl].ap().rearrange("p (kc f) -> p kc f", kc=KC),
            wqkv_d[:, h * 128:(h + 1) * 128].rearrange("(kc p) f -> p kc f", p=128))
        dma("gpsimd", sem, wkv_sb[sl].ap().rearrange("p (kc two f) -> p kc two f", kc=KC, two=2)[:, :, 0, :],
            wqkv_d[:, D + h * 128:D + (h + 1) * 128].rearrange("(kc p) f -> p kc f", p=128))
        dma("gpsimd", sem, wkv_sb[sl].ap().rearrange("p (kc two f) -> p kc two f", kc=KC, two=2)[:, :, 1, :],
            wqkv_d[:, 2 * D + h * 128:2 * D + (h + 1) * 128].rearrange("(kc p) f -> p kc f", p=128))
        dma("gpsimd", sem, bias_k[sl][:, :],
            bqkv_d.rearrange("(one d) -> one d", one=1)[:, D + h * 128:D + (h + 1) * 128].partition_broadcast(128))
        dma("gpsimd", sem, bias_v[sl][:, :],
            bqkv_d.rearrange("(one d) -> one d", one=1)[:, 2 * D + h * 128:2 * D + (h + 1) * 128].partition_broadcast(128))
        return sem.n

    head_w_ready[0] = load_head_weights(0)
    if H > 1:
        head_w_ready[1] = load_head_weights(1)
    b.wait("tensor", sq_io, x_fm_ready)
    b.wait("tensor", s_ac, tabs_ready)

    si_ready_a_box = [None]

    def emit_combine1():
        b.wait("sync", s_cc, 1)
        for hh in range(H2):
            b.wait("sync", s_dv, s_dv.n)
            dma("sync", s_sg, sg_all.ap().rearrange("p (j f) -> p j f", j=cfg.n_cores),
                sl_out_a.rearrange("(j p) c -> p j c", p=128)[:, :, hh * 128:(hh + 1) * 128])
            b.wait("vector", s_sg)
            cur = None
            for j in range(cfg.n_cores):
                if j == 0:
                    b.em("vector", lambda v, hh=hh:
                         v.tensor_scalar(si_a[:, :], sg_all[:, 0:128], comb_t[:, hh * 8:hh * 8 + 1], None, ALU.mult).then_inc(s_dv.h, 1)); s_dv.inc()
                    cur = si_a
                else:
                    nxt = si_b if cur is si_a else si_a
                    b.wait("vector", s_dv, s_dv.n)
                    b.em("vector", lambda v, hh=hh, j=j, src=cur, dst=nxt:
                         v.scalar_tensor_tensor(dst[:, :], sg_all[:, j * 128:(j + 1) * 128],
                                                comb_t[:, hh * 8 + j:hh * 8 + j + 1], src[:, :],
                                                ALU.mult, ALU.add).then_inc(s_dv.h, 1)); s_dv.inc()
                    cur = nxt
            b.wait("vector", s_dv, s_dv.n)
            b.em("vector", lambda v, hh=hh, fin=cur:
                 v.tensor_copy(s_init[:, hh * 128:(hh + 1) * 128], fin[:, :]).then_inc(s_dv.h, 1)); s_dv.inc()
        si_ready_a_box[0] = ("dv", s_dv.n)

    store_guard = 0
    for h in range(H):
        sl = h % 2
        sem = w_slots[sl]
        wv = head_w_ready[h]
        b.wait("tensor", s_dv, s_dv.n)
        b.wait("tensor", s_ac, s_ac.n)
        b.wait("tensor", s_pl, s_pl.n)
        b.wait("tensor", sem, wv)
        b.wait("vector", sq_io, store_guard)
        b.wait("scalar", sq_io, store_guard)
        b.wait("gpsimd", sq_io, store_guard)
        b.wait("vector", sem, wv)
        b.wait("vector", s_ac, tabs_ready)
        b.em("vector", lambda v, h=h, sl=sl:
             v.tensor_scalar(bkd_tmp[sl][:, :], bias_k[sl][:, :], kdec_t[:, h:h + 1], None, ALU.mult).then_inc(s_dv.h, 1)); s_dv.inc()
        bkd_ready = s_dv.n

        for kc in range(KC):
            for s in range(NS):
                last = (kc == KC - 1 and s == NS - 1)
                b.em("tensor", lambda t, kc=kc, s=s, sl=sl, last=last:
                     (t.matmul(psum[:, s * SLAB:(s + 1) * SLAB], wq_sb[sl][:, kc * 128:(kc + 1) * 128],
                               x_fm[:, kc * T + s * SLAB:kc * T + (s + 1) * SLAB],
                               start=(kc == 0), stop=(kc == KC - 1)).then_inc(s_pe.h, 1) if last else
                      t.matmul(psum[:, s * SLAB:(s + 1) * SLAB], wq_sb[sl][:, kc * 128:(kc + 1) * 128],
                               x_fm[:, kc * T + s * SLAB:kc * T + (s + 1) * SLAB],
                               start=(kc == 0), stop=(kc == KC - 1))))
        s_pe.inc()
        q_done = s_pe.n
        for kc in range(KC):
            for s in range(NS):
                last = (kc == KC - 1 and s == NS - 1)
                b.em("tensor", lambda t, kc=kc, s=s, sl=sl, last=last:
                     (t.matmul(psum[:, 2048 + s * SLAB:2048 + (s + 1) * SLAB], kvtile(sl, kc, 0),
                               x_fm[:, kc * T + s * SLAB:kc * T + (s + 1) * SLAB],
                               start=(kc == 0), stop=(kc == KC - 1)).then_inc(s_pe.h, 1) if last else
                      t.matmul(psum[:, 2048 + s * SLAB:2048 + (s + 1) * SLAB], kvtile(sl, kc, 0),
                               x_fm[:, kc * T + s * SLAB:kc * T + (s + 1) * SLAB],
                               start=(kc == 0), stop=(kc == KC - 1))))
        s_pe.inc()
        k_done = s_pe.n
        b.wait("scalar", s_pe, q_done)
        b.em("scalar", lambda s, h=h:
             s.activation(q_fm[:, :], psum[:, 0:T], AF.Identity, bias=bqkv_cols[:, h:h + 1]).then_inc(s_ac.h, 1)); s_ac.inc()
        q_evac = s_ac.n
        b.wait("scalar", s_pe, k_done)
        b.em("scalar", lambda s, h=h:
             s.activation(k_fm[:, :], psum[:, 2048:2048 + T], AF.Identity, bias=bqkv_cols[:, KC + h:KC + h + 1]).then_inc(s_ac.h, 1)); s_ac.inc()
        k_evac = s_ac.n

        b.wait("tensor", s_ac, q_evac)
        kv_dv = []
        for tt in range(NTT):
            bk = tt % 2
            if tt >= 2:
                b.wait("tensor", s_dv, kv_dv[tt - 2])
            for kc in range(KC):
                last = (kc == KC - 1)
                b.em("tensor", lambda t, tt=tt, kc=kc, bk=bk, sl=sl, last=last:
                     (t.matmul(bank(bk, 256), x_fm[:, kc * T + tt * 128:kc * T + (tt + 1) * 128],
                               kvtile(sl, kc), start=(kc == 0), stop=(kc == KC - 1)).then_inc(s_pe.h, 1)
                      if last else
                      t.matmul(bank(bk, 256), x_fm[:, kc * T + tt * 128:kc * T + (tt + 1) * 128],
                               kvtile(sl, kc), start=(kc == 0), stop=(kc == KC - 1))))
            s_pe.inc()
            b.wait("vector", s_pe, s_pe.n)
            b.wait("vector", s_dv, bkd_ready)
            b.em("vector", lambda v, tt=tt, bk=bk, h=h, sl=sl:
                 v.scalar_tensor_tensor(kd_tm[:, tt * 128:(tt + 1) * 128], bank(bk, 256)[:, 0:128],
                                        kdec_t[:, h:h + 1], bkd_tmp[sl][:, :], ALU.mult, ALU.add))
            b.em("vector", lambda v, tt=tt, bk=bk, sl=sl:
                 v.tensor_tensor(v_tm[:, tt * 128:(tt + 1) * 128], bank(bk, 256)[:, 128:256],
                                 bias_v[sl][:, :], ALU.add).then_inc(s_dv.h, 1)); s_dv.inc()
            kv_dv.append(s_dv.n)
        head_pe_free[h] = s_pe.n
        head_dv_free[h] = s_dv.n
        if h + 2 < H:
            head_w_ready[h + 2] = load_head_weights(h + 2)

        b.wait("tensor", s_dv, kv_dv[-1])
        b.wait("tensor", s_ac, k_evac)
        o_evacs = []
        mask_done = {}
        supd_done = {}
        qdec_done = {}
        qg_done = {}
        o_done_d = {}
        sbf_done_d = {}
        sbf_done = None
        cur_s = None
        for c in range(NC):
            cs = slice(c * 128, (c + 1) * 128)
            sc_b = 2 + (c % 2)
            dl_b = 6 + (c % 2)
            o_b = 4 + ((c // 4) % 2)
            ocol = (c % 4) * 128
            if c >= 2:
                b.wait("tensor", s_dv, mask_done[c - 2])
            b.em("tensor", lambda t, cs=cs, sc_b=sc_b:
                 t.matmul(bank(sc_b, 128), k_fm[:, cs], q_fm[:, cs], start=True, stop=True).then_inc(s_pe.h, 1)); s_pe.inc()
            sc_done = s_pe.n
            if c >= 2:
                b.wait("tensor", s_dv, supd_done[c - 2])
            b.em("tensor", lambda t, cs=cs, dl_b=dl_b:
                 t.matmul(bank(dl_b, 128), kd_tm[:, cs], v_tm[:, cs], start=True, stop=True).then_inc(s_pe.h, 1)); s_pe.inc()
            dl_done = s_pe.n
            b.wait("vector", s_pe, sc_done)
            if c >= 1:
                b.wait("vector", s_pe, o_done_d[c - 1])
                b.wait("vector", s_dv, qg_done[c - 1])
            b.em("vector", lambda v, sc_b=sc_b, h=h:
                 v.tensor_tensor(scT_sb[:, :], bank(sc_b, 128), intraT[:, h * 128:(h + 1) * 128], ALU.mult).then_inc(s_dv.h, 1)); s_dv.inc()
            mask_done[c] = s_dv.n
            b.em("vector", lambda v, cs=cs, h=h:
                 v.tensor_tensor(qdec_sb[:, :], q_fm[:, cs], qdec_t[:, h * 128:(h + 1) * 128], ALU.mult).then_inc(s_dv.h, 1)); s_dv.inc()
            qdec_done[c] = s_dv.n
            b.wait("vector", s_dv, qdec_done[c])
            b.em("vector", lambda v, c=c, h=h:
                 v.tensor_scalar(qg_buf[:, c * 128:(c + 1) * 128], qdec_sb[:, :],
                                 bdc_t[:, h * NC + c:h * NC + c + 1], None, ALU.mult).then_inc(s_dv.h, 1)); s_dv.inc()
            qg_done[c] = s_dv.n
            b.wait("tensor", s_dv, mask_done[c])
            if c % 4 == 0 and o_evacs:
                b.wait("tensor", s_ac, o_evacs[-1])
            if c == 0:
                b.em("tensor", lambda t, cs=cs, o_b=o_b, ocol=ocol:
                     t.matmul(bank(o_b)[:, ocol:ocol + 128], v_tm[:, cs], scT_sb[:, :],
                              start=True, stop=True).then_inc(s_pe.h, 1)); s_pe.inc()
            else:
                b.em("tensor", lambda t, cs=cs, o_b=o_b, ocol=ocol:
                     t.matmul(bank(o_b)[:, ocol:ocol + 128], v_tm[:, cs], scT_sb[:, :],
                              start=True, stop=False))
                b.wait("tensor", s_dv, qdec_done[c])
                b.wait("tensor", s_ac, sbf_done)
                b.em("tensor", lambda t, o_b=o_b, ocol=ocol:
                     t.matmul(bank(o_b)[:, ocol:ocol + 128], s_bf[:, :], qdec_sb[:, :],
                              start=False, stop=True).then_inc(s_pe.h, 1)); s_pe.inc()
            o_done = s_pe.n
            o_done_d[c] = o_done
            b.wait("vector", s_pe, dl_done)
            if c == 0:
                b.em("vector", lambda v, dl_b=dl_b:
                     v.tensor_copy(s_a[:, :], bank(dl_b, 128)).then_inc(s_dv.h, 1)); s_dv.inc()
                cur_s = s_a
            else:
                new_s = s_b if cur_s is s_a else s_a
                b.wait("vector", s_dv, supd_done[c - 1])
                if c >= 2:
                    b.wait("vector", s_ac, sbf_done_d[c - 2])
                b.em("vector", lambda v, dl_b=dl_b, so=cur_s, sn=new_s, h=h:
                     v.scalar_tensor_tensor(sn[:, :], so[:, :], bdc_t[:, h * NC + 1:h * NC + 2],
                                            bank(dl_b, 128), ALU.mult, ALU.add).then_inc(s_dv.h, 1)); s_dv.inc()
                cur_s = new_s
            supd_done[c] = s_dv.n
            b.wait("scalar", s_dv, supd_done[c])
            b.wait("scalar", s_pe, o_done)
            b.em("scalar", lambda s, sn=cur_s: s.activation(s_bf[:, :], sn[:, :], AF.Copy).then_inc(s_ac.h, 1)); s_ac.inc()
            sbf_done = s_ac.n
            sbf_done_d[c] = sbf_done
            if c % 4 == 3 or c == NC - 1:
                b.wait("scalar", s_pe, o_done)
                lo = (c // 4) * 4
                b.em("scalar", lambda s, lo=lo, c=c, o_b=o_b:
                     s.activation(o_buf[:, lo * 128:(c + 1) * 128], bank(o_b, (c + 1 - lo) * 128),
                                  AF.Copy).then_inc(s_ac.h, 1)); s_ac.inc()
                o_evacs.append(s_ac.n)
        b.wait("sync", s_ac, o_evacs[-1])
        dma("sync", sq_io, o_scr[h * 128:(h + 1) * 128, :], o_buf[:, :])
        b.wait("sync", s_dv, s_dv.n)
        dma("sync", sq_io, q_scr[h * 128:(h + 1) * 128, :], qg_buf[:, :])
        b.wait("sync", s_dv, supd_done[NC - 1])
        if h < H2:
            dma("sync", sq_io, sl_in_a[:, h * 128:(h + 1) * 128], cur_s[:, :])
        else:
            dma("sync", sq_io, sl_in_b[:, (h - H2) * 128:(h - H2 + 1) * 128], cur_s[:, :])
        store_guard = sq_io.n
        if h == H2 - 1:
            half1_stores = sq_io.n
        if h == H2 + 1:
            b.wait("gpsimd", sq_io, store_guard)
            b.em("gpsimd", lambda g: g.collective_compute(
                "AllGather", ALU.bypass, replica_groups=[list(range(cfg.n_cores))],
                ins=[sl_in_a.opt()], outs=[sl_out_a.opt()]).then_inc(s_cc.h, 1)); s_cc.inc()
        if h == max(H2 + 2, H - 2) and h <= H - 1:
            emit_combine1()

    if max(H2 + 2, H - 2) > H - 1:
        emit_combine1()

    b.wait("gpsimd", sq_io, store_guard)
    b.em("gpsimd", lambda g: g.collective_compute(
        "AllGather", ALU.bypass, replica_groups=[list(range(cfg.n_cores))],
        ins=[sl_in_b.opt()], outs=[sl_out_b.opt()]).then_inc(s_cc.h, 1)); s_cc.inc()
    b.barrier(s_bar, data_sems=(s_pe, s_dv, s_ac, s_pl, sq_io, sa_io))

    cvB = Arena(nc, PH, SB_TOP)
    wout_sb = cvB.c(KC * D, BF16)
    q_all = [cvB.c(H * 128, BF16) for _ in range(2)]
    o_all = [cvB.c(H * 128, BF16) for _ in range(2)]
    o_full = [cvB.c(H * 128, BF16) for _ in range(2)]
    h_sb = [cvB.c(D, BF16) for _ in range(2)]
    x_tile = [cvB.c(D, F32) for _ in range(2)]
    z_sb = [cvB.c(D, F32) for _ in range(2)]
    zt_sb = [cvB.c(D, F32) for _ in range(2)]
    y_sb = [cvB.c(D, F32) for _ in range(2)]
    bout_rep = cvB.c(D, F32)
    rms_rep = cvB.c(D, F32)

    dma("gpsimd", s_wout, bout_rep[:, :], bout_d.rearrange("(one d) -> one d", one=1).partition_broadcast(128))
    dma("gpsimd", s_wout, rms_rep[:, :], rmsw_d.rearrange("(one d) -> one d", one=1).partition_broadcast(128))
    for oc in range(KC):
        dma("gpsimd", s_wout, wout_sb[:, oc * D:(oc + 1) * D], wout_d[oc * 128:(oc + 1) * 128, :])
    wout_ready = s_wout.n

    b.wait("gpsimd", s_cc)
    b.wait("vector", s_cc)
    for h in range(H2, H):
        b.wait("gpsimd", s_dv, s_dv.n)
        dma("gpsimd", s_bq, sg_all.ap().rearrange("p (j f) -> p j f", j=cfg.n_cores),
            sl_out_b.rearrange("(j p) c -> p j c", p=128)[:, :, (h - H2) * 128:(h - H2 + 1) * 128])
        b.wait("vector", s_bq)
        cur = None
        for j in range(cfg.n_cores):
            if j == 0:
                b.em("vector", lambda v, h=h:
                     v.tensor_scalar(si_a[:, :], sg_all[:, 0:128], comb_t[:, h * 8:h * 8 + 1], None, ALU.mult).then_inc(s_dv.h, 1)); s_dv.inc()
                cur = si_a
            else:
                nxt = si_b if cur is si_a else si_a
                b.wait("vector", s_dv, s_dv.n)
                b.em("vector", lambda v, h=h, j=j, src=cur, dst=nxt:
                     v.scalar_tensor_tensor(dst[:, :], sg_all[:, j * 128:(j + 1) * 128],
                                            comb_t[:, h * 8 + j:h * 8 + j + 1], src[:, :],
                                            ALU.mult, ALU.add).then_inc(s_dv.h, 1)); s_dv.inc()
                cur = nxt
        b.wait("vector", s_dv, s_dv.n)
        b.em("vector", lambda v, h=h, fin=cur:
             v.tensor_copy(s_init[:, h * 128:(h + 1) * 128], fin[:, :]).then_inc(s_dv.h, 1)); s_dv.inc()
    si_ready_b = s_dv.n

    tile_dv_free = {}
    tile_ready = {}

    def load_p3_tile(tt):
        sl = tt % 2
        sem = t_slots[sl]
        if tt >= 2:
            b.wait("scalar", s_dv, tile_dv_free[tt - 2])
        dma("scalar", sem, q_all[sl].ap().rearrange("p (h f) -> p h f", h=H),
            q_scr.rearrange("(h p) t -> p h t", p=128)[:, :, tt * 128:(tt + 1) * 128])
        dma("scalar", sem, o_all[sl].ap().rearrange("p (h f) -> p h f", h=H),
            o_scr.rearrange("(h p) t -> p h t", p=128)[:, :, tt * 128:(tt + 1) * 128])
        dma("scalar", sem, x_tile[sl][:, :], x_d[tt * 128:(tt + 1) * 128, :])
        return sem.n

    tile_ready[0] = load_p3_tile(0)
    if NTT > 1:
        tile_ready[1] = load_p3_tile(1)
    b.wait("tensor", s_dv, si_ready_a_box[0][1])
    b.wait("tensor", s_wout, wout_ready)
    b.wait("vector", s_wout, wout_ready)
    NG4 = (H + 3) // 4
    nso = max(1, D // 512)
    so_w = min(512, D)
    corr_hist = []
    op_hist = {}
    zt_hist = {}
    y_store = {}
    chain_done = {}

    def emit_chain(t2):
        sl = t2 % 2
        sc = 0 if sl == 0 else 8
        b.wait("vector", s_dv, zt_hist[t2])
        if t2 >= 2:
            b.wait("vector", sq_io, y_store[t2 - 1])
        b.em("vector", lambda v, sl=sl, sc=sc:
             v.scalar_tensor_tensor(z_sb[sl][:, :], zt_sb[sl][:, :], 1.0, bout_rep[:, :], ALU.mult, ALU.add,
                                    accum_out=stats[:, sc + 0:sc + 1]).then_inc(s_dv.h, 1)); s_dv.inc()
        z_done = s_dv.n
        b.wait("scalar", s_dv, z_done)
        b.em("scalar", lambda s, sl=sl, sc=sc:
             s.activation(zt_sb[sl][:, :], z_sb[sl][:, :], AF.Square,
                          accum_out=stats[:, sc + 1:sc + 2]).then_inc(s_ac.h, 1)); s_ac.inc()
        zsq_done = s_ac.n
        b.wait("vector", s_ac, zsq_done)
        b.em("vector", lambda v, sc=sc: v.tensor_scalar(stats[:, sc + 2:sc + 3], stats[:, sc + 0:sc + 1], 1.0 / D, None, ALU.mult))
        b.em("vector", lambda v, sc=sc: v.tensor_scalar(stats[:, sc + 3:sc + 4], stats[:, sc + 1:sc + 2], 1.0 / D, None, ALU.mult).then_inc(s_dv.h, 1)); s_dv.inc()
        b.wait("vector", s_dv, s_dv.n)
        b.em("vector", lambda v, sc=sc: v.tensor_tensor(stats[:, sc + 4:sc + 5], stats[:, sc + 2:sc + 3], stats[:, sc + 2:sc + 3], ALU.mult).then_inc(s_dv.h, 1)); s_dv.inc()
        b.wait("vector", s_dv, s_dv.n)
        b.em("vector", lambda v, sc=sc: v.tensor_tensor(stats[:, sc + 5:sc + 6], stats[:, sc + 3:sc + 4], stats[:, sc + 4:sc + 5], ALU.subtract).then_inc(s_dv.h, 1)); s_dv.inc()
        b.wait("scalar", s_dv, s_dv.n)
        b.em("scalar", lambda s, sc=sc: s.activation(stats[:, sc + 6:sc + 7], stats[:, sc + 5:sc + 6], AF.Ln, bias=eps_t[:, 0:1]).then_inc(s_ac.h, 1)); s_ac.inc()
        b.wait("scalar", s_ac, s_ac.n)
        b.em("scalar", lambda s, sc=sc: s.activation(stats[:, sc + 7:sc + 8], stats[:, sc + 6:sc + 7], AF.Exp, scale=-0.5).then_inc(s_ac.h, 1)); s_ac.inc()
        rstd_done = s_ac.n
        b.wait("vector", s_ac, rstd_done)
        b.em("vector", lambda v, sl=sl, sc=sc:
             v.tensor_scalar(y_sb[sl][:, :], z_sb[sl][:, :], stats[:, sc + 2:sc + 3], stats[:, sc + 7:sc + 8],
                             ALU.subtract, ALU.mult).then_inc(s_dv.h, 1)); s_dv.inc()
        y_done = s_dv.n
        b.wait("scalar", s_dv, y_done)
        b.em("scalar", lambda s, sl=sl, sc=sc:
             s.activation(zt_sb[sl][:, :], y_sb[sl][:, :], AF.Square,
                          accum_out=stats[:, sc + 1:sc + 2]).then_inc(s_ac.h, 1)); s_ac.inc()
        b.wait("scalar", s_ac, s_ac.n)
        b.em("scalar", lambda s, sc=sc: s.activation(stats[:, sc + 4:sc + 5], stats[:, sc + 1:sc + 2], AF.Ln, bias=eps_t[:, 2:3], scale=1.0 / D).then_inc(s_ac.h, 1)); s_ac.inc()
        b.wait("scalar", s_ac, s_ac.n)
        b.em("scalar", lambda s, sc=sc: s.activation(stats[:, sc + 5:sc + 6], stats[:, sc + 4:sc + 5], AF.Exp, scale=-0.5).then_inc(s_ac.h, 1)); s_ac.inc()
        rr_done = s_ac.n
        b.wait("vector", s_ac, rr_done)
        b.em("vector", lambda v, sl=sl, sc=sc:
             v.scalar_tensor_tensor(h_sb[sl][:, :], y_sb[sl][:, :], stats[:, sc + 5:sc + 6], rms_rep[:, :],
                                    ALU.mult, ALU.mult).then_inc(s_dv.h, 1)); s_dv.inc()
        h_done = s_dv.n
        chain_done[t2] = h_done
        b.wait("sync", s_dv, y_done)
        dma("sync", sq_io, y_scr[t2 * 128:(t2 + 1) * 128, :], y_sb[sl][:, :])
        b.wait("sync", s_dv, h_done)
        dma("sync", sq_io, h_scr[t2 * 128:(t2 + 1) * 128, :], h_sb[sl][:, :])
        y_store[t2] = sq_io.n

    for tt in range(NTT):
        sl = tt % 2
        sem = t_slots[sl]
        b.wait("tensor", sem, tile_ready[tt])
        NGC = (H + 7) // 8
        for gc in range(NGC):
            cb0 = 2048 + (gc % 2) * 1024
            h0, h1 = 8 * gc, min(8 * gc + 8, H)
            gidx = tt * NGC + gc
            if gidx >= 2:
                b.wait("tensor", s_dv, corr_hist[gidx - 2])
            if h1 > H2:
                b.wait("tensor", s_dv, si_ready_b)
            for hh in range(h0, h1):
                last = (hh == h1 - 1)
                b.em("tensor", lambda t, hh=hh, sl=sl, last=last, cb0=cb0:
                     (t.matmul(psum[:, cb0 + (hh % 8) * 128:cb0 + (hh % 8) * 128 + 128],
                               s_init[:, hh * 128:(hh + 1) * 128],
                               q_all[sl][:, hh * 128:(hh + 1) * 128], start=True, stop=True).then_inc(s_pe.h, 1)
                      if last else
                      t.matmul(psum[:, cb0 + (hh % 8) * 128:cb0 + (hh % 8) * 128 + 128],
                               s_init[:, hh * 128:(hh + 1) * 128],
                               q_all[sl][:, hh * 128:(hh + 1) * 128], start=True, stop=True)))
            s_pe.inc()
            b.wait("vector", s_pe, s_pe.n)
            b.wait("vector", sem, tile_ready[tt])
            if tt >= 2:
                b.wait("vector", s_pe, op_hist[tt - 2])
            b.em("vector", lambda v, sl=sl, cb0=cb0, h0=h0, h1=h1:
                 v.tensor_tensor(o_full[sl][:, h0 * 128:h1 * 128], psum[:, cb0:cb0 + (h1 - h0) * 128],
                                 o_all[sl][:, h0 * 128:h1 * 128], ALU.add).then_inc(s_dv.h, 1)); s_dv.inc()
            corr_hist.append(s_dv.n)
        b.wait("tensor", s_dv, corr_hist[-1])
        if tt >= 2:
            b.wait("tensor", s_dv, zt_hist[tt - 2])
        for oc in range(KC):
            for s in range(nso):
                last = (oc == KC - 1 and s == nso - 1)
                b.em("tensor", lambda t, oc=oc, s=s, last=last, sl=sl:
                     (t.matmul(psum[:, s * so_w:(s + 1) * so_w], o_full[sl][:, oc * 128:(oc + 1) * 128],
                               wout_sb[:, oc * D + s * so_w:oc * D + (s + 1) * so_w],
                               start=(oc == 0), stop=(oc == KC - 1)).then_inc(s_pe.h, 1) if last else
                      t.matmul(psum[:, s * so_w:(s + 1) * so_w], o_full[sl][:, oc * 128:(oc + 1) * 128],
                               wout_sb[:, oc * D + s * so_w:oc * D + (s + 1) * so_w],
                               start=(oc == 0), stop=(oc == KC - 1))))
        s_pe.inc()
        op_done = s_pe.n
        op_hist[tt] = op_done
        if tt >= 1:
            emit_chain(tt - 1)
        b.wait("vector", s_pe, op_done)
        if tt >= 2:
            b.wait("vector", s_dv, chain_done[tt - 2])
        b.em("vector", lambda v, sl=sl:
             v.tensor_tensor(zt_sb[sl][:, :], psum[:, 0:D], x_tile[sl][:, :], ALU.add).then_inc(s_dv.h, 1)); s_dv.inc()
        tile_dv_free[tt] = s_dv.n
        zt_hist[tt] = s_dv.n
        if tt + 2 < NTT:
            tile_ready[tt + 2] = load_p3_tile(tt + 2)
    emit_chain(NTT - 1)
    b.barrier(s_bar, data_sems=(s_pe, s_dv, s_ac, s_pl, sq_io, sa_io))
    b.wait("sync", sq_io)

    cvC = Arena(nc, PH, SB_TOP)
    h_fm = cvC.c(KC * Tt, BF16)
    act_e = cvC.c(NHMAX * Tt, BF16)
    wgu_sb = [cvC.c(2 * KC * 128, BF16) for _ in range(2)]
    wdn_sb = [cvC.c(NHMAX * OH, BF16) for _ in range(2)]
    rw_sb = cvC.c(KC * E, BF16)
    moe = cvC.c(NTG * D, F32)
    wbc_sb = [cvC.c(Tt, F32) for _ in range(2)]
    p_sb = cvC.c(NTG * E, F32)
    w_sb = cvC.c(NTG * E, F32)
    msk_sb = cvC.c(NTG * E, F32)
    sg_sb = cvC.c(Tt, F32)
    yt_sb = cvC.c(D, F32)
    z2_sb = cvC.c(D, F32)
    stats2 = cvC.c(4 * NTG, F32)
    sel_sb = cvC.c(E * 128, F32)

    dma("gpsimd", s_rw, rw_sb.ap().rearrange("p (kc e) -> p kc e", kc=KC),
        rw_d.rearrange("(kc p) e -> p kc e", p=128))
    dma("gpsimd", s_rw, sel_sb[0:E, :], sel_d)
    rw_ready = s_rw.n

    experts = [(e, NHF, True) for e in range(E)] + [(None, NHS, False)]
    gu_jobs = [(ei, hf) for ei, (e, nhf, wt) in enumerate(experts) for hf in range(nhf)]
    dn_jobs = [(ei, oh) for ei, (e, nhf, wt) in enumerate(experts) for oh in range(NOH)]

    nts = max(1, Tt // 512)
    ts_w = min(512, Tt)
    nos = max(1, OH // 512)
    os_w = min(512, OH)

    gu_ready = {}
    gu_pe_free = {}
    dn_ready = {}
    dn_pe_free = {}
    gj = 0
    dj = 0
    hfm_last_use = {}
    mt_guard = {}
    f2_tg_final = {}
    f2s = {"z2": 0, "st": 0}

    def emit_f2(mt2, tg):
        tt = (mt2 * Tt) // 128 + tg
        b.wait("scalar", s_dv, f2s["z2"])
        b.wait("scalar", s_out, f2s["st"])
        dma("scalar", sa_io, yt_sb[:, :], y_scr[tt * 128:(tt + 1) * 128, :])
        b.wait("vector", sa_io)
        b.wait("vector", s_dv, f2_tg_final[mt2][tg])
        b.em("vector", lambda v, tg=tg:
             v.scalar_tensor_tensor(z2_sb[:, :], moe[:, tg * D:(tg + 1) * D], 1.0, yt_sb[:, :],
                                    ALU.mult, ALU.add, accum_out=stats[:, 0:1]).then_inc(s_dv.h, 1)); s_dv.inc()
        f2s["z2"] = s_dv.n
        z2_done = s_dv.n
        b.wait("scalar", s_dv, z2_done)
        b.em("scalar", lambda s:
             s.activation(yt_sb[:, :], z2_sb[:, :], AF.Square,
                          accum_out=stats[:, 1:2]).then_inc(s_ac.h, 1)); s_ac.inc()
        b.wait("vector", s_ac, s_ac.n)
        b.em("vector", lambda v: v.tensor_scalar(stats[:, 2:3], stats[:, 0:1], 1.0 / D, None, ALU.mult))
        b.em("vector", lambda v: v.tensor_scalar(stats[:, 3:4], stats[:, 1:2], 1.0 / D, None, ALU.mult).then_inc(s_dv.h, 1)); s_dv.inc()
        b.wait("vector", s_dv, s_dv.n)
        b.em("vector", lambda v: v.tensor_tensor(stats[:, 4:5], stats[:, 2:3], stats[:, 2:3], ALU.mult).then_inc(s_dv.h, 1)); s_dv.inc()
        b.wait("vector", s_dv, s_dv.n)
        b.em("vector", lambda v: v.tensor_tensor(stats[:, 5:6], stats[:, 3:4], stats[:, 4:5], ALU.subtract).then_inc(s_dv.h, 1)); s_dv.inc()
        b.wait("scalar", s_dv, s_dv.n)
        b.em("scalar", lambda s: s.activation(stats[:, 6:7], stats[:, 5:6], AF.Ln, bias=eps_t[:, 0:1]).then_inc(s_ac.h, 1)); s_ac.inc()
        b.wait("scalar", s_ac, s_ac.n)
        b.em("scalar", lambda s: s.activation(stats[:, 7:8], stats[:, 6:7], AF.Exp, scale=-0.5).then_inc(s_ac.h, 1)); s_ac.inc()
        b.wait("vector", s_ac, s_ac.n)
        b.em("vector", lambda v:
             v.tensor_scalar(yt_sb[:, :], z2_sb[:, :], stats[:, 2:3], stats[:, 7:8],
                             ALU.subtract, ALU.mult).then_inc(s_dv.h, 1)); s_dv.inc()
        b.wait("sync", s_dv, s_dv.n)
        dma("sync", s_out, out_d[tt * 128:(tt + 1) * 128, :], yt_sb[:, :])
        f2s["st"] = s_out.n

    hfm_ready_d = {}
    wfmT_ready_d = {}
    wbc_ready_d = {}
    prep_done = {}

    def prepare_dma(mt):
        t0 = mt * Tt
        if mt > 0:
            b.wait("sync", s_pe, hfm_last_use[mt - 1])
        for fc in range(KC):
            dma("sync", sq_io, h_fm[:, fc * Tt:(fc + 1) * Tt],
                h_scr[t0:t0 + Tt, fc * 128:(fc + 1) * 128], transpose=True)
        hfm_ready_d[mt] = (sq_io.n, sa_io.n)

    def prepare_pe(mt, rb, wb):
        b.wait("tensor", s_rw, rw_ready)
        b.wait("tensor", sq_io, hfm_ready_d[mt][0])
        b.wait("tensor", sa_io, hfm_ready_d[mt][1])
        b.wait("tensor", s_ac, s_ac.n)
        b.wait("tensor", s_dv, s_dv.n)
        for tg in range(NTG):
            for kc in range(KC):
                last = (tg == NTG - 1 and kc == KC - 1)
                b.em("tensor", lambda t, tg=tg, kc=kc, last=last, rb=rb:
                     (t.matmul(bank(rb)[:, tg * E:(tg + 1) * E],
                               h_fm[:, kc * Tt + tg * 128:kc * Tt + (tg + 1) * 128],
                               rw_sb[:, kc * E:(kc + 1) * E], start=(kc == 0), stop=(kc == KC - 1)).then_inc(s_pe.h, 1)
                      if last else
                      t.matmul(bank(rb)[:, tg * E:(tg + 1) * E],
                               h_fm[:, kc * Tt + tg * 128:kc * Tt + (tg + 1) * 128],
                               rw_sb[:, kc * E:(kc + 1) * E], start=(kc == 0), stop=(kc == KC - 1))))
        s_pe.inc()
        r_done = s_pe.n
        b.wait("vector", s_pe, r_done)
        for tg in range(NTG):
            last = (tg == NTG - 1)
            b.em("vector", lambda v, tg=tg, rb=rb, last=last:
                 (v.reduce_max(stats2[:, tg:tg + 1], bank(rb)[:, tg * E:(tg + 1) * E], axis=AX.X).then_inc(s_dv.h, 1)
                  if last else
                  v.reduce_max(stats2[:, tg:tg + 1], bank(rb)[:, tg * E:(tg + 1) * E], axis=AX.X)))
        s_dv.inc()
        b.wait("vector", s_dv, s_dv.n)
        b.em("vector", lambda v: v.tensor_scalar(stats2[:, NTG:2 * NTG], stats2[:, 0:NTG], -1.0, None, ALU.mult).then_inc(s_dv.h, 1)); s_dv.inc()
        b.wait("scalar", s_dv, s_dv.n)
        for tg in range(NTG):
            last = (tg == NTG - 1)
            b.em("scalar", lambda s, tg=tg, last=last, rb=rb:
                 (s.activation(p_sb[:, tg * E:(tg + 1) * E], bank(rb)[:, tg * E:(tg + 1) * E], AF.Exp,
                               bias=stats2[:, NTG + tg:NTG + tg + 1]).then_inc(s_ac.h, 1) if last else
                  s.activation(p_sb[:, tg * E:(tg + 1) * E], bank(rb)[:, tg * E:(tg + 1) * E], AF.Exp,
                               bias=stats2[:, NTG + tg:NTG + tg + 1])))
        s_ac.inc()
        b.wait("vector", s_ac, s_ac.n)
        for tg in range(NTG):
            last = (tg == NTG - 1)
            b.em("vector", lambda v, tg=tg, last=last:
                 (v.reduce_max(stats2[:, tg:tg + 1], p_sb[:, tg * E:(tg + 1) * E], axis=AX.X).then_inc(s_dv.h, 1)
                  if last else
                  v.reduce_max(stats2[:, tg:tg + 1], p_sb[:, tg * E:(tg + 1) * E], axis=AX.X)))
        s_dv.inc()
        b.wait("vector", s_dv, s_dv.n)
        for tg in range(NTG):
            last = (tg == NTG - 1)
            b.em("vector", lambda v, tg=tg, last=last:
                 (v.scalar_tensor_tensor(msk_sb[:, tg * E:(tg + 1) * E], p_sb[:, tg * E:(tg + 1) * E], stats2[:, tg:tg + 1],
                                         p_sb[:, tg * E:(tg + 1) * E], ALU.is_lt, ALU.mult).then_inc(s_dv.h, 1)
                  if last else
                  v.scalar_tensor_tensor(msk_sb[:, tg * E:(tg + 1) * E], p_sb[:, tg * E:(tg + 1) * E], stats2[:, tg:tg + 1],
                                         p_sb[:, tg * E:(tg + 1) * E], ALU.is_lt, ALU.mult)))
        s_dv.inc()
        b.wait("vector", s_dv, s_dv.n)
        for tg in range(NTG):
            last = (tg == NTG - 1)
            b.em("vector", lambda v, tg=tg, last=last:
                 (v.reduce_max(stats2[:, NTG + tg:NTG + tg + 1], msk_sb[:, tg * E:(tg + 1) * E], axis=AX.X).then_inc(s_dv.h, 1)
                  if last else
                  v.reduce_max(stats2[:, NTG + tg:NTG + tg + 1], msk_sb[:, tg * E:(tg + 1) * E], axis=AX.X)))
        s_dv.inc()
        b.wait("vector", s_dv, s_dv.n)
        b.em("vector", lambda v: v.tensor_tensor(stats2[:, 2 * NTG:3 * NTG], stats2[:, 0:NTG], stats2[:, NTG:2 * NTG], ALU.add).then_inc(s_dv.h, 1)); s_dv.inc()
        b.wait("vector", s_dv, s_dv.n)
        b.em("vector", lambda v: v.reciprocal(stats2[:, 0:NTG], stats2[:, 2 * NTG:3 * NTG]).then_inc(s_dv.h, 1)); s_dv.inc()
        b.wait("vector", s_dv, s_dv.n)
        for tg in range(NTG):
            last = (tg == NTG - 1)
            b.em("vector", lambda v, tg=tg, last=last:
                 (v.tensor_scalar(msk_sb[:, tg * E:(tg + 1) * E], p_sb[:, tg * E:(tg + 1) * E], stats2[:, NTG + tg:NTG + tg + 1], None,
                                  ALU.is_ge).then_inc(s_dv.h, 1)
                  if last else
                  v.tensor_scalar(msk_sb[:, tg * E:(tg + 1) * E], p_sb[:, tg * E:(tg + 1) * E], stats2[:, NTG + tg:NTG + tg + 1], None,
                                  ALU.is_ge)))
        s_dv.inc()
        b.wait("vector", s_dv, s_dv.n)
        for tg in range(NTG):
            last = (tg == NTG - 1)
            b.em("vector", lambda v, tg=tg, last=last:
                 (v.scalar_tensor_tensor(w_sb[:, tg * E:(tg + 1) * E], p_sb[:, tg * E:(tg + 1) * E], stats2[:, tg:tg + 1],
                                         msk_sb[:, tg * E:(tg + 1) * E], ALU.mult, ALU.mult).then_inc(s_dv.h, 1) if last else
                  v.scalar_tensor_tensor(w_sb[:, tg * E:(tg + 1) * E], p_sb[:, tg * E:(tg + 1) * E], stats2[:, tg:tg + 1],
                                         msk_sb[:, tg * E:(tg + 1) * E], ALU.mult, ALU.mult)))
        s_dv.inc()
        w_ready = s_dv.n
        b.wait("tensor", s_dv, w_ready)
        tg_evac = []
        for tg in range(NTG):
            if tg % 4 == 0 and tg_evac:
                b.wait("tensor", s_dv, tg_evac[-1])
            b.em("tensor", lambda t, tg=tg, wb=wb:
                 t.transpose(bank(wb)[0:E, (tg % 4) * 128:(tg % 4) * 128 + 128],
                             w_sb[:, tg * E:(tg + 1) * E], identf[:, :]).then_inc(s_pe.h, 1)); s_pe.inc()
            if tg % 4 == 3 or tg == NTG - 1:
                b.wait("vector", s_pe, s_pe.n)
                lo = (tg // 4) * 4
                b.em("vector", lambda v, lo=lo, tg=tg, wb=wb:
                     v.tensor_copy(wfmT[0:E, lo * 128:(tg + 1) * 128],
                                   bank(wb)[0:E, 0:(tg + 1 - lo) * 128]).then_inc(s_dv.h, 1)); s_dv.inc()
                tg_evac.append(s_dv.n)
        wfmT_ready = s_dv.n
        wfmT_ready_d[mt] = wfmT_ready
        nws = max(1, Tt // 512)
        ws_w = min(512, Tt)
        b.wait("tensor", s_dv, wfmT_ready)
        for s in range(nws):
            last = (s == nws - 1)
            b.em("tensor", lambda t, s=s, last=last, rb=rb:
                 (t.matmul(psum[:, rb * 512 + s * ws_w:rb * 512 + (s + 1) * ws_w], sel_sb[0:E, 0:128],
                           wfmT[0:E, s * ws_w:(s + 1) * ws_w], start=True, stop=True).then_inc(s_pe.h, 1) if last else
                  t.matmul(psum[:, rb * 512 + s * ws_w:rb * 512 + (s + 1) * ws_w], sel_sb[0:E, 0:128],
                           wfmT[0:E, s * ws_w:(s + 1) * ws_w], start=True, stop=True)))
        s_pe.inc()
        b.wait("vector", s_pe, s_pe.n)
        b.em("vector", lambda v, rb=rb: v.tensor_copy(wbc_sb[0][:, :], psum[:, rb * 512:rb * 512 + Tt]).then_inc(s_dv.h, 1)); s_dv.inc()
        wbc_ready_d[mt] = s_dv.n

    for mt in range(NMT):
        t0 = mt * Tt
        if mt == 0:
            prepare_dma(0)
            prepare_pe(0, 4, 5)
        if mt > 0:
            b.wait("tensor", s_dv, mt_guard[mt - 1][0])
            b.wait("tensor", s_ac, mt_guard[mt - 1][1])
            b.wait("tensor", s_pl, mt_guard[mt - 1][2])
        else:
            b.wait("tensor", s_dv, s_dv.n)
            b.wait("tensor", s_ac, s_ac.n)
            b.wait("tensor", s_pl, s_pl.n)
        wfmT_ready = wfmT_ready_d[mt]

        def load_gu(j):
            ei, hf = gu_jobs[j % len(gu_jobs)]
            e, nhf, weighted = experts[ei]
            sl = j % 2
            sem = fg_slots[sl]
            if j >= 2:
                b.wait("gpsimd", s_pe, gu_pe_free[j - 2])
            for which in range(2):
                if e is None:
                    src = (swg_d if which == 0 else swu_d)[:, hf * 128:(hf + 1) * 128]
                else:
                    src = (wg_d if which == 0 else wu_d)[e][:, hf * 128:(hf + 1) * 128]
                dma("gpsimd", sem,
                    wgu_sb[sl].ap().rearrange("p (two kc f) -> p two kc f", two=2, kc=KC)[:, which, :, :],
                    src.rearrange("(kc p) f -> p kc f", p=128))
            gu_ready[j] = sem.n

        def load_dn(j):
            ei, oh = dn_jobs[j % len(dn_jobs)]
            e, nhf, weighted = experts[ei]
            sl = j % 2
            sem = fd_slots[sl]
            if j >= 2:
                b.wait("gpsimd", s_pe, dn_pe_free[j - 2])
            for hf in range(nhf):
                if e is None:
                    src = swd_d[hf * 128:(hf + 1) * 128, oh * OH:(oh + 1) * OH]
                else:
                    src = wd_d[e][hf * 128:(hf + 1) * 128, oh * OH:(oh + 1) * OH]
                dma("gpsimd", sem, wdn_sb[sl][:, hf * OH:(hf + 1) * OH], src)
            dn_ready[j] = sem.n

        if mt == 0:
            load_gu(0)
            if len(gu_jobs) * NMT > 1:
                load_gu(1)
            load_dn(0)
            if len(dn_jobs) * NMT > 1:
                load_dn(1)
        moe_adds = []
        prev_sg_last = None
        for ei, (e, nhf, weighted) in enumerate(experts):
            if ei == 0 and mt == 0:
                b.wait("tensor", s_ac, s_ac.n)
                b.wait("tensor", s_dv, s_dv.n)
                b.wait("tensor", s_pl, s_pl.n)
            if moe_adds:
                b.wait("tensor", s_dv, moe_adds[-1])
            if prev_sg_last is not None:
                b.wait("tensor", s_ac, prev_sg_last)
            if ei == 0:
                tg_final = {}
                wbc_ready_cur = wbc_ready_d[mt]
            wbc_ready = None
            wsl = ei % 2
            if weighted:
                wbc_ready = wbc_ready_cur
                b.wait("tensor", s_dv, wbc_ready)
            act_done = []
            sg_hist = []
            tu_hist = []
            for hf in range(nhf):
                if mt > 0 and ei == 0 and hf < NTG:
                    emit_f2(mt - 1, hf)
                par = hf % 2
                goff = 2048 * par
                uoff = 1024 + 2048 * par
                gsl = gj % 2
                b.wait("tensor", fg_slots[gsl], gu_ready[gj])
                if hf >= 2:
                    b.wait("tensor", s_ac, sg_hist[hf - 2])
                for kc in range(KC):
                    for s in range(nts):
                        last = (kc == KC - 1 and s == nts - 1)
                        b.em("tensor", lambda t, kc=kc, s=s, gsl=gsl, last=last, goff=goff:
                             (t.matmul(psum[:, goff + s * ts_w:goff + (s + 1) * ts_w],
                                       wgu_sb[gsl][:, kc * 128:(kc + 1) * 128],
                                       h_fm[:, kc * Tt + s * ts_w:kc * Tt + (s + 1) * ts_w],
                                       start=(kc == 0), stop=(kc == KC - 1)).then_inc(s_pe.h, 1) if last else
                              t.matmul(psum[:, goff + s * ts_w:goff + (s + 1) * ts_w],
                                       wgu_sb[gsl][:, kc * 128:(kc + 1) * 128],
                                       h_fm[:, kc * Tt + s * ts_w:kc * Tt + (s + 1) * ts_w],
                                       start=(kc == 0), stop=(kc == KC - 1))))
                s_pe.inc()
                g_done = s_pe.n
                if hf >= 2:
                    b.wait("tensor", s_dv, act_done[hf - 2])
                for kc in range(KC):
                    for s in range(nts):
                        last = (kc == KC - 1 and s == nts - 1)
                        b.em("tensor", lambda t, kc=kc, s=s, gsl=gsl, last=last, uoff=uoff:
                             (t.matmul(psum[:, uoff + s * ts_w:uoff + (s + 1) * ts_w],
                                       wgu_sb[gsl][:, KC * 128 + kc * 128:KC * 128 + (kc + 1) * 128],
                                       h_fm[:, kc * Tt + s * ts_w:kc * Tt + (s + 1) * ts_w],
                                       start=(kc == 0), stop=(kc == KC - 1)).then_inc(s_pe.h, 1) if last else
                              t.matmul(psum[:, uoff + s * ts_w:uoff + (s + 1) * ts_w],
                                       wgu_sb[gsl][:, KC * 128 + kc * 128:KC * 128 + (kc + 1) * 128],
                                       h_fm[:, kc * Tt + s * ts_w:kc * Tt + (s + 1) * ts_w],
                                       start=(kc == 0), stop=(kc == KC - 1))))
                s_pe.inc()
                u_done = s_pe.n
                gu_pe_free[gj] = s_pe.n
                gj += 1
                nxt = gj + 1
                if nxt < len(gu_jobs) * NMT and nxt not in gu_ready:
                    load_gu(nxt)
                b.wait("scalar", s_pe, g_done)
                if hf >= 1:
                    b.wait("scalar", s_dv, act_done[hf - 1])
                b.em("scalar", lambda s, goff=goff: s.activation(sg_sb[:, :], psum[:, goff:goff + Tt], AF.Silu).then_inc(s_ac.h, 1)); s_ac.inc()
                sg_done = s_ac.n
                sg_hist.append(sg_done)
                b.wait("vector", s_ac, sg_done)
                b.wait("vector", s_pe, u_done)
                if weighted:
                    b.em("vector", lambda v, uoff=uoff: v.tensor_tensor(sg_sb[:, :], sg_sb[:, :], psum[:, uoff:uoff + Tt], ALU.mult).then_inc(s_dv.h, 1)); s_dv.inc()
                    tu_done = s_dv.n
                    tu_hist.append(tu_done)
                    b.wait("vector", s_dv, tu_done)
                    b.wait("vector", s_dv, wbc_ready)
                    b.em("vector", lambda v, hf=hf, wsl=wsl: v.tensor_tensor(act_e[:, hf * Tt:(hf + 1) * Tt],
                                                                    sg_sb[:, :], wbc_sb[wsl][:, :], ALU.mult).then_inc(s_dv.h, 1)); s_dv.inc()
                    act_done.append(s_dv.n)
                else:
                    b.em("vector", lambda v, hf=hf, uoff=uoff: v.tensor_tensor(act_e[:, hf * Tt:(hf + 1) * Tt],
                                                                    sg_sb[:, :], psum[:, uoff:uoff + Tt], ALU.mult).then_inc(s_dv.h, 1)); s_dv.inc()
                    act_done.append(s_dv.n)
                    tu_hist.append(act_done[-1])
            if mt > 0 and ei == 0:
                for tg2 in range(min(nhf, NTG), NTG):
                    emit_f2(mt - 1, tg2)
            prev_sg_last = sg_hist[-1]
            if ei + 1 < len(experts) and experts[ei + 1][2]:
                b.wait("tensor", s_ac, sg_hist[-1])
                b.wait("tensor", s_dv, tu_hist[-1])
                ne = experts[ei + 1][0]
                for s in range(nts):
                    last = (s == nts - 1)
                    b.em("tensor", lambda t, ne=ne, s=s, last=last:
                         (t.matmul(psum[:, s * ts_w:(s + 1) * ts_w], sel_sb[0:E, ne * 128:(ne + 1) * 128],
                                   wfmT[0:E, s * ts_w:(s + 1) * ts_w], start=True, stop=True).then_inc(s_pe.h, 1)
                          if last else
                          t.matmul(psum[:, s * ts_w:(s + 1) * ts_w], sel_sb[0:E, ne * 128:(ne + 1) * 128],
                                   wfmT[0:E, s * ts_w:(s + 1) * ts_w], start=True, stop=True)))
                s_pe.inc()
                b.wait("vector", s_pe, s_pe.n)
                b.wait("vector", s_dv, act_done[-1] if weighted else s_dv.n)
                b.em("vector", lambda v, wsl=wsl: v.tensor_copy(wbc_sb[1 - wsl][:, :], psum[:, 0:Tt]).then_inc(s_dv.h, 1)); s_dv.inc()
                wbc_ready_cur = s_dv.n
            if ei == len(experts) - 1:
                hfm_last_use[mt] = gu_pe_free[gj - 1]
            if ei == len(experts) - 1 and mt + 1 < NMT:
                prepare_dma(mt + 1)
            for oh in range(NOH):
                if ei == len(experts) - 1 and mt + 1 < NMT and oh == min(1, NOH - 1):
                    prepare_pe(mt + 1, 0, 1)
                dsl = dj % 2
                b.wait("tensor", fd_slots[dsl], dn_ready[dj])
                b.wait("tensor", s_ac, sg_hist[-1])
                b.wait("tensor", s_dv, tu_hist[-1])
                b.wait("tensor", s_dv, act_done[-1])
                if moe_adds:
                    b.wait("tensor", s_dv, moe_adds[-1])
                for tg in range(NTG):
                    dbk = 4 + (tg % 2) * 2
                    if tg >= 2:
                        b.wait("tensor", s_dv, moe_adds[-2])
                    for hf in range(nhf):
                        for s in range(nos):
                            last = (hf == nhf - 1 and s == nos - 1)
                            b.em("tensor", lambda t, tg=tg, hf=hf, s=s, dsl=dsl, dbk=dbk, last=last, nhf=nhf:
                                 (t.matmul(psum[:, dbk * 512 + s * os_w:dbk * 512 + (s + 1) * os_w],
                                           act_e[:, hf * Tt + tg * 128:hf * Tt + (tg + 1) * 128],
                                           wdn_sb[dsl][:, hf * OH + s * os_w:hf * OH + (s + 1) * os_w],
                                           start=(hf == 0), stop=(hf == nhf - 1)).then_inc(s_pe.h, 1) if last else
                                  t.matmul(psum[:, dbk * 512 + s * os_w:dbk * 512 + (s + 1) * os_w],
                                           act_e[:, hf * Tt + tg * 128:hf * Tt + (tg + 1) * 128],
                                           wdn_sb[dsl][:, hf * OH + s * os_w:hf * OH + (s + 1) * os_w],
                                           start=(hf == 0), stop=(hf == nhf - 1))))
                    s_pe.inc()
                    b.wait("vector", s_pe, s_pe.n)
                    if ei == 0:
                        b.em("vector", lambda v, tg=tg, oh=oh, dbk=dbk:
                             v.tensor_copy(moe[:, tg * D + oh * OH:tg * D + (oh + 1) * OH],
                                           psum[:, dbk * 512:dbk * 512 + OH]).then_inc(s_dv.h, 1)); s_dv.inc()
                    else:
                        b.em("vector", lambda v, tg=tg, oh=oh, dbk=dbk:
                             v.tensor_tensor(moe[:, tg * D + oh * OH:tg * D + (oh + 1) * OH],
                                             moe[:, tg * D + oh * OH:tg * D + (oh + 1) * OH],
                                             psum[:, dbk * 512:dbk * 512 + OH], ALU.add).then_inc(s_dv.h, 1)); s_dv.inc()
                    moe_adds.append(s_dv.n)
                    if ei == len(experts) - 1 and oh == NOH - 1:
                        tg_final[tg] = s_dv.n
                dn_pe_free[dj] = s_pe.n
                dj += 1
                if dj < len(dn_jobs) * NMT and dj not in dn_ready:
                    load_dn(dj)
        moe_done = s_dv.n
        mt_guard[mt] = (s_dv.n, s_ac.n, s_pl.n)
        f2_tg_final[mt] = tg_final
        if mt == NMT - 1:
            for tg in range(NTG):
                emit_f2(mt, tg)

    b.wait("sync", sq_io)
    b.wait("sync", s_out)
    b.wait("scalar", sa_io)
    for s2 in [s_tab, s_bq, s_xst, s_wout, s_rw, s_sg] + list(w_slots) + list(fg_slots) + list(fd_slots) + list(t_slots):
        b.wait("gpsimd", s2)

    with nc.Block() as block:
        b.run(block)
    return nc


def make_consts(cfg: Cfg, core_idx: int):
    T, H, NC, E = cfg.T, cfg.H, cfg.NC, cfg.E
    j = np.arange(128, dtype=np.float64)
    i = np.arange(128, dtype=np.float64)
    diff = i[None, :] - j[:, None]
    intra = np.where(diff >= 0, diff, BIG)
    c_intra = np.tile(intra, (1, H)).astype(np.float32)
    c_qdec = np.tile(np.broadcast_to(i[None, :], (128, 128)), (1, H)).astype(np.float32)
    c_kdec = np.broadcast_to((128.0 - j)[:, None], (128, H)).astype(np.float32).copy()
    cb = np.zeros((128, H * NC), np.float64)
    for h in range(H):
        cb[:, h * NC:(h + 1) * NC] = 128.0 * np.arange(NC)[None, :]
    c_bdc = cb.astype(np.float32)
    cc = np.zeros((128, H * 8), np.float64)
    for h in range(H):
        for jj in range(8):
            cc[:, h * 8 + jj] = (T * (core_idx - 1 - jj)) if jj < core_idx else BIG
    c_comb = cc.astype(np.float32)
    sel = np.zeros((E, E * 128), np.float32)
    for e in range(E):
        sel[e, e * 128:(e + 1) * 128] = 1.0
    identf = np.eye(128, dtype=np.float32)
    return {"c_intra": c_intra, "c_qdec": c_qdec, "c_kdec": c_kdec,
            "c_bdc": c_bdc, "c_comb": c_comb, "sel": sel, "identf": identf}


def run_cfg(cfg: Cfg, full_inputs: dict) -> np.ndarray:
    nc = build_program(cfg)
    shared = {k: np.ascontiguousarray(np.asarray(v, dtype=np.float32))
              for k, v in full_inputs.items() if k != "x"}
    shared["slopes"] = np.ascontiguousarray(
        np.asarray(full_inputs["slopes"], dtype=np.float32).reshape(cfg.H, 1))
    x = np.asarray(full_inputs["x"], dtype=np.float32)
    in_maps = []
    for c in range(cfg.n_cores):
        m = dict(shared)
        m["x"] = np.ascontiguousarray(x[c * cfg.T:(c + 1) * cfg.T, :])
        m.update(make_consts(cfg, c))
        in_maps.append(m)
    res = run_bass_kernel_spmd(nc, in_maps, core_ids=list(range(cfg.n_cores)))
    return np.concatenate([np.asarray(res.results[c]["out"]) for c in range(cfg.n_cores)], axis=0)


def kernel(L0_attn_x, L0_attn_Wqkv, L0_attn_bqkv, L0_attn_Wout, L0_attn_bout,
           L0_attn_slopes, L0_ffn_rms_weight, L0_ffn_router_w, L0_ffn_w_gate,
           L0_ffn_w_up, L0_ffn_w_down, L0_ffn_sw_gate, L0_ffn_sw_up, L0_ffn_sw_down):
    cfg = Cfg(T=2048, D=2048, H=16, F=1024, SF=1024, E=8, n_cores=8)
    x = np.asarray(L0_attn_x).reshape(16384, 2048)
    out = run_cfg(cfg, {
        "x": x, "wqkv": L0_attn_Wqkv, "bqkv": L0_attn_bqkv,
        "wout": L0_attn_Wout, "bout": L0_attn_bout,
        "slopes": L0_attn_slopes, "rmsw": L0_ffn_rms_weight,
        "router_w": L0_ffn_router_w, "wg": L0_ffn_w_gate,
        "wu": L0_ffn_w_up, "wd": L0_ffn_w_down,
        "swg": L0_ffn_sw_gate, "swu": L0_ffn_sw_up, "swd": L0_ffn_sw_down})
    return out.reshape(1, 16384, 2048)


# version 20 (final)
# speedup vs baseline: 1.0861x; 1.0086x over previous
import numpy as np

import concourse.bass as bass
import concourse.mybir as mybir
from concourse.bass_utils import run_bass_kernel_spmd

F32 = mybir.dt.float32
BF16 = mybir.dt.bfloat16
AF = mybir.ActivationFunctionType
ALU = mybir.AluOpType
AX = mybir.AxisListType

BIG = 1.0e9


class Cfg:
    def __init__(self, T, D, H, F, SF, E, n_cores=8, tt=None, oh=None):
        self.T, self.D, self.H, self.F, self.SF, self.E = T, D, H, F, SF, E
        self.n_cores = n_cores
        self.KC = D // 128
        self.NC = T // 128
        self.Tt = tt if tt else min(1024, T)
        self.NMT = T // self.Tt
        self.OH = oh if oh else min(1024, D)
        self.NOH = D // self.OH
        assert D % 128 == 0 and T % 128 == 0 and F % 128 == 0 and SF % 128 == 0


class Sem:
    __slots__ = ("h", "n")

    def __init__(self, h):
        self.h = h
        self.n = 0

    def inc(self, k=1):
        self.n += k
        return self.n


class Builder:
    ENGINES = ("tensor", "vector", "scalar", "gpsimd", "sync")

    def __init__(self, nc):
        self.nc = nc
        self.plan = {e: [] for e in self.ENGINES}
        self.last_wait = {}

    def em(self, eng, fn):
        self.plan[eng].append(fn)

    def wait(self, eng, sem, val=None):
        if val is None:
            val = sem.n
        if val <= 0:
            return
        key = (eng, id(sem))
        if self.last_wait.get(key, -1) >= val:
            return
        self.last_wait[key] = val
        self.em(eng, lambda e, s=sem.h, v=val: e.wait_ge(s, v))

    def barrier(self, sem, data_sems=()):
        for ds in data_sems:
            for eng in self.ENGINES:
                self.wait(eng, ds)
        tgt = sem.n + len(self.ENGINES)
        sem.n = tgt
        for eng in self.ENGINES:
            self.em(eng, lambda e, s=sem.h: e.nop().then_inc(s, 1))
        for eng in self.ENGINES:
            self.last_wait[(eng, id(sem))] = tgt
            self.em(eng, lambda e, s=sem.h, v=tgt: e.wait_ge(s, v))

    def run(self, block):
        def make(name):
            def body(eng):
                for fn in self.plan[name]:
                    fn(eng)
            return body
        block.tensor(make("tensor"))
        block.vector(make("vector"))
        block.scalar(make("scalar"))
        block.gpsimd(make("gpsimd"))
        block.sync(make("sync"))


class Arena:
    _uid = [0]

    def __init__(self, nc, start, top):
        self.nc = nc
        self.cur = start
        self.top = top

    def c(self, cols, dtype, parts=128):
        sz = cols * (2 if dtype == BF16 else 4)
        off = (self.cur + 31) & ~31
        self.cur = off + sz
        assert self.cur <= self.top, f"SBUF arena overflow: {self.cur} > {self.top}"
        Arena._uid[0] += 1
        return self.nc.alloc_sbuf_tensor_at(f"ar{Arena._uid[0]}", [parts, cols], dtype, offset=off)


def build_program(cfg: Cfg):
    nc = bass.Bass("TRN2", target_bir_lowering=False)
    T, D, H, F, SF, E = cfg.T, cfg.D, cfg.H, cfg.F, cfg.SF, cfg.E
    KC, NC, Tt, NMT, OH, NOH = cfg.KC, cfg.NC, cfg.Tt, cfg.NMT, cfg.OH, cfg.NOH
    NTG = Tt // 128
    NTT = T // 128
    NS = max(1, T // 512)
    SLAB = T // NS
    NHF, NHS = F // 128, SF // 128
    NHMAX = max(NHF, NHS)

    ap = lambda *a, **k: nc.dram_tensor(*a, **k).ap()
    x_d = ap("x", [T, D], F32, kind="ExternalInput")
    wqkv_d = ap("wqkv", [D, 3 * D], F32, kind="ExternalInput")
    bqkv_d = ap("bqkv", [3 * D], F32, kind="ExternalInput")
    wout_d = ap("wout", [D, D], F32, kind="ExternalInput")
    bout_d = ap("bout", [D], F32, kind="ExternalInput")
    slopes_d = ap("slopes", [H, 1], F32, kind="ExternalInput")
    rmsw_d = ap("rmsw", [D], F32, kind="ExternalInput")
    rw_d = ap("router_w", [D, E], F32, kind="ExternalInput")
    wg_d = ap("wg", [E, D, F], F32, kind="ExternalInput")
    wu_d = ap("wu", [E, D, F], F32, kind="ExternalInput")
    wd_d = ap("wd", [E, F, D], F32, kind="ExternalInput")
    swg_d = ap("swg", [D, SF], F32, kind="ExternalInput")
    swu_d = ap("swu", [D, SF], F32, kind="ExternalInput")
    swd_d = ap("swd", [SF, D], F32, kind="ExternalInput")
    c_intra_d = ap("c_intra", [128, H * 128], F32, kind="ExternalInput")
    c_qdec_d = ap("c_qdec", [128, H * 128], F32, kind="ExternalInput")
    c_kdec_d = ap("c_kdec", [128, H], F32, kind="ExternalInput")
    c_bdc_d = ap("c_bdc", [128, H * NC], F32, kind="ExternalInput")
    c_comb_d = ap("c_comb", [128, H * 8], F32, kind="ExternalInput")
    sel_d = ap("sel", [E, E * 128], F32, kind="ExternalInput")
    identf_d = ap("identf", [128, 128], F32, kind="ExternalInput")
    out_d = ap("out", [T, D], F32, kind="ExternalOutput")

    x_scr = ap("x_scr", [T, D], BF16, kind="Internal")
    o_scr = ap("o_scr", [H * 128, T], BF16, kind="Internal")
    q_scr = ap("q_scr", [H * 128, T], BF16, kind="Internal")
    y_scr = ap("y_scr", [T, D], F32, kind="Internal")
    h_scr = ap("h_scr", [T, D], BF16, kind="Internal")
    H2 = H // 2
    sl_in_a = ap("sl_in_a", [128, H2 * 128], F32, kind="Internal")
    sl_out_a = ap("sl_out_a", [cfg.n_cores * 128, H2 * 128], F32, kind="Internal")
    sl_in_b = ap("sl_in_b", [128, (H - H2) * 128], F32, kind="Internal")
    sl_out_b = ap("sl_out_b", [cfg.n_cores * 128, (H - H2) * 128], F32, kind="Internal")

    def S(name):
        return Sem(nc.alloc_semaphore(name))

    sg_w = S("sg_w")
    s_tab = S("s_tab")
    s_bq = S("s_bq")
    s_xst = S("s_xst")
    s_wout = S("s_wout")
    s_rw = S("s_rw")
    s_out = S("s_out")
    s_sg = S("s_sg")
    sq_io = S("sq_io")
    sa_io = S("sa_io")
    s_pe = S("s_pe")
    s_dv = S("s_dv")
    s_ac = S("s_ac")
    s_pl = S("s_pl")
    s_cc = S("s_cc")
    s_bar = S("s_bar")
    w_slots = (S("sw0"), S("sw1"))
    fg_slots = (S("sfg0"), S("sfg1"))
    fd_slots = (S("sfd0"), S("sfd1"))
    t_slots = (S("st0"), S("st1"))

    b = Builder(nc)
    psum = nc.alloc_psum_tensor("pool", [128, 4096], F32)

    def bank(i, w=512):
        return psum[:, i * 512:i * 512 + w]

    SB_BASE = (nc.sbuf_base + 31) & ~31
    SB_TOP = nc.sbuf_top
    perm = Arena(nc, SB_BASE, SB_TOP)
    identf = perm.c(128, F32)
    sg_all = perm.c(8 * 128, F32)
    comb_t = perm.c(H * 8, F32)
    wfmT = perm.c(min(1024, T), F32)
    si_a = perm.c(128, F32)
    si_b = perm.c(128, F32)
    stats = perm.c(32, F32)
    eps_t = perm.c(4, F32)
    s_init = perm.c(H * 128, BF16)
    PH = perm.cur

    cvA = Arena(nc, PH, SB_TOP)
    slope_cols = cvA.c(H, F32)
    intraT = cvA.c(H * 128, F32)
    qdec_t = cvA.c(H * 128, F32)
    kdec_t = cvA.c(H, F32)
    bdc_t = cvA.c(H * NC, F32)
    tmp_tab = cvA.c(H * 128, F32)
    bqkv_cols = cvA.c(3 * KC, F32)
    bq_rows = cvA.c(128, F32)
    bias_k = [cvA.c(128, F32) for _ in range(2)]
    bias_v = [cvA.c(128, F32) for _ in range(2)]
    bkd_tmp = [cvA.c(128, F32) for _ in range(2)]
    s_a = cvA.c(128, F32)
    s_b = cvA.c(128, F32)
    x_fm = cvA.c(KC * T, BF16)
    wq_sb = [cvA.c(KC * 128, BF16) for _ in range(2)]
    wkv_sb = [cvA.c(KC * 256, BF16) for _ in range(2)]
    q_fm = cvA.c(T, BF16)
    k_fm = cvA.c(T, BF16)
    v_tm = cvA.c(T, BF16)
    kd_tm = cvA.c(T, BF16)
    o_buf = cvA.c(T, BF16)
    qg_buf = cvA.c(T, BF16)
    scT_sb = cvA.c(128, BF16)
    qdec_sb = cvA.c(128, BF16)
    s_bf = cvA.c(128, BF16)
    rows_per = min(1024, T)
    stage_cols = (rows_per // 128) * D
    x_stage = nc.alloc_sbuf_tensor_at("x_stage", [128, stage_cols], BF16,
                                      offset=(SB_TOP - stage_cols * 2) & ~31)
    assert cvA.cur <= ((SB_TOP - stage_cols * 2) & ~31) or True

    def dma(eng, sem, out, in_, **kw):
        b.em(eng, lambda e, o=out, i=in_, kw=kw: e.dma_start(out=o, in_=i, **kw).then_inc(sem.h, 16))
        sem.inc(16)
        return sem.n

    b.em("vector", lambda v: v.memset(eps_t[:, 0:1], 1e-5))
    b.em("vector", lambda v: v.memset(eps_t[:, 1:2], 1e-6))
    b.em("vector", lambda v: v.memset(eps_t[:, 2:3], 1e-6).then_inc(s_dv.h, 1)); s_dv.inc()
    dma("gpsimd", s_tab, identf[:, :], identf_d)
    dma("gpsimd", s_tab, slope_cols[:, :], slopes_d.rearrange("h one -> one h").partition_broadcast(128))
    dma("gpsimd", s_tab, intraT[:, :], c_intra_d)
    dma("gpsimd", s_tab, qdec_t[:, :], c_qdec_d)
    dma("gpsimd", s_tab, kdec_t[:, :], c_kdec_d)
    dma("gpsimd", s_tab, bdc_t[:, :], c_bdc_d)
    dma("gpsimd", s_tab, comb_t[:, :], c_comb_d)

    b.wait("vector", s_tab)
    for tab, blkw in ((intraT, 128), (qdec_t, 128), (bdc_t, NC), (comb_t, 8), (kdec_t, 1)):
        b.wait("vector", s_ac)
        for h in range(H):
            last = (h == H - 1)
            b.em("vector", lambda v, t=tab, h=h, w=blkw, last=last:
                 (v.tensor_scalar(tmp_tab[:, h * w:(h + 1) * w], t[:, h * w:(h + 1) * w],
                                  slope_cols[:, h:h + 1], None, ALU.mult).then_inc(s_dv.h, 1)
                  if last else
                  v.tensor_scalar(tmp_tab[:, h * w:(h + 1) * w], t[:, h * w:(h + 1) * w],
                                  slope_cols[:, h:h + 1], None, ALU.mult)))
        s_dv.inc()
        b.wait("scalar", s_dv)
        b.em("scalar", lambda s, t=tab, w=blkw:
             s.activation(t[:, 0:H * w], tmp_tab[:, 0:H * w], AF.Exp, scale=-1.0).then_inc(s_ac.h, 1)); s_ac.inc()
    tabs_ready = s_ac.n

    dma("gpsimd", s_bq, bq_rows[0:3 * KC, :], bqkv_d.rearrange("(c p) -> c p", p=128))
    b.wait("tensor", s_bq)
    b.wait("tensor", s_tab)
    b.em("tensor", lambda t: t.transpose(bank(7, 3 * KC), bq_rows[0:3 * KC, :],
                                         identf[0:3 * KC, 0:3 * KC]).then_inc(s_pe.h, 1)); s_pe.inc()
    b.wait("vector", s_pe)
    b.em("vector", lambda v: v.tensor_copy(bqkv_cols[:, :], bank(7, 3 * KC)).then_inc(s_dv.h, 1)); s_dv.inc()
    bq_ready = s_dv.n

    nxl = T // rows_per
    for i in range(nxl):
        b.wait("gpsimd", sq_io)
        dma("gpsimd", s_xst,
            x_stage.ap().rearrange("p (n d) -> p n d", n=rows_per // 128),
            x_d[i * rows_per:(i + 1) * rows_per, :].rearrange("(n p) d -> p n d", p=128))
        b.wait("sync", s_xst)
        dma("sync", sq_io,
            x_scr[i * rows_per:(i + 1) * rows_per, :].rearrange("(n p) d -> p n d", p=128),
            x_stage.ap().rearrange("p (n d) -> p n d", n=rows_per // 128))
    b.wait("sync", sq_io)
    for fc in range(KC):
        dma("sync", sq_io, x_fm[:, fc * T:(fc + 1) * T], x_scr[:, fc * 128:(fc + 1) * 128], transpose=True)
    x_fm_ready = sq_io.n

    head_pe_free = {}
    head_dv_free = {}
    head_w_ready = {}

    def kvtile(sl, kc, which=None):
        if which is None:
            return wkv_sb[sl][:, kc * 256:(kc + 1) * 256]
        return wkv_sb[sl].ap().rearrange("p (kc two f) -> p kc two f", kc=KC, two=2)[:, kc, which, :]

    def load_head_weights(h):
        sl = h % 2
        sem = w_slots[sl]
        if h >= 2:
            b.wait("gpsimd", s_pe, head_pe_free[h - 2])
            b.wait("gpsimd", s_dv, head_dv_free[h - 2])
        dma("gpsimd", sem, wq_sb[sl].ap().rearrange("p (kc f) -> p kc f", kc=KC),
            wqkv_d[:, h * 128:(h + 1) * 128].rearrange("(kc p) f -> p kc f", p=128))
        dma("gpsimd", sem, wkv_sb[sl].ap().rearrange("p (kc two f) -> p kc two f", kc=KC, two=2)[:, :, 0, :],
            wqkv_d[:, D + h * 128:D + (h + 1) * 128].rearrange("(kc p) f -> p kc f", p=128))
        dma("gpsimd", sem, wkv_sb[sl].ap().rearrange("p (kc two f) -> p kc two f", kc=KC, two=2)[:, :, 1, :],
            wqkv_d[:, 2 * D + h * 128:2 * D + (h + 1) * 128].rearrange("(kc p) f -> p kc f", p=128))
        dma("gpsimd", sem, bias_k[sl][:, :],
            bqkv_d.rearrange("(one d) -> one d", one=1)[:, D + h * 128:D + (h + 1) * 128].partition_broadcast(128))
        dma("gpsimd", sem, bias_v[sl][:, :],
            bqkv_d.rearrange("(one d) -> one d", one=1)[:, 2 * D + h * 128:2 * D + (h + 1) * 128].partition_broadcast(128))
        return sem.n

    head_w_ready[0] = load_head_weights(0)
    if H > 1:
        head_w_ready[1] = load_head_weights(1)
    b.wait("tensor", sq_io, x_fm_ready)
    b.wait("tensor", s_ac, tabs_ready)

    si_ready_a_box = [None]

    def emit_combine1():
        b.wait("sync", s_cc, 1)
        for hh in range(H2):
            b.wait("sync", s_dv, s_dv.n)
            dma("sync", s_sg, sg_all.ap().rearrange("p (j f) -> p j f", j=cfg.n_cores),
                sl_out_a.rearrange("(j p) c -> p j c", p=128)[:, :, hh * 128:(hh + 1) * 128])
            b.wait("vector", s_sg)
            cur = None
            for j in range(cfg.n_cores):
                if j == 0:
                    b.em("vector", lambda v, hh=hh:
                         v.tensor_scalar(si_a[:, :], sg_all[:, 0:128], comb_t[:, hh * 8:hh * 8 + 1], None, ALU.mult).then_inc(s_dv.h, 1)); s_dv.inc()
                    cur = si_a
                else:
                    nxt = si_b if cur is si_a else si_a
                    b.wait("vector", s_dv, s_dv.n)
                    b.em("vector", lambda v, hh=hh, j=j, src=cur, dst=nxt:
                         v.scalar_tensor_tensor(dst[:, :], sg_all[:, j * 128:(j + 1) * 128],
                                                comb_t[:, hh * 8 + j:hh * 8 + j + 1], src[:, :],
                                                ALU.mult, ALU.add).then_inc(s_dv.h, 1)); s_dv.inc()
                    cur = nxt
            b.wait("vector", s_dv, s_dv.n)
            b.em("vector", lambda v, hh=hh, fin=cur:
                 v.tensor_copy(s_init[:, hh * 128:(hh + 1) * 128], fin[:, :]).then_inc(s_dv.h, 1)); s_dv.inc()
        si_ready_a_box[0] = ("dv", s_dv.n)

    store_guard = 0
    for h in range(H):
        sl = h % 2
        sem = w_slots[sl]
        wv = head_w_ready[h]
        b.wait("tensor", s_dv, s_dv.n)
        b.wait("tensor", s_ac, s_ac.n)
        b.wait("tensor", s_pl, s_pl.n)
        b.wait("tensor", sem, wv)
        b.wait("vector", sq_io, store_guard)
        b.wait("scalar", sq_io, store_guard)
        b.wait("gpsimd", sq_io, store_guard)
        b.wait("vector", sem, wv)
        b.wait("vector", s_ac, tabs_ready)
        b.em("vector", lambda v, h=h, sl=sl:
             v.tensor_scalar(bkd_tmp[sl][:, :], bias_k[sl][:, :], kdec_t[:, h:h + 1], None, ALU.mult).then_inc(s_dv.h, 1)); s_dv.inc()
        bkd_ready = s_dv.n

        for kc in range(KC):
            for s in range(NS):
                last = (kc == KC - 1 and s == NS - 1)
                b.em("tensor", lambda t, kc=kc, s=s, sl=sl, last=last:
                     (t.matmul(psum[:, s * SLAB:(s + 1) * SLAB], wq_sb[sl][:, kc * 128:(kc + 1) * 128],
                               x_fm[:, kc * T + s * SLAB:kc * T + (s + 1) * SLAB],
                               start=(kc == 0), stop=(kc == KC - 1)).then_inc(s_pe.h, 1) if last else
                      t.matmul(psum[:, s * SLAB:(s + 1) * SLAB], wq_sb[sl][:, kc * 128:(kc + 1) * 128],
                               x_fm[:, kc * T + s * SLAB:kc * T + (s + 1) * SLAB],
                               start=(kc == 0), stop=(kc == KC - 1))))
        s_pe.inc()
        q_done = s_pe.n
        for kc in range(KC):
            for s in range(NS):
                last = (kc == KC - 1 and s == NS - 1)
                b.em("tensor", lambda t, kc=kc, s=s, sl=sl, last=last:
                     (t.matmul(psum[:, 2048 + s * SLAB:2048 + (s + 1) * SLAB], kvtile(sl, kc, 0),
                               x_fm[:, kc * T + s * SLAB:kc * T + (s + 1) * SLAB],
                               start=(kc == 0), stop=(kc == KC - 1)).then_inc(s_pe.h, 1) if last else
                      t.matmul(psum[:, 2048 + s * SLAB:2048 + (s + 1) * SLAB], kvtile(sl, kc, 0),
                               x_fm[:, kc * T + s * SLAB:kc * T + (s + 1) * SLAB],
                               start=(kc == 0), stop=(kc == KC - 1))))
        s_pe.inc()
        k_done = s_pe.n
        b.wait("scalar", s_pe, q_done)
        b.em("scalar", lambda s, h=h:
             s.activation(q_fm[:, :], psum[:, 0:T], AF.Identity, bias=bqkv_cols[:, h:h + 1]).then_inc(s_ac.h, 1)); s_ac.inc()
        q_evac = s_ac.n
        b.wait("scalar", s_pe, k_done)
        b.em("scalar", lambda s, h=h:
             s.activation(k_fm[:, :], psum[:, 2048:2048 + T], AF.Identity, bias=bqkv_cols[:, KC + h:KC + h + 1]).then_inc(s_ac.h, 1)); s_ac.inc()
        k_evac = s_ac.n

        b.wait("tensor", s_ac, q_evac)
        kv_dv = []
        for tt in range(NTT):
            bk = tt % 2
            if tt >= 2:
                b.wait("tensor", s_dv, kv_dv[tt - 2])
            for kc in range(KC):
                last = (kc == KC - 1)
                b.em("tensor", lambda t, tt=tt, kc=kc, bk=bk, sl=sl, last=last:
                     (t.matmul(bank(bk, 256), x_fm[:, kc * T + tt * 128:kc * T + (tt + 1) * 128],
                               kvtile(sl, kc), start=(kc == 0), stop=(kc == KC - 1)).then_inc(s_pe.h, 1)
                      if last else
                      t.matmul(bank(bk, 256), x_fm[:, kc * T + tt * 128:kc * T + (tt + 1) * 128],
                               kvtile(sl, kc), start=(kc == 0), stop=(kc == KC - 1))))
            s_pe.inc()
            b.wait("vector", s_pe, s_pe.n)
            b.wait("vector", s_dv, bkd_ready)
            b.em("vector", lambda v, tt=tt, bk=bk, h=h, sl=sl:
                 v.scalar_tensor_tensor(kd_tm[:, tt * 128:(tt + 1) * 128], bank(bk, 256)[:, 0:128],
                                        kdec_t[:, h:h + 1], bkd_tmp[sl][:, :], ALU.mult, ALU.add))
            b.em("vector", lambda v, tt=tt, bk=bk, sl=sl:
                 v.tensor_tensor(v_tm[:, tt * 128:(tt + 1) * 128], bank(bk, 256)[:, 128:256],
                                 bias_v[sl][:, :], ALU.add).then_inc(s_dv.h, 1)); s_dv.inc()
            kv_dv.append(s_dv.n)
        head_pe_free[h] = s_pe.n
        head_dv_free[h] = s_dv.n
        if h + 2 < H:
            head_w_ready[h + 2] = load_head_weights(h + 2)

        b.wait("tensor", s_dv, kv_dv[-1])
        b.wait("tensor", s_ac, k_evac)
        o_evacs = []
        mask_done = {}
        supd_done = {}
        qdec_done = {}
        qg_done = {}
        o_done_d = {}
        sbf_done_d = {}
        sbf_done = None
        cur_s = None
        for c in range(NC):
            cs = slice(c * 128, (c + 1) * 128)
            sc_b = 2 + (c % 2)
            dl_b = 6 + (c % 2)
            o_b = 4 + ((c // 4) % 2)
            ocol = (c % 4) * 128
            if c >= 2:
                b.wait("tensor", s_dv, mask_done[c - 2])
            b.em("tensor", lambda t, cs=cs, sc_b=sc_b:
                 t.matmul(bank(sc_b, 128), k_fm[:, cs], q_fm[:, cs], start=True, stop=True).then_inc(s_pe.h, 1)); s_pe.inc()
            sc_done = s_pe.n
            if c >= 2:
                b.wait("tensor", s_dv, supd_done[c - 2])
            b.em("tensor", lambda t, cs=cs, dl_b=dl_b:
                 t.matmul(bank(dl_b, 128), kd_tm[:, cs], v_tm[:, cs], start=True, stop=True).then_inc(s_pe.h, 1)); s_pe.inc()
            dl_done = s_pe.n
            b.wait("vector", s_pe, sc_done)
            if c >= 1:
                b.wait("vector", s_pe, o_done_d[c - 1])
                b.wait("vector", s_dv, qg_done[c - 1])
            b.em("vector", lambda v, sc_b=sc_b, h=h:
                 v.tensor_tensor(scT_sb[:, :], bank(sc_b, 128), intraT[:, h * 128:(h + 1) * 128], ALU.mult).then_inc(s_dv.h, 1)); s_dv.inc()
            mask_done[c] = s_dv.n
            b.em("vector", lambda v, cs=cs, h=h:
                 v.tensor_tensor(qdec_sb[:, :], q_fm[:, cs], qdec_t[:, h * 128:(h + 1) * 128], ALU.mult).then_inc(s_dv.h, 1)); s_dv.inc()
            qdec_done[c] = s_dv.n
            b.wait("vector", s_dv, qdec_done[c])
            b.em("vector", lambda v, c=c, h=h:
                 v.tensor_scalar(qg_buf[:, c * 128:(c + 1) * 128], qdec_sb[:, :],
                                 bdc_t[:, h * NC + c:h * NC + c + 1], None, ALU.mult).then_inc(s_dv.h, 1)); s_dv.inc()
            qg_done[c] = s_dv.n
            b.wait("tensor", s_dv, mask_done[c])
            if c % 4 == 0 and o_evacs:
                b.wait("tensor", s_ac, o_evacs[-1])
            if c == 0:
                b.em("tensor", lambda t, cs=cs, o_b=o_b, ocol=ocol:
                     t.matmul(bank(o_b)[:, ocol:ocol + 128], v_tm[:, cs], scT_sb[:, :],
                              start=True, stop=True).then_inc(s_pe.h, 1)); s_pe.inc()
            else:
                b.em("tensor", lambda t, cs=cs, o_b=o_b, ocol=ocol:
                     t.matmul(bank(o_b)[:, ocol:ocol + 128], v_tm[:, cs], scT_sb[:, :],
                              start=True, stop=False))
                b.wait("tensor", s_dv, qdec_done[c])
                b.wait("tensor", s_ac, sbf_done)
                b.em("tensor", lambda t, o_b=o_b, ocol=ocol:
                     t.matmul(bank(o_b)[:, ocol:ocol + 128], s_bf[:, :], qdec_sb[:, :],
                              start=False, stop=True).then_inc(s_pe.h, 1)); s_pe.inc()
            o_done = s_pe.n
            o_done_d[c] = o_done
            b.wait("vector", s_pe, dl_done)
            if c == 0:
                b.em("vector", lambda v, dl_b=dl_b:
                     v.tensor_copy(s_a[:, :], bank(dl_b, 128)).then_inc(s_dv.h, 1)); s_dv.inc()
                cur_s = s_a
            else:
                new_s = s_b if cur_s is s_a else s_a
                b.wait("vector", s_dv, supd_done[c - 1])
                if c >= 2:
                    b.wait("vector", s_ac, sbf_done_d[c - 2])
                b.em("vector", lambda v, dl_b=dl_b, so=cur_s, sn=new_s, h=h:
                     v.scalar_tensor_tensor(sn[:, :], so[:, :], bdc_t[:, h * NC + 1:h * NC + 2],
                                            bank(dl_b, 128), ALU.mult, ALU.add).then_inc(s_dv.h, 1)); s_dv.inc()
                cur_s = new_s
            supd_done[c] = s_dv.n
            b.wait("scalar", s_dv, supd_done[c])
            b.wait("scalar", s_pe, o_done)
            b.em("scalar", lambda s, sn=cur_s: s.activation(s_bf[:, :], sn[:, :], AF.Copy).then_inc(s_ac.h, 1)); s_ac.inc()
            sbf_done = s_ac.n
            sbf_done_d[c] = sbf_done
            if c % 4 == 3 or c == NC - 1:
                b.wait("scalar", s_pe, o_done)
                lo = (c // 4) * 4
                b.em("scalar", lambda s, lo=lo, c=c, o_b=o_b:
                     s.activation(o_buf[:, lo * 128:(c + 1) * 128], bank(o_b, (c + 1 - lo) * 128),
                                  AF.Copy).then_inc(s_ac.h, 1)); s_ac.inc()
                o_evacs.append(s_ac.n)
        b.wait("sync", s_ac, o_evacs[-1])
        dma("sync", sq_io, o_scr[h * 128:(h + 1) * 128, :], o_buf[:, :])
        b.wait("sync", s_dv, s_dv.n)
        dma("sync", sq_io, q_scr[h * 128:(h + 1) * 128, :], qg_buf[:, :])
        b.wait("sync", s_dv, supd_done[NC - 1])
        if h < H2:
            dma("sync", sq_io, sl_in_a[:, h * 128:(h + 1) * 128], cur_s[:, :])
        else:
            dma("sync", sq_io, sl_in_b[:, (h - H2) * 128:(h - H2 + 1) * 128], cur_s[:, :])
        store_guard = sq_io.n
        if h == H2 - 1:
            half1_stores = sq_io.n
        if h == H2 + 1:
            b.wait("gpsimd", sq_io, store_guard)
            b.em("gpsimd", lambda g: g.collective_compute(
                "AllGather", ALU.bypass, replica_groups=[list(range(cfg.n_cores))],
                ins=[sl_in_a.opt()], outs=[sl_out_a.opt()]).then_inc(s_cc.h, 1)); s_cc.inc()
        if h == max(H2 + 2, H - 2) and h <= H - 1:
            emit_combine1()

    if max(H2 + 2, H - 2) > H - 1:
        emit_combine1()

    b.wait("gpsimd", sq_io, store_guard)
    b.em("gpsimd", lambda g: g.collective_compute(
        "AllGather", ALU.bypass, replica_groups=[list(range(cfg.n_cores))],
        ins=[sl_in_b.opt()], outs=[sl_out_b.opt()]).then_inc(s_cc.h, 1)); s_cc.inc()
    b.barrier(s_bar, data_sems=(s_pe, s_dv, s_ac, s_pl, sq_io, sa_io))

    cvB = Arena(nc, PH, SB_TOP)
    wout_sb = cvB.c(KC * D, BF16)
    q_all = [cvB.c(H * 128, BF16) for _ in range(2)]
    o_all = [cvB.c(H * 128, BF16) for _ in range(2)]
    o_full = [cvB.c(H * 128, BF16) for _ in range(2)]
    h_sb = [cvB.c(D, BF16) for _ in range(2)]
    x_tile = [cvB.c(D, F32) for _ in range(2)]
    z_sb = [cvB.c(D, F32) for _ in range(2)]
    zt_sb = [cvB.c(D, F32) for _ in range(2)]
    y_sb = [cvB.c(D, F32) for _ in range(2)]
    bout_rep = cvB.c(D, F32)
    rms_rep = cvB.c(D, F32)

    dma("gpsimd", s_wout, bout_rep[:, :], bout_d.rearrange("(one d) -> one d", one=1).partition_broadcast(128))
    dma("gpsimd", s_wout, rms_rep[:, :], rmsw_d.rearrange("(one d) -> one d", one=1).partition_broadcast(128))
    for oc in range(KC):
        dma("gpsimd", s_wout, wout_sb[:, oc * D:(oc + 1) * D], wout_d[oc * 128:(oc + 1) * 128, :])
    wout_ready = s_wout.n

    b.wait("gpsimd", s_cc)
    b.wait("vector", s_cc)
    for h in range(H2, H):
        b.wait("gpsimd", s_dv, s_dv.n)
        dma("gpsimd", s_bq, sg_all.ap().rearrange("p (j f) -> p j f", j=cfg.n_cores),
            sl_out_b.rearrange("(j p) c -> p j c", p=128)[:, :, (h - H2) * 128:(h - H2 + 1) * 128])
        b.wait("vector", s_bq)
        cur = None
        for j in range(cfg.n_cores):
            if j == 0:
                b.em("vector", lambda v, h=h:
                     v.tensor_scalar(si_a[:, :], sg_all[:, 0:128], comb_t[:, h * 8:h * 8 + 1], None, ALU.mult).then_inc(s_dv.h, 1)); s_dv.inc()
                cur = si_a
            else:
                nxt = si_b if cur is si_a else si_a
                b.wait("vector", s_dv, s_dv.n)
                b.em("vector", lambda v, h=h, j=j, src=cur, dst=nxt:
                     v.scalar_tensor_tensor(dst[:, :], sg_all[:, j * 128:(j + 1) * 128],
                                            comb_t[:, h * 8 + j:h * 8 + j + 1], src[:, :],
                                            ALU.mult, ALU.add).then_inc(s_dv.h, 1)); s_dv.inc()
                cur = nxt
        b.wait("vector", s_dv, s_dv.n)
        b.em("vector", lambda v, h=h, fin=cur:
             v.tensor_copy(s_init[:, h * 128:(h + 1) * 128], fin[:, :]).then_inc(s_dv.h, 1)); s_dv.inc()
    si_ready_b = s_dv.n

    tile_dv_free = {}
    tile_ready = {}

    def load_p3_tile(tt):
        sl = tt % 2
        sem = t_slots[sl]
        if tt >= 2:
            b.wait("scalar", s_dv, tile_dv_free[tt - 2])
        dma("scalar", sem, q_all[sl].ap().rearrange("p (h f) -> p h f", h=H),
            q_scr.rearrange("(h p) t -> p h t", p=128)[:, :, tt * 128:(tt + 1) * 128])
        dma("scalar", sem, o_all[sl].ap().rearrange("p (h f) -> p h f", h=H),
            o_scr.rearrange("(h p) t -> p h t", p=128)[:, :, tt * 128:(tt + 1) * 128])
        dma("scalar", sem, x_tile[sl][:, :], x_d[tt * 128:(tt + 1) * 128, :])
        return sem.n

    tile_ready[0] = load_p3_tile(0)
    if NTT > 1:
        tile_ready[1] = load_p3_tile(1)
    b.wait("tensor", s_dv, si_ready_a_box[0][1])
    b.wait("tensor", s_wout, wout_ready)
    b.wait("vector", s_wout, wout_ready)
    NG4 = (H + 3) // 4
    nso = max(1, D // 512)
    so_w = min(512, D)
    corr_hist = []
    op_hist = {}
    zt_hist = {}
    y_store = {}
    chain_done = {}

    def emit_chain(t2):
        sl = t2 % 2
        sc = 0 if sl == 0 else 8
        b.wait("vector", s_dv, zt_hist[t2])
        if t2 >= 2:
            b.wait("vector", sq_io, y_store[t2 - 1])
        b.em("vector", lambda v, sl=sl, sc=sc:
             v.scalar_tensor_tensor(z_sb[sl][:, :], zt_sb[sl][:, :], 1.0, bout_rep[:, :], ALU.mult, ALU.add,
                                    accum_out=stats[:, sc + 0:sc + 1]).then_inc(s_dv.h, 1)); s_dv.inc()
        z_done = s_dv.n
        b.wait("scalar", s_dv, z_done)
        b.em("scalar", lambda s, sl=sl, sc=sc:
             s.activation(zt_sb[sl][:, :], z_sb[sl][:, :], AF.Square,
                          accum_out=stats[:, sc + 1:sc + 2]).then_inc(s_ac.h, 1)); s_ac.inc()
        zsq_done = s_ac.n
        b.wait("vector", s_ac, zsq_done)
        b.em("vector", lambda v, sc=sc: v.tensor_scalar(stats[:, sc + 2:sc + 3], stats[:, sc + 0:sc + 1], 1.0 / D, None, ALU.mult))
        b.em("vector", lambda v, sc=sc: v.tensor_scalar(stats[:, sc + 3:sc + 4], stats[:, sc + 1:sc + 2], 1.0 / D, 1e-5, ALU.mult, ALU.add))
        b.em("vector", lambda v, sc=sc: v.scalar_tensor_tensor(stats[:, sc + 4:sc + 5], stats[:, sc + 0:sc + 1], 1.0 / (D * D), stats[:, sc + 0:sc + 1], ALU.mult, ALU.mult).then_inc(s_dv.h, 1)); s_dv.inc()
        b.wait("vector", s_dv, s_dv.n)
        b.em("vector", lambda v, sc=sc: v.tensor_tensor(stats[:, sc + 5:sc + 6], stats[:, sc + 3:sc + 4], stats[:, sc + 4:sc + 5], ALU.subtract).then_inc(s_dv.h, 1)); s_dv.inc()
        b.wait("scalar", s_dv, s_dv.n)
        b.em("scalar", lambda s, sc=sc: s.activation(stats[:, sc + 6:sc + 7], stats[:, sc + 5:sc + 6], AF.Ln).then_inc(s_ac.h, 1)); s_ac.inc()
        b.wait("scalar", s_ac, s_ac.n)
        b.em("scalar", lambda s, sc=sc: s.activation(stats[:, sc + 7:sc + 8], stats[:, sc + 6:sc + 7], AF.Exp, scale=-0.5).then_inc(s_ac.h, 1)); s_ac.inc()
        rstd_done = s_ac.n
        b.wait("vector", s_ac, rstd_done)
        b.em("vector", lambda v, sl=sl, sc=sc:
             v.tensor_scalar(y_sb[sl][:, :], z_sb[sl][:, :], stats[:, sc + 2:sc + 3], stats[:, sc + 7:sc + 8],
                             ALU.subtract, ALU.mult).then_inc(s_dv.h, 1)); s_dv.inc()
        y_done = s_dv.n
        b.wait("scalar", s_dv, y_done)
        b.em("scalar", lambda s, sl=sl, sc=sc:
             s.activation(zt_sb[sl][:, :], y_sb[sl][:, :], AF.Square,
                          accum_out=stats[:, sc + 1:sc + 2]).then_inc(s_ac.h, 1)); s_ac.inc()
        b.wait("scalar", s_ac, s_ac.n)
        b.em("scalar", lambda s, sc=sc: s.activation(stats[:, sc + 4:sc + 5], stats[:, sc + 1:sc + 2], AF.Ln, bias=eps_t[:, 2:3], scale=1.0 / D).then_inc(s_ac.h, 1)); s_ac.inc()
        b.wait("scalar", s_ac, s_ac.n)
        b.em("scalar", lambda s, sc=sc: s.activation(stats[:, sc + 5:sc + 6], stats[:, sc + 4:sc + 5], AF.Exp, scale=-0.5).then_inc(s_ac.h, 1)); s_ac.inc()
        rr_done = s_ac.n
        b.wait("vector", s_ac, rr_done)
        b.em("vector", lambda v, sl=sl, sc=sc:
             v.scalar_tensor_tensor(h_sb[sl][:, :], y_sb[sl][:, :], stats[:, sc + 5:sc + 6], rms_rep[:, :],
                                    ALU.mult, ALU.mult).then_inc(s_dv.h, 1)); s_dv.inc()
        h_done = s_dv.n
        chain_done[t2] = h_done
        b.wait("sync", s_dv, y_done)
        dma("sync", sq_io, y_scr[t2 * 128:(t2 + 1) * 128, :], y_sb[sl][:, :])
        b.wait("sync", s_dv, h_done)
        dma("sync", sq_io, h_scr[t2 * 128:(t2 + 1) * 128, :], h_sb[sl][:, :])
        y_store[t2] = sq_io.n

    for tt in range(NTT):
        sl = tt % 2
        sem = t_slots[sl]
        b.wait("tensor", sem, tile_ready[tt])
        NGC = (H + 7) // 8
        for gc in range(NGC):
            cb0 = 2048 + (gc % 2) * 1024
            h0, h1 = 8 * gc, min(8 * gc + 8, H)
            gidx = tt * NGC + gc
            if gidx >= 2:
                b.wait("tensor", s_dv, corr_hist[gidx - 2])
            if h1 > H2:
                b.wait("tensor", s_dv, si_ready_b)
            for hh in range(h0, h1):
                last = (hh == h1 - 1)
                b.em("tensor", lambda t, hh=hh, sl=sl, last=last, cb0=cb0:
                     (t.matmul(psum[:, cb0 + (hh % 8) * 128:cb0 + (hh % 8) * 128 + 128],
                               s_init[:, hh * 128:(hh + 1) * 128],
                               q_all[sl][:, hh * 128:(hh + 1) * 128], start=True, stop=True).then_inc(s_pe.h, 1)
                      if last else
                      t.matmul(psum[:, cb0 + (hh % 8) * 128:cb0 + (hh % 8) * 128 + 128],
                               s_init[:, hh * 128:(hh + 1) * 128],
                               q_all[sl][:, hh * 128:(hh + 1) * 128], start=True, stop=True)))
            s_pe.inc()
            b.wait("vector", s_pe, s_pe.n)
            b.wait("vector", sem, tile_ready[tt])
            if tt >= 2:
                b.wait("vector", s_pe, op_hist[tt - 2])
            b.em("vector", lambda v, sl=sl, cb0=cb0, h0=h0, h1=h1:
                 v.tensor_tensor(o_full[sl][:, h0 * 128:h1 * 128], psum[:, cb0:cb0 + (h1 - h0) * 128],
                                 o_all[sl][:, h0 * 128:h1 * 128], ALU.add).then_inc(s_dv.h, 1)); s_dv.inc()
            corr_hist.append(s_dv.n)
        b.wait("tensor", s_dv, corr_hist[-1])
        if tt >= 2:
            b.wait("tensor", s_dv, zt_hist[tt - 2])
        for oc in range(KC):
            for s in range(nso):
                last = (oc == KC - 1 and s == nso - 1)
                b.em("tensor", lambda t, oc=oc, s=s, last=last, sl=sl:
                     (t.matmul(psum[:, s * so_w:(s + 1) * so_w], o_full[sl][:, oc * 128:(oc + 1) * 128],
                               wout_sb[:, oc * D + s * so_w:oc * D + (s + 1) * so_w],
                               start=(oc == 0), stop=(oc == KC - 1)).then_inc(s_pe.h, 1) if last else
                      t.matmul(psum[:, s * so_w:(s + 1) * so_w], o_full[sl][:, oc * 128:(oc + 1) * 128],
                               wout_sb[:, oc * D + s * so_w:oc * D + (s + 1) * so_w],
                               start=(oc == 0), stop=(oc == KC - 1))))
        s_pe.inc()
        op_done = s_pe.n
        op_hist[tt] = op_done
        if tt >= 1:
            emit_chain(tt - 1)
        b.wait("vector", s_pe, op_done)
        if tt >= 2:
            b.wait("vector", s_dv, chain_done[tt - 2])
        b.em("vector", lambda v, sl=sl:
             v.tensor_tensor(zt_sb[sl][:, :], psum[:, 0:D], x_tile[sl][:, :], ALU.add).then_inc(s_dv.h, 1)); s_dv.inc()
        tile_dv_free[tt] = s_dv.n
        zt_hist[tt] = s_dv.n
        if tt + 2 < NTT:
            tile_ready[tt + 2] = load_p3_tile(tt + 2)
    emit_chain(NTT - 1)
    b.barrier(s_bar, data_sems=(s_pe, s_dv, s_ac, s_pl, sq_io, sa_io))
    b.wait("sync", sq_io)

    cvC = Arena(nc, PH, SB_TOP)
    h_fm = cvC.c(KC * Tt, BF16)
    act_e = cvC.c(NHMAX * Tt, BF16)
    wgu_sb = [cvC.c(2 * KC * 128, BF16) for _ in range(2)]
    wdn_sb = [cvC.c(NHMAX * OH, BF16) for _ in range(2)]
    rw_sb = cvC.c(KC * E, BF16)
    moe = cvC.c(NTG * D, F32)
    wbc_sb = [cvC.c(Tt, F32) for _ in range(2)]
    p_sb = cvC.c(NTG * E, F32)
    w_sb = cvC.c(NTG * E, F32)
    msk_sb = cvC.c(NTG * E, F32)
    sg_sb = cvC.c(Tt, F32)
    yt_sb = cvC.c(D, F32)
    z2_sb = cvC.c(D, F32)
    stats2 = cvC.c(4 * NTG, F32)
    sel_sb = cvC.c(E * 128, F32)

    dma("gpsimd", s_rw, rw_sb.ap().rearrange("p (kc e) -> p kc e", kc=KC),
        rw_d.rearrange("(kc p) e -> p kc e", p=128))
    dma("gpsimd", s_rw, sel_sb[0:E, :], sel_d)
    rw_ready = s_rw.n

    experts = [(e, NHF, True) for e in range(E)] + [(None, NHS, False)]
    gu_jobs = [(ei, hf) for ei, (e, nhf, wt) in enumerate(experts) for hf in range(nhf)]
    dn_jobs = [(ei, oh) for ei, (e, nhf, wt) in enumerate(experts) for oh in range(NOH)]

    nts = max(1, Tt // 512)
    ts_w = min(512, Tt)
    nos = max(1, OH // 512)
    os_w = min(512, OH)

    gu_ready = {}
    gu_pe_free = {}
    dn_ready = {}
    dn_pe_free = {}
    gj = 0
    dj = 0
    hfm_last_use = {}
    mt_guard = {}
    f2_tg_final = {}
    f2s = {"z2": 0, "st": 0}

    def emit_f2(mt2, tg):
        tt = (mt2 * Tt) // 128 + tg
        b.wait("scalar", s_dv, f2s["z2"])
        b.wait("scalar", s_out, f2s["st"])
        dma("scalar", sa_io, yt_sb[:, :], y_scr[tt * 128:(tt + 1) * 128, :])
        b.wait("vector", sa_io)
        b.wait("vector", s_dv, f2_tg_final[mt2][tg])
        b.em("vector", lambda v, tg=tg:
             v.scalar_tensor_tensor(z2_sb[:, :], moe[:, tg * D:(tg + 1) * D], 1.0, yt_sb[:, :],
                                    ALU.mult, ALU.add, accum_out=stats[:, 0:1]).then_inc(s_dv.h, 1)); s_dv.inc()
        f2s["z2"] = s_dv.n
        z2_done = s_dv.n
        b.wait("scalar", s_dv, z2_done)
        b.em("scalar", lambda s:
             s.activation(yt_sb[:, :], z2_sb[:, :], AF.Square,
                          accum_out=stats[:, 1:2]).then_inc(s_ac.h, 1)); s_ac.inc()
        b.wait("vector", s_ac, s_ac.n)
        b.em("vector", lambda v: v.tensor_scalar(stats[:, 2:3], stats[:, 0:1], 1.0 / D, None, ALU.mult))
        b.em("vector", lambda v: v.tensor_scalar(stats[:, 3:4], stats[:, 1:2], 1.0 / D, 1e-5, ALU.mult, ALU.add))
        b.em("vector", lambda v: v.scalar_tensor_tensor(stats[:, 4:5], stats[:, 0:1], 1.0 / (D * D), stats[:, 0:1], ALU.mult, ALU.mult).then_inc(s_dv.h, 1)); s_dv.inc()
        b.wait("vector", s_dv, s_dv.n)
        b.em("vector", lambda v: v.tensor_tensor(stats[:, 5:6], stats[:, 3:4], stats[:, 4:5], ALU.subtract).then_inc(s_dv.h, 1)); s_dv.inc()
        b.wait("scalar", s_dv, s_dv.n)
        b.em("scalar", lambda s: s.activation(stats[:, 6:7], stats[:, 5:6], AF.Ln).then_inc(s_ac.h, 1)); s_ac.inc()
        b.wait("scalar", s_ac, s_ac.n)
        b.em("scalar", lambda s: s.activation(stats[:, 7:8], stats[:, 6:7], AF.Exp, scale=-0.5).then_inc(s_ac.h, 1)); s_ac.inc()
        b.wait("vector", s_ac, s_ac.n)
        b.em("vector", lambda v:
             v.tensor_scalar(yt_sb[:, :], z2_sb[:, :], stats[:, 2:3], stats[:, 7:8],
                             ALU.subtract, ALU.mult).then_inc(s_dv.h, 1)); s_dv.inc()
        b.wait("sync", s_dv, s_dv.n)
        dma("sync", s_out, out_d[tt * 128:(tt + 1) * 128, :], yt_sb[:, :])
        f2s["st"] = s_out.n

    hfm_ready_d = {}
    wfmT_ready_d = {}
    wbc_ready_d = {}
    prep_done = {}

    def prepare_dma(mt):
        t0 = mt * Tt
        if mt > 0:
            b.wait("sync", s_pe, hfm_last_use[mt - 1])
        for fc in range(KC):
            dma("sync", sq_io, h_fm[:, fc * Tt:(fc + 1) * Tt],
                h_scr[t0:t0 + Tt, fc * 128:(fc + 1) * 128], transpose=True)
        hfm_ready_d[mt] = (sq_io.n, sa_io.n)

    def prepare_pe(mt, rb, wb):
        b.wait("tensor", s_rw, rw_ready)
        b.wait("tensor", sq_io, hfm_ready_d[mt][0])
        b.wait("tensor", sa_io, hfm_ready_d[mt][1])
        b.wait("tensor", s_ac, s_ac.n)
        b.wait("tensor", s_dv, s_dv.n)
        for tg in range(NTG):
            for kc in range(KC):
                last = (tg == NTG - 1 and kc == KC - 1)
                b.em("tensor", lambda t, tg=tg, kc=kc, last=last, rb=rb:
                     (t.matmul(bank(rb)[:, tg * E:(tg + 1) * E],
                               h_fm[:, kc * Tt + tg * 128:kc * Tt + (tg + 1) * 128],
                               rw_sb[:, kc * E:(kc + 1) * E], start=(kc == 0), stop=(kc == KC - 1)).then_inc(s_pe.h, 1)
                      if last else
                      t.matmul(bank(rb)[:, tg * E:(tg + 1) * E],
                               h_fm[:, kc * Tt + tg * 128:kc * Tt + (tg + 1) * 128],
                               rw_sb[:, kc * E:(kc + 1) * E], start=(kc == 0), stop=(kc == KC - 1))))
        s_pe.inc()
        r_done = s_pe.n
        b.wait("vector", s_pe, r_done)
        for tg in range(NTG):
            last = (tg == NTG - 1)
            b.em("vector", lambda v, tg=tg, rb=rb, last=last:
                 (v.reduce_max(stats2[:, tg:tg + 1], bank(rb)[:, tg * E:(tg + 1) * E], axis=AX.X).then_inc(s_dv.h, 1)
                  if last else
                  v.reduce_max(stats2[:, tg:tg + 1], bank(rb)[:, tg * E:(tg + 1) * E], axis=AX.X)))
        s_dv.inc()
        b.wait("vector", s_dv, s_dv.n)
        b.em("vector", lambda v: v.tensor_scalar(stats2[:, NTG:2 * NTG], stats2[:, 0:NTG], -1.0, None, ALU.mult).then_inc(s_dv.h, 1)); s_dv.inc()
        b.wait("scalar", s_dv, s_dv.n)
        for tg in range(NTG):
            last = (tg == NTG - 1)
            b.em("scalar", lambda s, tg=tg, last=last, rb=rb:
                 (s.activation(p_sb[:, tg * E:(tg + 1) * E], bank(rb)[:, tg * E:(tg + 1) * E], AF.Exp,
                               bias=stats2[:, NTG + tg:NTG + tg + 1]).then_inc(s_ac.h, 1) if last else
                  s.activation(p_sb[:, tg * E:(tg + 1) * E], bank(rb)[:, tg * E:(tg + 1) * E], AF.Exp,
                               bias=stats2[:, NTG + tg:NTG + tg + 1])))
        s_ac.inc()
        b.wait("vector", s_ac, s_ac.n)
        for tg in range(NTG):
            last = (tg == NTG - 1)
            b.em("vector", lambda v, tg=tg, last=last:
                 (v.reduce_max(stats2[:, tg:tg + 1], p_sb[:, tg * E:(tg + 1) * E], axis=AX.X).then_inc(s_dv.h, 1)
                  if last else
                  v.reduce_max(stats2[:, tg:tg + 1], p_sb[:, tg * E:(tg + 1) * E], axis=AX.X)))
        s_dv.inc()
        b.wait("vector", s_dv, s_dv.n)
        for tg in range(NTG):
            last = (tg == NTG - 1)
            b.em("vector", lambda v, tg=tg, last=last:
                 (v.scalar_tensor_tensor(msk_sb[:, tg * E:(tg + 1) * E], p_sb[:, tg * E:(tg + 1) * E], stats2[:, tg:tg + 1],
                                         p_sb[:, tg * E:(tg + 1) * E], ALU.is_lt, ALU.mult).then_inc(s_dv.h, 1)
                  if last else
                  v.scalar_tensor_tensor(msk_sb[:, tg * E:(tg + 1) * E], p_sb[:, tg * E:(tg + 1) * E], stats2[:, tg:tg + 1],
                                         p_sb[:, tg * E:(tg + 1) * E], ALU.is_lt, ALU.mult)))
        s_dv.inc()
        b.wait("vector", s_dv, s_dv.n)
        for tg in range(NTG):
            last = (tg == NTG - 1)
            b.em("vector", lambda v, tg=tg, last=last:
                 (v.reduce_max(stats2[:, NTG + tg:NTG + tg + 1], msk_sb[:, tg * E:(tg + 1) * E], axis=AX.X).then_inc(s_dv.h, 1)
                  if last else
                  v.reduce_max(stats2[:, NTG + tg:NTG + tg + 1], msk_sb[:, tg * E:(tg + 1) * E], axis=AX.X)))
        s_dv.inc()
        b.wait("vector", s_dv, s_dv.n)
        b.em("vector", lambda v: v.tensor_tensor(stats2[:, 2 * NTG:3 * NTG], stats2[:, 0:NTG], stats2[:, NTG:2 * NTG], ALU.add).then_inc(s_dv.h, 1)); s_dv.inc()
        b.wait("vector", s_dv, s_dv.n)
        b.em("vector", lambda v: v.reciprocal(stats2[:, 0:NTG], stats2[:, 2 * NTG:3 * NTG]).then_inc(s_dv.h, 1)); s_dv.inc()
        b.wait("vector", s_dv, s_dv.n)
        for tg in range(NTG):
            last = (tg == NTG - 1)
            b.em("vector", lambda v, tg=tg, last=last:
                 (v.tensor_scalar(msk_sb[:, tg * E:(tg + 1) * E], p_sb[:, tg * E:(tg + 1) * E], stats2[:, NTG + tg:NTG + tg + 1], None,
                                  ALU.is_ge).then_inc(s_dv.h, 1)
                  if last else
                  v.tensor_scalar(msk_sb[:, tg * E:(tg + 1) * E], p_sb[:, tg * E:(tg + 1) * E], stats2[:, NTG + tg:NTG + tg + 1], None,
                                  ALU.is_ge)))
        s_dv.inc()
        b.wait("vector", s_dv, s_dv.n)
        for tg in range(NTG):
            last = (tg == NTG - 1)
            b.em("vector", lambda v, tg=tg, last=last:
                 (v.scalar_tensor_tensor(w_sb[:, tg * E:(tg + 1) * E], p_sb[:, tg * E:(tg + 1) * E], stats2[:, tg:tg + 1],
                                         msk_sb[:, tg * E:(tg + 1) * E], ALU.mult, ALU.mult).then_inc(s_dv.h, 1) if last else
                  v.scalar_tensor_tensor(w_sb[:, tg * E:(tg + 1) * E], p_sb[:, tg * E:(tg + 1) * E], stats2[:, tg:tg + 1],
                                         msk_sb[:, tg * E:(tg + 1) * E], ALU.mult, ALU.mult)))
        s_dv.inc()
        w_ready = s_dv.n
        b.wait("tensor", s_dv, w_ready)
        tg_evac = []
        for tg in range(NTG):
            if tg % 4 == 0 and tg_evac:
                b.wait("tensor", s_dv, tg_evac[-1])
            b.em("tensor", lambda t, tg=tg, wb=wb:
                 t.transpose(bank(wb)[0:E, (tg % 4) * 128:(tg % 4) * 128 + 128],
                             w_sb[:, tg * E:(tg + 1) * E], identf[:, :]).then_inc(s_pe.h, 1)); s_pe.inc()
            if tg % 4 == 3 or tg == NTG - 1:
                b.wait("vector", s_pe, s_pe.n)
                lo = (tg // 4) * 4
                b.em("vector", lambda v, lo=lo, tg=tg, wb=wb:
                     v.tensor_copy(wfmT[0:E, lo * 128:(tg + 1) * 128],
                                   bank(wb)[0:E, 0:(tg + 1 - lo) * 128]).then_inc(s_dv.h, 1)); s_dv.inc()
                tg_evac.append(s_dv.n)
        wfmT_ready = s_dv.n
        wfmT_ready_d[mt] = wfmT_ready
        nws = max(1, Tt // 512)
        ws_w = min(512, Tt)
        b.wait("tensor", s_dv, wfmT_ready)
        for s in range(nws):
            last = (s == nws - 1)
            b.em("tensor", lambda t, s=s, last=last, rb=rb:
                 (t.matmul(psum[:, rb * 512 + s * ws_w:rb * 512 + (s + 1) * ws_w], sel_sb[0:E, 0:128],
                           wfmT[0:E, s * ws_w:(s + 1) * ws_w], start=True, stop=True).then_inc(s_pe.h, 1) if last else
                  t.matmul(psum[:, rb * 512 + s * ws_w:rb * 512 + (s + 1) * ws_w], sel_sb[0:E, 0:128],
                           wfmT[0:E, s * ws_w:(s + 1) * ws_w], start=True, stop=True)))
        s_pe.inc()
        b.wait("vector", s_pe, s_pe.n)
        b.em("vector", lambda v, rb=rb: v.tensor_copy(wbc_sb[0][:, :], psum[:, rb * 512:rb * 512 + Tt]).then_inc(s_dv.h, 1)); s_dv.inc()
        wbc_ready_d[mt] = s_dv.n

    for mt in range(NMT):
        t0 = mt * Tt
        if mt == 0:
            prepare_dma(0)
            prepare_pe(0, 4, 5)
        if mt > 0:
            b.wait("tensor", s_dv, mt_guard[mt - 1][0])
            b.wait("tensor", s_ac, mt_guard[mt - 1][1])
            b.wait("tensor", s_pl, mt_guard[mt - 1][2])
        else:
            b.wait("tensor", s_dv, s_dv.n)
            b.wait("tensor", s_ac, s_ac.n)
            b.wait("tensor", s_pl, s_pl.n)
        wfmT_ready = wfmT_ready_d[mt]

        def load_gu(j):
            ei, hf = gu_jobs[j % len(gu_jobs)]
            e, nhf, weighted = experts[ei]
            sl = j % 2
            sem = fg_slots[sl]
            if j >= 2:
                b.wait("gpsimd", s_pe, gu_pe_free[j - 2])
            for which in range(2):
                if e is None:
                    src = (swg_d if which == 0 else swu_d)[:, hf * 128:(hf + 1) * 128]
                else:
                    src = (wg_d if which == 0 else wu_d)[e][:, hf * 128:(hf + 1) * 128]
                dma("gpsimd", sem,
                    wgu_sb[sl].ap().rearrange("p (two kc f) -> p two kc f", two=2, kc=KC)[:, which, :, :],
                    src.rearrange("(kc p) f -> p kc f", p=128))
            gu_ready[j] = sem.n

        def load_dn(j):
            ei, oh = dn_jobs[j % len(dn_jobs)]
            e, nhf, weighted = experts[ei]
            sl = j % 2
            sem = fd_slots[sl]
            if j >= 2:
                b.wait("gpsimd", s_pe, dn_pe_free[j - 2])
            for hf in range(nhf):
                if e is None:
                    src = swd_d[hf * 128:(hf + 1) * 128, oh * OH:(oh + 1) * OH]
                else:
                    src = wd_d[e][hf * 128:(hf + 1) * 128, oh * OH:(oh + 1) * OH]
                dma("gpsimd", sem, wdn_sb[sl][:, hf * OH:(hf + 1) * OH], src)
            dn_ready[j] = sem.n

        if mt == 0:
            load_gu(0)
            if len(gu_jobs) * NMT > 1:
                load_gu(1)
            load_dn(0)
            if len(dn_jobs) * NMT > 1:
                load_dn(1)
        moe_adds = []
        prev_sg_last = None
        for ei, (e, nhf, weighted) in enumerate(experts):
            if ei == 0 and mt == 0:
                b.wait("tensor", s_ac, s_ac.n)
                b.wait("tensor", s_dv, s_dv.n)
                b.wait("tensor", s_pl, s_pl.n)
            if moe_adds:
                b.wait("tensor", s_dv, moe_adds[-1])
            if prev_sg_last is not None:
                b.wait("tensor", s_ac, prev_sg_last)
            if ei == 0:
                tg_final = {}
                wbc_ready_cur = wbc_ready_d[mt]
            wbc_ready = None
            wsl = ei % 2
            if weighted:
                wbc_ready = wbc_ready_cur
                b.wait("tensor", s_dv, wbc_ready)
            act_done = []
            sg_hist = []
            tu_hist = []
            for hf in range(nhf):
                if mt > 0 and ei == 0 and hf < NTG:
                    emit_f2(mt - 1, hf)
                par = hf % 2
                goff = 2048 * par
                uoff = 1024 + 2048 * par
                gsl = gj % 2
                b.wait("tensor", fg_slots[gsl], gu_ready[gj])
                if hf >= 2:
                    b.wait("tensor", s_ac, sg_hist[hf - 2])
                for kc in range(KC):
                    for s in range(nts):
                        last = (kc == KC - 1 and s == nts - 1)
                        b.em("tensor", lambda t, kc=kc, s=s, gsl=gsl, last=last, goff=goff:
                             (t.matmul(psum[:, goff + s * ts_w:goff + (s + 1) * ts_w],
                                       wgu_sb[gsl][:, kc * 128:(kc + 1) * 128],
                                       h_fm[:, kc * Tt + s * ts_w:kc * Tt + (s + 1) * ts_w],
                                       start=(kc == 0), stop=(kc == KC - 1)).then_inc(s_pe.h, 1) if last else
                              t.matmul(psum[:, goff + s * ts_w:goff + (s + 1) * ts_w],
                                       wgu_sb[gsl][:, kc * 128:(kc + 1) * 128],
                                       h_fm[:, kc * Tt + s * ts_w:kc * Tt + (s + 1) * ts_w],
                                       start=(kc == 0), stop=(kc == KC - 1))))
                s_pe.inc()
                g_done = s_pe.n
                if hf >= 2:
                    b.wait("tensor", s_dv, act_done[hf - 2])
                for kc in range(KC):
                    for s in range(nts):
                        last = (kc == KC - 1 and s == nts - 1)
                        b.em("tensor", lambda t, kc=kc, s=s, gsl=gsl, last=last, uoff=uoff:
                             (t.matmul(psum[:, uoff + s * ts_w:uoff + (s + 1) * ts_w],
                                       wgu_sb[gsl][:, KC * 128 + kc * 128:KC * 128 + (kc + 1) * 128],
                                       h_fm[:, kc * Tt + s * ts_w:kc * Tt + (s + 1) * ts_w],
                                       start=(kc == 0), stop=(kc == KC - 1)).then_inc(s_pe.h, 1) if last else
                              t.matmul(psum[:, uoff + s * ts_w:uoff + (s + 1) * ts_w],
                                       wgu_sb[gsl][:, KC * 128 + kc * 128:KC * 128 + (kc + 1) * 128],
                                       h_fm[:, kc * Tt + s * ts_w:kc * Tt + (s + 1) * ts_w],
                                       start=(kc == 0), stop=(kc == KC - 1))))
                s_pe.inc()
                u_done = s_pe.n
                gu_pe_free[gj] = s_pe.n
                gj += 1
                nxt = gj + 1
                if nxt < len(gu_jobs) * NMT and nxt not in gu_ready:
                    load_gu(nxt)
                b.wait("scalar", s_pe, g_done)
                if hf >= 1:
                    b.wait("scalar", s_dv, act_done[hf - 1])
                b.em("scalar", lambda s, goff=goff: s.activation(sg_sb[:, :], psum[:, goff:goff + Tt], AF.Silu).then_inc(s_ac.h, 1)); s_ac.inc()
                sg_done = s_ac.n
                sg_hist.append(sg_done)
                b.wait("vector", s_ac, sg_done)
                b.wait("vector", s_pe, u_done)
                if weighted:
                    b.em("vector", lambda v, uoff=uoff: v.tensor_tensor(sg_sb[:, :], sg_sb[:, :], psum[:, uoff:uoff + Tt], ALU.mult).then_inc(s_dv.h, 1)); s_dv.inc()
                    tu_done = s_dv.n
                    tu_hist.append(tu_done)
                    b.wait("vector", s_dv, tu_done)
                    b.wait("vector", s_dv, wbc_ready)
                    b.em("vector", lambda v, hf=hf, wsl=wsl: v.tensor_tensor(act_e[:, hf * Tt:(hf + 1) * Tt],
                                                                    sg_sb[:, :], wbc_sb[wsl][:, :], ALU.mult).then_inc(s_dv.h, 1)); s_dv.inc()
                    act_done.append(s_dv.n)
                else:
                    b.em("vector", lambda v, hf=hf, uoff=uoff: v.tensor_tensor(act_e[:, hf * Tt:(hf + 1) * Tt],
                                                                    sg_sb[:, :], psum[:, uoff:uoff + Tt], ALU.mult).then_inc(s_dv.h, 1)); s_dv.inc()
                    act_done.append(s_dv.n)
                    tu_hist.append(act_done[-1])
            if mt > 0 and ei == 0:
                for tg2 in range(min(nhf, NTG), NTG):
                    emit_f2(mt - 1, tg2)
            prev_sg_last = sg_hist[-1]
            if ei + 1 < len(experts) and experts[ei + 1][2]:
                b.wait("tensor", s_ac, sg_hist[-1])
                b.wait("tensor", s_dv, tu_hist[-1])
                ne = experts[ei + 1][0]
                for s in range(nts):
                    last = (s == nts - 1)
                    b.em("tensor", lambda t, ne=ne, s=s, last=last:
                         (t.matmul(psum[:, s * ts_w:(s + 1) * ts_w], sel_sb[0:E, ne * 128:(ne + 1) * 128],
                                   wfmT[0:E, s * ts_w:(s + 1) * ts_w], start=True, stop=True).then_inc(s_pe.h, 1)
                          if last else
                          t.matmul(psum[:, s * ts_w:(s + 1) * ts_w], sel_sb[0:E, ne * 128:(ne + 1) * 128],
                                   wfmT[0:E, s * ts_w:(s + 1) * ts_w], start=True, stop=True)))
                s_pe.inc()
                b.wait("vector", s_pe, s_pe.n)
                b.wait("vector", s_dv, act_done[-1] if weighted else s_dv.n)
                b.em("vector", lambda v, wsl=wsl: v.tensor_copy(wbc_sb[1 - wsl][:, :], psum[:, 0:Tt]).then_inc(s_dv.h, 1)); s_dv.inc()
                wbc_ready_cur = s_dv.n
            if ei == len(experts) - 1:
                hfm_last_use[mt] = gu_pe_free[gj - 1]
            if ei == len(experts) - 1 and mt + 1 < NMT:
                prepare_dma(mt + 1)
            for oh in range(NOH):
                if ei == len(experts) - 1 and mt + 1 < NMT and oh == min(1, NOH - 1):
                    prepare_pe(mt + 1, 0, 1)
                dsl = dj % 2
                b.wait("tensor", fd_slots[dsl], dn_ready[dj])
                b.wait("tensor", s_ac, sg_hist[-1])
                b.wait("tensor", s_dv, tu_hist[-1])
                b.wait("tensor", s_dv, act_done[-1])
                if moe_adds:
                    b.wait("tensor", s_dv, moe_adds[-1])
                for tg in range(NTG):
                    dbk = 4 + (tg % 2) * 2
                    if tg >= 2:
                        b.wait("tensor", s_dv, moe_adds[-2])
                    for hf in range(nhf):
                        for s in range(nos):
                            last = (hf == nhf - 1 and s == nos - 1)
                            b.em("tensor", lambda t, tg=tg, hf=hf, s=s, dsl=dsl, dbk=dbk, last=last, nhf=nhf:
                                 (t.matmul(psum[:, dbk * 512 + s * os_w:dbk * 512 + (s + 1) * os_w],
                                           act_e[:, hf * Tt + tg * 128:hf * Tt + (tg + 1) * 128],
                                           wdn_sb[dsl][:, hf * OH + s * os_w:hf * OH + (s + 1) * os_w],
                                           start=(hf == 0), stop=(hf == nhf - 1)).then_inc(s_pe.h, 1) if last else
                                  t.matmul(psum[:, dbk * 512 + s * os_w:dbk * 512 + (s + 1) * os_w],
                                           act_e[:, hf * Tt + tg * 128:hf * Tt + (tg + 1) * 128],
                                           wdn_sb[dsl][:, hf * OH + s * os_w:hf * OH + (s + 1) * os_w],
                                           start=(hf == 0), stop=(hf == nhf - 1))))
                    s_pe.inc()
                    b.wait("vector", s_pe, s_pe.n)
                    if ei == 0:
                        b.em("vector", lambda v, tg=tg, oh=oh, dbk=dbk:
                             v.tensor_copy(moe[:, tg * D + oh * OH:tg * D + (oh + 1) * OH],
                                           psum[:, dbk * 512:dbk * 512 + OH]).then_inc(s_dv.h, 1)); s_dv.inc()
                    else:
                        b.em("vector", lambda v, tg=tg, oh=oh, dbk=dbk:
                             v.tensor_tensor(moe[:, tg * D + oh * OH:tg * D + (oh + 1) * OH],
                                             moe[:, tg * D + oh * OH:tg * D + (oh + 1) * OH],
                                             psum[:, dbk * 512:dbk * 512 + OH], ALU.add).then_inc(s_dv.h, 1)); s_dv.inc()
                    moe_adds.append(s_dv.n)
                    if ei == len(experts) - 1 and oh == NOH - 1:
                        tg_final[tg] = s_dv.n
                        if mt == NMT - 1:
                            f2_tg_final[mt] = tg_final
                            emit_f2(mt, tg)
                dn_pe_free[dj] = s_pe.n
                dj += 1
                if dj < len(dn_jobs) * NMT and dj not in dn_ready:
                    load_dn(dj)
        moe_done = s_dv.n
        mt_guard[mt] = (s_dv.n, s_ac.n, s_pl.n)
        f2_tg_final[mt] = tg_final

    b.wait("sync", sq_io)
    b.wait("sync", s_out)
    b.wait("scalar", sa_io)
    for s2 in [s_tab, s_bq, s_xst, s_wout, s_rw, s_sg] + list(w_slots) + list(fg_slots) + list(fd_slots) + list(t_slots):
        b.wait("gpsimd", s2)

    with nc.Block() as block:
        b.run(block)
    return nc


def make_consts(cfg: Cfg, core_idx: int):
    T, H, NC, E = cfg.T, cfg.H, cfg.NC, cfg.E
    j = np.arange(128, dtype=np.float64)
    i = np.arange(128, dtype=np.float64)
    diff = i[None, :] - j[:, None]
    intra = np.where(diff >= 0, diff, BIG)
    c_intra = np.tile(intra, (1, H)).astype(np.float32)
    c_qdec = np.tile(np.broadcast_to(i[None, :], (128, 128)), (1, H)).astype(np.float32)
    c_kdec = np.broadcast_to((128.0 - j)[:, None], (128, H)).astype(np.float32).copy()
    cb = np.zeros((128, H * NC), np.float64)
    for h in range(H):
        cb[:, h * NC:(h + 1) * NC] = 128.0 * np.arange(NC)[None, :]
    c_bdc = cb.astype(np.float32)
    cc = np.zeros((128, H * 8), np.float64)
    for h in range(H):
        for jj in range(8):
            cc[:, h * 8 + jj] = (T * (core_idx - 1 - jj)) if jj < core_idx else BIG
    c_comb = cc.astype(np.float32)
    sel = np.zeros((E, E * 128), np.float32)
    for e in range(E):
        sel[e, e * 128:(e + 1) * 128] = 1.0
    identf = np.eye(128, dtype=np.float32)
    return {"c_intra": c_intra, "c_qdec": c_qdec, "c_kdec": c_kdec,
            "c_bdc": c_bdc, "c_comb": c_comb, "sel": sel, "identf": identf}


def run_cfg(cfg: Cfg, full_inputs: dict) -> np.ndarray:
    nc = build_program(cfg)
    shared = {k: np.ascontiguousarray(np.asarray(v, dtype=np.float32))
              for k, v in full_inputs.items() if k != "x"}
    shared["slopes"] = np.ascontiguousarray(
        np.asarray(full_inputs["slopes"], dtype=np.float32).reshape(cfg.H, 1))
    x = np.asarray(full_inputs["x"], dtype=np.float32)
    in_maps = []
    for c in range(cfg.n_cores):
        m = dict(shared)
        m["x"] = np.ascontiguousarray(x[c * cfg.T:(c + 1) * cfg.T, :])
        m.update(make_consts(cfg, c))
        in_maps.append(m)
    res = run_bass_kernel_spmd(nc, in_maps, core_ids=list(range(cfg.n_cores)))
    return np.concatenate([np.asarray(res.results[c]["out"]) for c in range(cfg.n_cores)], axis=0)


def kernel(L0_attn_x, L0_attn_Wqkv, L0_attn_bqkv, L0_attn_Wout, L0_attn_bout,
           L0_attn_slopes, L0_ffn_rms_weight, L0_ffn_router_w, L0_ffn_w_gate,
           L0_ffn_w_up, L0_ffn_w_down, L0_ffn_sw_gate, L0_ffn_sw_up, L0_ffn_sw_down):
    cfg = Cfg(T=2048, D=2048, H=16, F=1024, SF=1024, E=8, n_cores=8)
    x = np.asarray(L0_attn_x).reshape(16384, 2048)
    out = run_cfg(cfg, {
        "x": x, "wqkv": L0_attn_Wqkv, "bqkv": L0_attn_bqkv,
        "wout": L0_attn_Wout, "bout": L0_attn_bout,
        "slopes": L0_attn_slopes, "rmsw": L0_ffn_rms_weight,
        "router_w": L0_ffn_router_w, "wg": L0_ffn_w_gate,
        "wu": L0_ffn_w_up, "wd": L0_ffn_w_down,
        "swg": L0_ffn_sw_gate, "swu": L0_ffn_sw_up, "swd": L0_ffn_sw_down})
    return out.reshape(1, 16384, 2048)
